# version 03
# baseline (speedup 1.0000x reference)
import numpy as np
import concourse.bass as bass
import concourse.mybir as mybir
from concourse.bass_utils import run_bass_kernel_spmd

F32 = mybir.dt.float32
F16 = mybir.dt.float16
BF16 = mybir.dt.bfloat16
FP8 = mybir.dt.float8e4
AF = mybir.ActivationFunctionType
ALU = mybir.AluOpType
AX = mybir.AxisListType

P = 128
D = 2048
TOWN = 1024
NT = 8
HEADS, GROUPS, HD = 16, 4, 128
NB = 8
EPS_LN = 1e-5
EPS_RMS = 1e-6

ENGINES = ("tensor", "vector", "scalar", "sync", "gpsimd")


def ap_md(t_ap, offset_elems, dims):
    return bass.AP(tensor=t_ap.tensor, offset=t_ap.offset + offset_elems,
                   ap=[list(t_ap.ap[0])] + [[s, c] for (s, c) in dims])


def hap(t_ap, offset_elems, dims):
    return bass.AP(tensor=t_ap.tensor, offset=t_ap.offset + offset_elems,
                   ap=[[s, c] for (s, c) in dims])


ENG_SEM = {"tensor": "mm", "vector": "ve", "scalar": "se", "gpsimd": "gp"}


class Prog:

    def __init__(self):
        self.q = {e: [] for e in ENGINES}
        self.count = {}
        self.waited = {e: {} for e in ENGINES}
        self.force_next = {e: False for e in ENGINES}

    def inc(self, sem, n=1):
        self.count[sem] = self.count.get(sem, 0) + n
        return self.count[sem]

    def cur(self, sem):
        return self.count.get(sem, 0)

    def step(self, eng, build, waits=(), sem=None, n=1, ptr=False):
        waits = list(waits)
        if eng == "vector":
            ptr = True
        if (ptr or self.force_next[eng]) and eng in ENG_SEM:
            waits.append((ENG_SEM[eng], self.cur(ENG_SEM[eng])))
        self.force_next[eng] = bool(ptr)
        wl = []
        for s, v in waits:
            if v > 0 and self.waited[eng].get(s, -1) < v:
                wl.append((s, v))
                self.waited[eng][s] = v
        tgt = self.inc(sem, n) if sem else None
        def fn(e, SEM, wl=wl, sem=sem, n=n):
            for s, v in wl:
                e.wait_ge(SEM[s], v)
            inst = build(e)
            if sem is not None:
                if inst is None:
                    raise RuntimeError("step with sem must return instruction")
                inst.then_inc(SEM[sem], n)
        self.q[eng].append(fn)
        return tgt

    def barrier_wait(self, eng, waits):
        self.step(eng, lambda e: None, waits=waits)


def build_nc(j, TRUNC):
    nc = bass.Bass("TRN2", target_bir_lowering=False)

    x_in = nc.dram_tensor("x", [TOWN, D], F32, kind="ExternalInput").ap()
    wq_in = nc.dram_tensor("wq", [D, D], F32, kind="ExternalInput").ap()
    wk_in = nc.dram_tensor("wk", [D, 512], F32, kind="ExternalInput").ap()
    wv_in = nc.dram_tensor("wv", [D, 512], F32, kind="ExternalInput").ap()
    wo_in = nc.dram_tensor("wo", [D, D], F32, kind="ExternalInput").ap()
    wfT_in = nc.dram_tensor("wfT", [D, D], F32, kind="ExternalInput").ap()
    bq_in = nc.dram_tensor("bq", [1, D], F32, kind="ExternalInput").ap()
    bk_in = nc.dram_tensor("bk", [1, 512], F32, kind="ExternalInput").ap()
    bv_in = nc.dram_tensor("bv", [1, 512], F32, kind="ExternalInput").ap()
    bo_in = nc.dram_tensor("bo", [1, D], F32, kind="ExternalInput").ap()
    bf_in = nc.dram_tensor("bf", [1, D], F32, kind="ExternalInput").ap()
    ws_in = nc.dram_tensor("ws", [1, 16], F32, kind="ExternalInput").ap()
    qnsr_in = nc.dram_tensor("qnsr", [P, D], F32, kind="ExternalInput").ap()
    knsr_in = nc.dram_tensor("knsr", [P, 512], F32, kind="ExternalInput").ap()
    qns_in = nc.dram_tensor("qns", [1, HD], F32, kind="ExternalInput").ap()
    kns_in = nc.dram_tensor("kns", [1, HD], F32, kind="ExternalInput").ap()
    cos_in = nc.dram_tensor("cosn", [P, NT * 64], F32, kind="ExternalInput").ap()
    sin_in = nc.dram_tensor("sinn", [P, NT * 64], F32, kind="ExternalInput").ap()
    tri_in = nc.dram_tensor("tri", [P, 4 * 128], F16, kind="ExternalInput").ap()

    y_out = nc.dram_tensor("y", [TOWN, D], F32, kind="ExternalOutput").ap()

    ar1i = nc.dram_tensor("ar1i", [1, 16], F32)
    ar1o = nc.dram_tensor("ar1o", [1, 16], F32)
    ar2i = nc.dram_tensor("ar2i", [1, 16], F32)
    ar2o = nc.dram_tensor("ar2o", [1, 16], F32)
    AGN = GROUPS * HD * TOWN + TOWN * 512
    agi = nc.dram_tensor("agi", [1, AGN], F16)
    ago = nc.dram_tensor("ago", [4, AGN], F16)
    VOFF = GROUPS * HD * TOWN

    BASE = ((nc.sbuf_base + 63) // 64) * 64
    off = [BASE]
    def alloc(name, shape, dtype, nbytes):
        t = nc.alloc_sbuf_tensor_at(name, shape, dtype, offset=off[0])
        off[0] += (nbytes + 63) // 64 * 64
        return t
    def at(name, shape, dtype, byteoff):
        return nc.alloc_sbuf_tensor_at(name, shape, dtype, offset=BASE + byteoff)

    ones16 = alloc("ones16", [P, 128], F16, 256)
    onesbf = alloc("onesbf", [P, 128], BF16, 256)
    ones32 = alloc("ones32", [1, 128], F32, 512)
    stats = alloc("stats", [P, 96], F32, 384)
    misc = alloc("misc", [P, 16], F32, 64)
    qstat = alloc("qstat", [P, 64], F32, 256)
    grow = alloc("grow", [1, 128], F32, 512)
    g1 = alloc("g1", [1, 16], F32, 64)
    cosn = alloc("cosn_sb", [P, NT * 64], F32, NT * 64 * 4)
    sinn = alloc("sinn_sb", [P, NT * 64], F32, NT * 64 * 4)
    tri = alloc("tri_sb", [P, 4 * 128], F16, 4 * 128 * 2)
    bq_row = alloc("bq_row", [1, D], F16, D * 2)
    bo_row = alloc("bo_row", [1, D], F16, D * 2)
    bk_row = alloc("bk_row", [1, 512], F16, 512 * 2)
    bv_row = alloc("bv_row", [1, 512], F16, 512 * 2)
    bf_row = alloc("bf_row", [1, D], BF16, D * 2)
    qnsr = alloc("qnsr_sb", [P, D], F16, D * 2)
    knsr = alloc("knsr_sb", [P, 512], F16, 512 * 2)
    qnsraw = alloc("qns_raw", [1, HD], F32, 512)
    knsraw = alloc("kns_raw", [1, HD], F32, 512)
    wsrow = alloc("ws_sb", [1, 16], F32, 64)
    cbuf = alloc("cbuf", [1, 16], F16, 64)
    S_END = off[0] - BASE
    assert S_END <= 29 * 1024, S_END

    A_OFF = 29 * 1024
    W_OFF = A_OFF + 32 * 1024
    X_OFF = W_OFF + 64 * 1024
    C_OFF = X_OFF + 64 * 1024
    TOP = C_OFF + 17 * 1024
    assert BASE + TOP <= nc.sbuf_top, (BASE, TOP, nc.sbuf_top)

    xqf = at("xqf", [P, 16 * TOWN], BF16, A_OFF)
    xhf = at("xhf", [P, 16 * TOWN], F16, A_OFF)
    attn = at("attn", [P, 16 * TOWN], F16, A_OFF)
    wfT = at("wfT", [P, 16 * D], BF16, W_OFF)
    wqh = at("wqh", [P, 16 * 1024], F16, W_OFF)
    wk = at("wk_sb", [P, 16 * 512], F16, W_OFF + 32 * 1024)
    wv = at("wv_sb", [P, 16 * 512], F16, W_OFF + 48 * 1024)
    wo = at("wo_sb", [P, 16 * D], F16, W_OFF)
    x = at("x_sb", [P, NT * D], F32, X_OFF)
    qhat = at("qhat", [P, HEADS * TOWN], F16, X_OFF)
    kown = at("kown", [P, GROUPS * TOWN], F16, X_OFF + 32 * 1024)
    vown = at("vown", [P, NT * 512], F16, X_OFF + 40 * 1024)
    qscr = at("qscr", [P, D], F32, X_OFF + 48 * 1024)
    kblk = at("kblk", [P, 2 * 512], F16, X_OFF + 32 * 1024)
    vblk = at("vblk", [P, 2 * 512], F16, X_OFF + 34 * 1024)
    pblk = at("pblk", [P, 2 * 1024], F16, X_OFF + 36 * 1024)
    sums_row = at("sums_row", [1, 4096], F16, X_OFF + 40 * 1024)
    r_row = at("r_row", [1, 4096], F16, X_OFF + 48 * 1024)
    rspread = at("rspread", [P, 32], F32, X_OFF + 56 * 1024)
    rb_sb = at("rb_sb", [P, 512], F32, X_OFF + 56 * 1024 + 256)
    scr32 = at("scr32", [P, D], F32, C_OFF)
    scr8 = at("scr8", [P, D], FP8, C_OFF + 8 * 1024)
    scr16 = at("scr16", [P, D], BF16, C_OFF + 10 * 1024)
    scr16h = at("scr16h", [P, D], F16, C_OFF + 10 * 1024)
    scr16q = at("scr16q", [P, 1024], F16, C_OFF + 14 * 1024)
    scr16k = at("scr16k", [P, 512], F16, C_OFF + 16 * 1024)

    xspill = nc.dram_tensor("xspill", [TOWN, D], F32)

    p = Prog()
    state = {}
    sems = ["wx", "ww", "wsp", "wst", "wag", "mm", "ve", "se",
            "gp", "cc"] + ["wtp%d" % i for i in range(8)]
    def rr_sem():
        sn = "wtp%d" % (tp_idx[0] % 8)
        tp_idx[0] += 1
        return sn
    tp_idx = [0]
    def tp_step(build, waits=()):
        sn = "wtp%d" % (tp_idx[0] % 8)
        tp_idx[0] += 1
        v = p.step("sync", build, waits=waits, sem=sn, n=16)
        return (sn, v)
    def tp_all():
        return [("wtp%d" % i, p.cur("wtp%d" % i)) for i in range(8)]

    def init_phase():
        p.step("sync", lambda e: e.dma_start(
            out=ap_md(x[:, :], 0, [(D, NT), (1, D)]),
            in_=hap(x_in, 0, [(D, P), (P * D, NT), (1, D)])), sem="wx", n=16)
        for nm, dst, src in [("cos", cosn, cos_in), ("sin", sinn, sin_in)]:
            p.step("sync", lambda e, d=dst, s=src: e.dma_start(out=d[:, :], in_=s),
                   sem="wx", n=16)
        p.step("sync", lambda e: e.dma_start(out=tri[:, :], in_=tri_in), sem="wx", n=16)
        for dst, src in [(wsrow, ws_in), (qnsraw, qns_in),
                         (knsraw, kns_in)]:
            p.step("sync", lambda e, d=dst, s=src: e.dma_start(out=d[:, :], in_=s),
                   sem="wx", n=16)
        for dst, src in [(bq_row, bq_in), (bk_row, bk_in), (bv_row, bv_in),
                         (bo_row, bo_in), (qnsr, qnsr_in), (knsr, knsr_in)]:
            p.step("gpsimd", lambda e, d=dst, s=src: e.dma_start(out=d[:, :], in_=s),
                   sem="ww", n=16)
        for c in range(16):
            p.step("gpsimd", lambda e, c=c: e.dma_start(
                out=wfT[:, c * D:(c + 1) * D],
                in_=hap(wfT_in, c * P * D, [(D, P), (1, D)])), sem="ww", n=16)
        for t, val in [(ones16, 1.0), (onesbf, 1.0), (ones32, 1.0), (g1, 0.0), (misc, 0.0)]:
            p.step("gpsimd", lambda e, t=t, v=val: e.memset(t[:, :], v), sem="gp")
        state["gp_init"] = p.cur("gp")
        p.step("sync", lambda e: e.dma_start(out=scr32[0:1, 0:D], in_=bf_in),
               sem="wx", n=16)
        wxv = p.cur("wx")
        p.step("vector", lambda e: e.reciprocal(misc[0:1, 3:4], wsrow[0:1, 0:1]),
               waits=[("wx", wxv), ("gp", state["gp_init"])], sem="ve")
        p.step("vector", lambda e: e.tensor_single_scalar(
            scr32[0:1, 0:D], scr32[0:1, 0:D], misc[0:1, 3:4], ALU.mult),
            sem="ve", ptr=True)
        p.step("vector", lambda e: e.tensor_copy(bf_row[:, :], scr32[0:1, 0:D]), sem="ve")
        wwv = p.cur("ww")
        p.step("tensor", lambda e: e.matmul(
            psB[:, 0:1], ones32[0:1, 0:128], wsrow[0:1, 0:1], start=True, stop=True),
            waits=[("wx", wxv), ("gp", p.cur("gp"))], sem="mm")
        p.step("vector", lambda e: e.tensor_scalar_mul(misc[:, 4:5], psB[:, 0:1], 0.5),
               waits=[("mm", p.cur("mm"))], sem="ve")
        state["ve_init"] = p.cur("ve")

    def ln_stats(minmax, xwait):
        for tb in range(NT):
            sl = x[:, tb * D:(tb + 1) * D]
            w = [("wx", xwait)] if tb == 0 else []
            p.step("vector", lambda e, sl=sl, tb=tb: e.reduce_sum(
                stats[:, tb:tb + 1], sl, axis=AX.X), waits=w, sem="ve")
            p.step("vector", lambda e, sl=sl, tb=tb: e.scalar_tensor_tensor(
                scr32[:, :], sl, 1.0, sl, ALU.mult, ALU.mult,
                accum_out=stats[:, 8 + tb:9 + tb]), sem="ve")
            if minmax:
                p.step("vector", lambda e, sl=sl, tb=tb: e.tensor_reduce(
                    stats[:, 16 + tb:17 + tb], sl, axis=AX.X, op=ALU.min), sem="ve")
                p.step("vector", lambda e, sl=sl, tb=tb: e.tensor_reduce(
                    stats[:, 24 + tb:25 + tb], sl, axis=AX.X, op=ALU.max), sem="ve")
        p.step("vector", lambda e: e.tensor_scalar_mul(
            stats[:, 32:40], stats[:, 0:8], 1.0 / D), sem="ve")
        p.step("vector", lambda e: e.tensor_scalar_mul(
            stats[:, 48:56], stats[:, 8:16], 1.0 / D), sem="ve")
        p.step("vector", lambda e: e.tensor_mul(
            stats[:, 56:64], stats[:, 32:40], stats[:, 32:40]), sem="ve")
        p.step("vector", lambda e: e.tensor_sub(
            stats[:, 48:56], stats[:, 48:56], stats[:, 56:64]), sem="ve")
        p.step("vector", lambda e: e.tensor_scalar_add(
            stats[:, 48:56], stats[:, 48:56], EPS_LN), sem="ve")
        vv = p.cur("ve")
        p.step("scalar", lambda e: e.activation(
            stats[:, 56:64], stats[:, 48:56], AF.Sqrt), waits=[("ve", vv)], sem="se")
        sv = p.cur("se")
        p.step("vector", lambda e: e.reciprocal(stats[:, 64:72], stats[:, 56:64]),
               waits=[("se", sv)], sem="ve")
        p.step("vector", lambda e: e.tensor_mul(
            stats[:, 64:72], stats[:, 64:72], stats[:, 48:56]), sem="ve")
        p.step("vector", lambda e: e.tensor_add(
            stats[:, 64:72], stats[:, 64:72], stats[:, 56:64]), sem="ve")
        p.step("vector", lambda e: e.tensor_scalar_mul(
            stats[:, 56:64], stats[:, 64:72], 0.5), sem="ve")
        p.step("vector", lambda e: e.reciprocal(stats[:, 40:48], stats[:, 56:64]),
               sem="ve")

    def ln_gmax_allreduce(ar_in_t, ar_out_t):
        p.step("vector", lambda e: e.tensor_sub(
            stats[:, 56:64], stats[:, 24:32], stats[:, 32:40]), sem="ve")
        p.step("vector", lambda e: e.tensor_sub(
            stats[:, 64:72], stats[:, 32:40], stats[:, 16:24]), sem="ve")
        p.step("vector", lambda e: e.tensor_max(
            stats[:, 88:96], stats[:, 56:64], stats[:, 64:72]), sem="ve")
        p.step("vector", lambda e: e.tensor_mul(
            stats[:, 88:96], stats[:, 88:96], stats[:, 40:48]), sem="ve")
        p.step("vector", lambda e: e.reduce_max(
            misc[:, 0:1], stats[:, 88:96], axis=AX.X), sem="ve")
        vv = p.cur("ve")
        p.step("sync", lambda e: e.dma_start(out=grow[0:1, :],
               in_=misc[:, 0:1]), waits=[("ve", vv)], sem="wsp", n=16)
        p.step("vector", lambda e: e.reduce_max(g1[0:1, 0:1], grow[0:1, :],
               axis=AX.X), waits=[("wsp", p.cur("wsp"))], sem="ve")
        vv = p.cur("ve")
        p.step("gpsimd", lambda e: e.dma_start(out=ar_in_t.ap(), in_=g1[:, :]),
               waits=[("ve", vv)], sem="wag", n=16)
        p.step("gpsimd", lambda e: e.collective_compute(
            "AllReduce", ALU.max, replica_groups=[list(range(8))],
            ins=[ar_in_t.ap()], outs=[ar_out_t.ap()]),
            waits=[("wag", p.cur("wag"))], sem="cc")
        p.step("gpsimd", lambda e: e.dma_start(out=g1[:, :], in_=ar_out_t.ap()),
               waits=[("cc", p.cur("cc"))], sem="wag", n=16)
        wagv = p.cur("wag")
        p.step("vector", lambda e: e.tensor_scalar(
            g1[0:1, 1:2], g1[0:1, 0:1], 1.0 / 448.0, 1e-7, ALU.mult, ALU.max),
            waits=[("wag", wagv)], sem="ve")
        p.step("vector", lambda e: e.reciprocal(g1[0:1, 2:3], g1[0:1, 1:2]), sem="ve")
        vv = p.cur("ve")
        p.step("tensor", lambda e: e.matmul(
            psB[:, 0:1], ones32[0:1, 0:128], g1[0:1, 2:3], start=True, stop=True),
            waits=[("ve", vv)], sem="mm")
        p.step("vector", lambda e: e.tensor_scalar_mul(
            misc[:, 2:3], psB[:, 0:1], 0.5), waits=[("mm", p.cur("mm"))], sem="ve")
        p.step("vector", lambda e: e.tensor_single_scalar(
            stats[:, 72:80], stats[:, 40:48], misc[:, 2:3], ALU.mult),
            sem="ve", ptr=True)
        p.step("vector", lambda e: e.tensor_mul(
            stats[:, 80:88], stats[:, 32:40], stats[:, 72:80]), sem="ve")
        p.step("vector", lambda e: e.tensor_scalar_mul(
            stats[:, 80:88], stats[:, 80:88], -1.0), sem="ve")
        return p.cur("ve")

    def ffn(scale_ready_ve, store_after, region_mm=0):
        tp_done = []
        for tb in range(NT):
            sl = x[:, tb * D:(tb + 1) * D]
            qw = [("ve", scale_ready_ve)]
            if tb == 0:
                qw.append(("mm", region_mm))
            else:
                qw.append(tp_done[tb - 1])
            p.step("vector", lambda e, sl=sl, tb=tb: e.tensor_scalar(
                scr8[:, :], sl, stats[:, 72 + tb:73 + tb], stats[:, 80 + tb:81 + tb],
                ALU.mult, ALU.add), waits=qw, sem="ve", ptr=True)
            p.step("vector", lambda e: e.tensor_scalar_mul(
                scr16[:, :], scr8[:, :], 2.0), sem="ve")
            vv = p.cur("ve")
            tp_done.append(tp_step(lambda e, tb=tb: e.dma_start(
                out=ap_md(xqf[:, :], tb * 128, [(TOWN, 16), (1, 128)]),
                in_=scr16[:, :], transpose=True),
                waits=[("ve", vv), ("mm", region_mm)]))
        wwv = p.cur("ww")
        ev_of_tb = []
        for tb in range(NT):
            ps = psA if tb % 2 == 0 else psB
            w = [tp_done[tb], ("ww", wwv)]
            if tb >= 2:
                w.append(("ve", ev_of_tb[tb - 2]))
            for c in range(16):
                lhs = xqf[:, c * TOWN + tb * 128: c * TOWN + tb * 128 + 128]
                for ob in range(4):
                    p.step("tensor", lambda e, lhs=lhs, ps=ps, c=c, ob=ob: e.matmul(
                        ps[:, ob * 512:(ob + 1) * 512], lhs,
                        wfT[:, c * D + ob * 512: c * D + ob * 512 + 512],
                        start=(c == 0), stop=False),
                        waits=(w if (c == 0 and ob == 0) else []),
                        sem=("mm" if (c == 15 and ob == 3) else None))
            for ob in range(4):
                p.step("tensor", lambda e, ps=ps, ob=ob: e.matmul(
                    ps[:, ob * 512:(ob + 1) * 512], onesbf[0:1, 0:128],
                    bf_row[0:1, ob * 512:(ob + 1) * 512], start=False, stop=True),
                    waits=[("ve", state["ve_init"])], sem="mm")
            mmv = p.cur("mm")
            for ob in range(4):
                p.step("vector", lambda e, ps=ps, tb=tb, ob=ob: e.scalar_tensor_tensor(
                    x[:, tb * D + ob * 512: tb * D + ob * 512 + 512],
                    ps[:, ob * 512:(ob + 1) * 512], misc[:, 4:5],
                    x[:, tb * D + ob * 512: tb * D + ob * 512 + 512],
                    ALU.mult, ALU.add),
                    waits=[("mm", mmv)], sem="ve", ptr=True)
            ev_of_tb.append(p.cur("ve"))
            if store_after:
                vv = p.cur("ve")
                p.step("sync", lambda e, tb=tb: e.dma_start(
                    out=hap(y_out, tb * 128 * D, [(D, P), (1, D)]),
                    in_=x[:, tb * D:(tb + 1) * D]), waits=[("ve", vv)],
                    sem="wst", n=16)
        return ev_of_tb, tp_done

    def rms_rope_transpose(src, nheads, scol, rcol, tcol, nsr, nsoff, tb, dst,
                           dstoff, dst16, h16_free):
        W = nheads * HD
        p.step("vector", lambda e: e.scalar_tensor_tensor(
            scr32[:, 0:W], src, 1.0, src, ALU.mult, ALU.mult),
            waits=[h16_free], sem="ve")
        p.step("vector", lambda e: e.reduce_sum(
            qstat[:, scol:scol + nheads],
            ap_md(scr32[:, :], 0, [(HD, nheads), (1, HD)]), axis=AX.X), sem="ve")
        p.step("vector", lambda e: e.tensor_scalar(
            qstat[:, scol:scol + nheads], qstat[:, scol:scol + nheads],
            1.0 / HD, EPS_RMS, ALU.mult, ALU.add), sem="ve")
        vv = p.cur("ve")
        p.step("scalar", lambda e: e.activation(
            qstat[:, rcol:rcol + nheads], qstat[:, scol:scol + nheads], AF.Sqrt),
            waits=[("ve", vv)], sem="se")
        sv = p.cur("se")
        p.step("vector", lambda e: e.reciprocal(
            qstat[:, tcol:tcol + nheads], qstat[:, rcol:rcol + nheads]),
            waits=[("se", sv)], sem="ve")
        p.step("vector", lambda e: e.tensor_mul(
            qstat[:, tcol:tcol + nheads], qstat[:, tcol:tcol + nheads],
            qstat[:, scol:scol + nheads]), sem="ve")
        p.step("vector", lambda e: e.tensor_add(
            qstat[:, tcol:tcol + nheads], qstat[:, tcol:tcol + nheads],
            qstat[:, rcol:rcol + nheads]), sem="ve")
        p.step("vector", lambda e: e.tensor_scalar_mul(
            qstat[:, rcol:rcol + nheads], qstat[:, tcol:tcol + nheads],
            0.5), sem="ve")
        p.step("vector", lambda e: e.reciprocal(
            qstat[:, rcol:rcol + nheads], qstat[:, rcol:rcol + nheads]), sem="ve")
        for hl in range(nheads):
            p.step("vector", lambda e, hl=hl: e.tensor_single_scalar(
                ap_md(src, hl * HD, [(1, HD)]), ap_md(src, hl * HD, [(1, HD)]),
                qstat[:, rcol + hl:rcol + hl + 1], ALU.mult), sem="ve")
        p.step("vector", lambda e: e.tensor_mul(
            src, src, nsr[:, nsoff:nsoff + W]), sem="ve")
        cosb = ap_md(cosn[:, :], tb * 64, [(0, nheads), (1, 64)])
        sinb = ap_md(sinn[:, :], tb * 64, [(0, nheads), (1, 64)])
        a3 = ap_md(src, 0, [(HD, nheads), (1, 64)])
        b3 = ap_md(src, 64, [(HD, nheads), (1, 64)])
        t1 = ap_md(scr32[:, :], 1024, [(64, nheads), (1, 64)])
        t2 = ap_md(scr32[:, :], 1024 + nheads * 64, [(64, nheads), (1, 64)])
        out_re = ap_md(dst16[:, :], 0, [(HD, nheads), (1, 64)])
        out_im = ap_md(dst16[:, :], 64, [(HD, nheads), (1, 64)])
        p.step("vector", lambda e: e.tensor_tensor(t1, a3, cosb, ALU.mult), sem="ve")
        p.step("vector", lambda e: e.tensor_tensor(t2, b3, sinb, ALU.mult), sem="ve")
        p.step("vector", lambda e: e.tensor_tensor(out_re, t1, t2, ALU.subtract), sem="ve")
        p.step("vector", lambda e: e.tensor_tensor(t1, a3, sinb, ALU.mult), sem="ve")
        p.step("vector", lambda e: e.tensor_tensor(t2, b3, cosb, ALU.mult), sem="ve")
        p.step("vector", lambda e: e.tensor_tensor(out_im, t1, t2, ALU.add), sem="ve")
        vv = p.cur("ve")
        return tp_step(lambda e: e.dma_start(
            out=ap_md(dst, dstoff + tb * 128, [(TOWN, nheads), (1, 128)]),
            in_=dst16[:, :], transpose=True),
            waits=[("ve", vv)])

    def qkv_phase(tp_ffn1):
        ln_stats(False, 0)
        p.step("vector", lambda e: e.tensor_mul(
            stats[:, 80:88], stats[:, 32:40], stats[:, 40:48]), sem="ve")
        p.step("vector", lambda e: e.tensor_scalar_mul(
            stats[:, 80:88], stats[:, 80:88], -1.0), sem="ve")
        sve = p.cur("ve")
        mm_ffn1 = p.cur("mm")
        for c in range(16):
            p.step("gpsimd", lambda e, c=c: e.dma_start(
                out=wqh[:, c * 1024:(c + 1) * 1024],
                in_=hap(wq_in, c * P * D, [(D, P), (1, 1024)])),
                waits=[("mm", mm_ffn1)], sem="ww", n=16)
            p.step("gpsimd", lambda e, c=c: e.dma_start(
                out=wk[:, c * 512:(c + 1) * 512],
                in_=hap(wk_in, c * P * 512, [(512, P), (1, 512)])), sem="ww", n=16)
            p.step("gpsimd", lambda e, c=c: e.dma_start(
                out=wv[:, c * 512:(c + 1) * 512],
                in_=hap(wv_in, c * P * 512, [(512, P), (1, 512)])), sem="ww", n=16)
        ww_A = p.cur("ww")
        tpxh = []
        for tb in range(NT):
            sl = x[:, tb * D:(tb + 1) * D]
            qw = [("ve", sve)]
            qw.append(tp_ffn1[-1] if tb == 0 else tpxh[tb - 1])
            p.step("vector", lambda e, sl=sl, tb=tb: e.tensor_scalar(
                scr16h[:, :], sl, stats[:, 40 + tb:41 + tb],
                stats[:, 80 + tb:81 + tb], ALU.mult, ALU.add), waits=qw,
                sem="ve", ptr=True)
            vv = p.cur("ve")
            tpxh.append(tp_step(lambda e, tb=tb: e.dma_start(
                out=ap_md(xhf[:, :], tb * 128, [(TOWN, 16), (1, 128)]),
                in_=scr16h[:, :], transpose=True), waits=[("ve", vv)]))
            p.step("sync", lambda e, tb=tb: e.dma_start(
                out=hap(xspill.ap(), tb * 128 * D, [(D, P), (1, D)]),
                in_=x[:, tb * D:(tb + 1) * D]), sem="wsp", n=16)
        spill_done = p.cur("wsp")
        last_tp = {"q": tpxh[-1], "k": tpxh[-1]}
        ev_prev = {}
        for half in range(2):
            if half == 1:
                kv_tp = tp_all()
                vev = p.cur("ve")
                p.step("sync", lambda e: e.dma_start(
                    out=hap(agi.ap(), 0, [(TOWN, P), (131072, 4), (1, TOWN)]),
                    in_=ap_md(kown[:, :], 0, [(TOWN, 4), (1, TOWN)])),
                    waits=kv_tp + [("ve", vev)], sem="wsp", n=16)
                p.step("sync", lambda e: e.dma_start(
                    out=hap(agi.ap(), VOFF, [(512, P), (512 * 128, 8), (1, 512)]),
                    in_=ap_md(vown[:, :], 0, [(512, 8), (1, 512)])),
                    sem="wsp", n=16)
                p.step("gpsimd", lambda e: e.collective_compute(
                    "AllGather", ALU.bypass,
                    replica_groups=[[0, 1, 2, 3], [4, 5, 6, 7]],
                    ins=[agi.ap()], outs=[ago.ap()]),
                    waits=[("wsp", p.cur("wsp"))], sem="cc")
                mm_passA = p.cur("mm")
                for c in range(16):
                    p.step("gpsimd", lambda e, c=c: e.dma_start(
                        out=wqh[:, c * 1024:(c + 1) * 1024],
                        in_=hap(wq_in, c * P * D + 1024, [(D, P), (1, 1024)])),
                        waits=[("mm", mm_passA)], sem="ww", n=16)
            wwv = p.cur("ww")
            for tb in range(NT):
                gt = half * NT + tb
                ps = psA if tb % 2 == 0 else psB
                w = [tpxh[tb], ("ww", wwv)]
                if gt >= 2:
                    w.append(("ve", ev_prev[gt - 2]))
                nmm = 4 if half == 0 else 2
                for c in range(16):
                    lhs = xhf[:, c * TOWN + tb * 128: c * TOWN + tb * 128 + 128]
                    outs = [ps[:, 0:512], ps[:, 512:1024]]
                    rhss = [wqh[:, c * 1024:c * 1024 + 512],
                            wqh[:, c * 1024 + 512:(c + 1) * 1024]]
                    if half == 0:
                        outs += [ps[:, 1024:1536], ps[:, 1536:2048]]
                        rhss += [wk[:, c * 512:(c + 1) * 512],
                                 wv[:, c * 512:(c + 1) * 512]]
                    for i in range(nmm):
                        p.step("tensor", lambda e, o=outs[i], lhs=lhs, r=rhss[i],
                               c=c: e.matmul(o, lhs, r, start=(c == 0), stop=False),
                               waits=(w if (c == 0 and i == 0) else []),
                               sem=None)
                boff = 0 if half == 0 else 1024
                bl = [(ps[:, 0:512], bq_row[0:1, boff:boff + 512]),
                      (ps[:, 512:1024], bq_row[0:1, boff + 512:boff + 1024])]
                if half == 0:
                    bl += [(ps[:, 1024:1536], bk_row[0:1, :]),
                           (ps[:, 1536:2048], bv_row[0:1, :])]
                for o, b in bl:
                    p.step("tensor", lambda e, o=o, b=b: e.matmul(
                        o, ones16[0:1, 0:128], b, start=False, stop=True), sem="mm")
                mmv = p.cur("mm")
                p.step("vector", lambda e, ps=ps: e.tensor_copy(
                    qscr[:, 0:1024], ps[:, 0:1024]),
                    waits=[("mm", mmv), ("wsp", spill_done)], sem="ve")
                if half == 0:
                    p.step("vector", lambda e, ps=ps: e.tensor_copy(
                        qscr[:, 1024:1536], ps[:, 1024:1536]), sem="ve")
                    p.step("vector", lambda e, ps=ps, tb=tb: e.tensor_copy(
                        vown[:, tb * 512:(tb + 1) * 512], ps[:, 1536:2048]), sem="ve")
                ev_prev[gt] = p.cur("ve")
                last_tp["q"] = rms_rope_transpose(
                    qscr[:, 0:1024], 8, 0, 16, 32, qnsr, half * 1024,
                    tb, qhat[:, :], half * 8 * TOWN, scr16q, last_tp["q"])
                if half == 0:
                    last_tp["k"] = rms_rope_transpose(
                        qscr[:, 1024:1536], 4, 8, 40, 56, knsr, 0,
                        tb, kown[:, :], 0, scr16k, last_tp["k"])
        qhat_tp = tp_all()
        return p.cur("cc"), qhat_tp

    def attention_phase(cc_ag, qhat_tp):
        p.step("vector", lambda e: e.reduce_max(
            g1[0:1, 4:5], qnsraw[0:1, :], axis=AX.X, apply_absolute_value=True),
            sem="ve")
        p.step("vector", lambda e: e.reduce_max(
            g1[0:1, 5:6], knsraw[0:1, :], axis=AX.X, apply_absolute_value=True),
            sem="ve")
        p.step("vector", lambda e: e.tensor_mul(
            g1[0:1, 6:7], g1[0:1, 4:5], g1[0:1, 5:6]), sem="ve")
        p.step("vector", lambda e: e.tensor_copy(cbuf[0:1, 0:1], g1[0:1, 6:7]),
               sem="ve")
        vv = p.cur("ve")
        p.step("tensor", lambda e: e.matmul(
            psB[:, 0:1], ones16[0:1, 0:128], cbuf[0:1, 0:1], start=True, stop=True),
            waits=[("ve", vv)], sem="mm")
        p.step("vector", lambda e: e.tensor_scalar_mul(
            misc[:, 5:6], psB[:, 0:1], -1.0), waits=[("mm", p.cur("mm"))], sem="ve")
        negc_ve = p.cur("ve")
        mm_qkv = p.cur("mm")
        for c in range(16):
            p.step("gpsimd", lambda e, c=c: e.dma_start(
                out=wo[:, c * D:(c + 1) * D],
                in_=hap(wo_in, c * P * D, [(D, P), (1, D)])),
                waits=[("mm", mm_qkv)], sem="ww", n=16)
        wo_done = p.cur("ww")

        ikv = 0
        it = 0
        kv_done = {}
        kv_free_mm = {0: 0, 1: 0}
        sp_free_se = {0: 0, 1: 0}
        pb_free_mm = {0: 0, 1: 0}
        norm_ve_prev = p.cur("ve")
        for s in range(4):
            for g in range(4):
                g_ev = p.cur("ve")
                first = {0: True, 1: True}
                for kb in range(2 * s + 2):
                    buf = ikv % 2
                    ikv += 1
                    kvsem = rr_sem()
                    p.step("sync", lambda e, g=g, kb=kb, buf=buf: e.dma_start(
                        out=kblk[:, buf * 512:(buf + 1) * 512],
                        in_=hap(ago.ap(), g * 131072 + kb * 128,
                                [(TOWN, P), (AGN, 4), (1, 128)])),
                        waits=[("cc", cc_ag), ("mm", kv_free_mm[buf])],
                        sem=kvsem, n=16)
                    p.step("sync", lambda e, g=g, kb=kb, buf=buf: e.dma_start(
                        out=vblk[:, buf * 512:(buf + 1) * 512],
                        in_=hap(ago.ap(), VOFF + kb * 65536 + g * 128,
                                [(512, P), (AGN, 4), (1, 128)])), sem=kvsem, n=16)
                    kvv = (kvsem, p.cur(kvsem))
                    B0rel = max(kb - 2 * s, 0)
                    nB = 2 - B0rel
                    W = nB * 128
                    diag = kb - 2 * s if kb >= 2 * s else None
                    for r in range(4):
                        pa = it % 2
                        it += 1
                        for hl in range(4):
                            w = []
                            if hl == 0:
                                w = [kvv, ("se", sp_free_se[pa]),
                                     ("ve", norm_ve_prev)] + qhat_tp
                            hg = g * 4 + hl
                            p.step("tensor", lambda e, hl=hl, hg=hg, r=r, pa=pa,
                                   s=s, B0rel=B0rel, W=W, buf=buf: e.matmul(
                                psB[:, pa * 1024 + hl * 256: pa * 1024 + hl * 256 + W],
                                kblk[:, buf * 512 + r * 128: buf * 512 + (r + 1) * 128],
                                qhat[:, hg * TOWN + s * 256 + B0rel * 128:
                                     hg * TOWN + s * 256 + B0rel * 128 + W],
                                start=True, stop=True), waits=w,
                                sem=("mm" if hl == 3 else None))
                        mm_sc = p.cur("mm")
                        p.step("scalar", lambda e, pa=pa, B0rel=B0rel, nB=nB: e.activation(
                            ap_md(pblk[:, :], pa * 1024 + B0rel * 512,
                                  [(128, 4), (512, nB), (1, 128)]),
                            ap_md(psB[:, :], pa * 1024,
                                  [(256, 4), (128, nB), (1, 128)]),
                            AF.Exp, bias=misc[:, 5:6], scale=1.0 / 128.0),
                            waits=[("mm", mm_sc), ("ve", negc_ve),
                                   ("mm", pb_free_mm[pa])], sem="se", ptr=True)
                        sp_free_se[pa] = p.cur("se")
                        sev = p.cur("se")
                        avw = [("se", sev)]
                        if diag is not None:
                            p.step("vector", lambda e, pa=pa, diag=diag, r=r:
                                   e.tensor_tensor(
                                ap_md(pblk[:, :], pa * 1024 + diag * 512,
                                      [(128, 4), (1, 128)]),
                                ap_md(pblk[:, :], pa * 1024 + diag * 512,
                                      [(128, 4), (1, 128)]),
                                ap_md(tri[:, :], r * 128, [(0, 4), (1, 128)]),
                                ALU.mult), waits=[("se", sev)], sem="ve")
                            avw.append(("ve", p.cur("ve")))
                        for Brel in range(B0rel, 2):
                            st_ = first[Brel]
                            lastk = (kb == 2 * s + Brel and r == 3)
                            if st_:
                                avw = avw + [("ve", g_ev)]
                            p.step("tensor", lambda e, Brel=Brel, pa=pa, st_=st_,
                                   lastk=lastk, buf=buf, r=r: e.matmul(
                                psA[:, Brel * 512:(Brel + 1) * 512],
                                vblk[:, buf * 512 + r * 128: buf * 512 + (r + 1) * 128],
                                pblk[:, pa * 1024 + Brel * 512: pa * 1024 + (Brel + 1) * 512],
                                start=st_, stop=lastk), waits=avw, sem="mm")
                            p.step("tensor", lambda e, Brel=Brel, pa=pa, st_=st_,
                                   lastk=lastk: e.matmul(
                                psA[0:1, 1024 + Brel * 512: 1024 + (Brel + 1) * 512],
                                ones16[:, 0:1],
                                pblk[:, pa * 1024 + Brel * 512: pa * 1024 + (Brel + 1) * 512],
                                start=st_, stop=lastk), sem="mm")
                            first[Brel] = False
                        pb_free_mm[pa] = p.cur("mm")
                        kv_free_mm[buf] = p.cur("mm")
                mmv = p.cur("mm")
                for Brel in range(2):
                    p.step("vector", lambda e, g=g, Brel=Brel: e.tensor_copy(
                        sums_row[0:1, (g * 2 + Brel) * 512:(g * 2 + Brel + 1) * 512],
                        psA[0:1, 1024 + Brel * 512:1024 + (Brel + 1) * 512]),
                        waits=[("mm", mmv)], sem="ve")
                    p.step("vector", lambda e, g=g, s=s, Brel=Brel: e.tensor_copy(
                        ap_md(attn[:, :], g * 4 * TOWN + s * 256 + Brel * 128,
                              [(TOWN, 4), (1, 128)]),
                        psA[:, Brel * 512:(Brel + 1) * 512]), sem="ve")
            vev = p.cur("ve")
            p.step("gpsimd", lambda e: e.dma_start(
                out=rspread[:, :],
                in_=ap_md(sums_row[0:1, :], 0, [(32, P), (1, 32)])),
                waits=[("ve", vev)], sem="wag", n=16)
            p.step("vector", lambda e: e.reciprocal(rspread[:, :], rspread[:, :]),
                   waits=[("wag", p.cur("wag"))], sem="ve")
            vv = p.cur("ve")
            p.step("gpsimd", lambda e: e.dma_start(
                out=ap_md(r_row[0:1, :], 0, [(32, P), (1, 32)]),
                in_=rspread[:, :]), waits=[("ve", vv)], sem="wag", n=16)
            rv = p.cur("wag")
            for g in range(4):
                for Brel in range(2):
                    w = [("wag", rv), ("se", sp_free_se[0]), ("se", sp_free_se[1]),
                         ("ve", p.cur("ve"))]
                    p.step("tensor", lambda e, g=g, Brel=Brel: e.matmul(
                        psB[:, 0:512], ones16[0:1, 0:128],
                        r_row[0:1, (g * 2 + Brel) * 512:(g * 2 + Brel + 1) * 512],
                        start=True, stop=True), waits=w, sem="mm")
                    p.step("vector", lambda e: e.tensor_copy(
                        rb_sb[:, :], psB[:, 0:512]),
                        waits=[("mm", p.cur("mm"))], sem="ve")
                    p.step("vector", lambda e, g=g, s=s, Brel=Brel: e.tensor_tensor(
                        ap_md(attn[:, :], g * 4 * TOWN + s * 256 + Brel * 128,
                              [(TOWN, 4), (1, 128)]),
                        ap_md(attn[:, :], g * 4 * TOWN + s * 256 + Brel * 128,
                              [(TOWN, 4), (1, 128)]),
                        rb_sb[:, :], ALU.mult), sem="ve")
            norm_ve_prev = p.cur("ve")
        return wo_done

    def oproj_phase(wo_done):
        mmv, vev = p.cur("mm"), p.cur("ve")
        xrel = []
        for tb in range(NT):
            sn = rr_sem()
            p.step("sync", lambda e, tb=tb: e.dma_start(
                out=x[:, tb * D:(tb + 1) * D],
                in_=hap(xspill.ap(), tb * 128 * D, [(D, P), (1, D)])),
                waits=[("mm", mmv), ("ve", vev), ("wsp", p.cur("wsp"))],
                sem=sn, n=16)
            xrel.append((sn, p.cur(sn)))
        ev_prev = {}
        for tb in range(NT):
            ps = psA if tb % 2 == 0 else psB
            w = [("ww", wo_done), ("ve", vev)]
            if tb >= 2:
                w.append(("ve", ev_prev[tb - 2]))
            for c in range(16):
                lhs = attn[:, c * TOWN + tb * 128: c * TOWN + tb * 128 + 128]
                for ob in range(4):
                    p.step("tensor", lambda e, lhs=lhs, ps=ps, c=c, ob=ob: e.matmul(
                        ps[:, ob * 512:(ob + 1) * 512], lhs,
                        wo[:, c * D + ob * 512: c * D + ob * 512 + 512],
                        start=(c == 0), stop=False),
                        waits=(w if (c == 0 and ob == 0) else []), sem=None)
            for ob in range(4):
                p.step("tensor", lambda e, ps=ps, ob=ob: e.matmul(
                    ps[:, ob * 512:(ob + 1) * 512], ones16[0:1, 0:128],
                    bo_row[0:1, ob * 512:(ob + 1) * 512], start=False, stop=True),
                    sem="mm")
            mmv2 = p.cur("mm")
            for ob in range(4):
                p.step("vector", lambda e, ps=ps, tb=tb, ob=ob: e.scalar_tensor_tensor(
                    x[:, tb * D + ob * 512: tb * D + ob * 512 + 512],
                    ps[:, ob * 512:(ob + 1) * 512], 1.0,
                    x[:, tb * D + ob * 512: tb * D + ob * 512 + 512],
                    ALU.mult, ALU.add),
                    waits=[("mm", mmv2), xrel[tb]], sem="ve")
            ev_prev[tb] = p.cur("ve")
        return p.cur("mm")

    import contextlib
    with contextlib.ExitStack() as es:
        psA = es.enter_context(nc.psum_tensor("psA", [P, 2048], F32))
        psB = es.enter_context(nc.psum_tensor("psB", [P, 2048], F32))
        SEM = {}
        for sname in sems:
            SEM[sname] = es.enter_context(nc.semaphore(sname))
        block = es.enter_context(nc.Block())

        init_phase()
        ln_stats(True, p.cur("wx"))
        sve = ln_gmax_allreduce(ar1i, ar1o)
        _, tp_ffn1 = ffn(sve, store_after=(TRUNC == 1))
        if TRUNC >= 2:
            cc_ag, qhat_tp = qkv_phase(tp_ffn1)
        if TRUNC >= 3:
            wo_done = attention_phase(cc_ag, qhat_tp)
            mm_oproj = oproj_phase(wo_done)
            if TRUNC == 3:
                vv = p.cur("ve")
                for tb in range(NT):
                    p.step("sync", lambda e, tb=tb: e.dma_start(
                        out=hap(y_out, tb * 128 * D, [(D, P), (1, D)]),
                        in_=x[:, tb * D:(tb + 1) * D]), waits=[("ve", vv)],
                        sem="wst", n=16)
        if TRUNC >= 4:
            for c in range(16):
                p.step("gpsimd", lambda e, c=c: e.dma_start(
                    out=wfT[:, c * D:(c + 1) * D],
                    in_=hap(wfT_in, c * P * D, [(D, P), (1, D)])),
                    waits=[("mm", mm_oproj)], sem="ww", n=16)
            ln_stats(True, 0)
            sve2 = ln_gmax_allreduce(ar2i, ar2o)
            ffn(sve2, store_after=True, region_mm=mm_oproj)

        stv = p.cur("wst")
        p.barrier_wait("sync", [("wst", stv)])

        for ename, attr in [("tensor", block.tensor), ("vector", block.vector),
                            ("scalar", block.scalar), ("sync", block.sync),
                            ("gpsimd", block.gpsimd)]:
            def body(eng, ename=ename):
                for fn in p.q[ename]:
                    fn(eng, SEM)
            attr(body)
    return nc


def make_inputs(ins, c, TRUNC):
    j = c % 4
    b = c // 4
    f32 = np.float32
    xfull = np.asarray(ins["L0_attn_x"], f32)
    xc = np.ascontiguousarray(xfull[b, j::4, :])
    pos = np.arange(j, 4096, 4)
    hdhalf = np.arange(0, HD, 2)
    inv = (1.0 / 10000.0 ** (hdhalf / HD))
    fr = np.outer(pos, inv)
    cosf, sinf = np.cos(fr).astype(f32), np.sin(fr).astype(f32)
    cost = np.ascontiguousarray(
        cosf.reshape(NT, 128, 64).transpose(1, 0, 2).reshape(128, NT * 64))
    sint = np.ascontiguousarray(
        sinf.reshape(NT, 128, 64).transpose(1, 0, 2).reshape(128, NT * 64))
    t_ = np.arange(128)
    tri = np.zeros((128, 4 * 128), np.float16)
    for r in range(4):
        m = (t_[None, :] > t_[:, None]) | ((t_[None, :] == t_[:, None]) & (j >= r))
        tri[:, r * 128:(r + 1) * 128] = m.astype(np.float16)
    qns = np.asarray(ins["L0_attn_q_norm_scale"], f32).reshape(1, HD)
    kns = np.asarray(ins["L0_attn_k_norm_scale"], f32).reshape(1, HD)
    return {
        "x": xc,
        "wq": np.asarray(ins["L0_attn_wq"], f32),
        "wk": np.asarray(ins["L0_attn_wk"], f32),
        "wv": np.asarray(ins["L0_attn_wv"], f32),
        "wo": np.asarray(ins["L0_attn_wo"], f32),
        "wfT": np.ascontiguousarray(np.asarray(ins["L0_ffn_weight"], f32).T),
        "bq": np.asarray(ins["L0_attn_bq"], f32).reshape(1, D),
        "bk": np.asarray(ins["L0_attn_bk"], f32).reshape(1, 512),
        "bv": np.asarray(ins["L0_attn_bv"], f32).reshape(1, 512),
        "bo": np.asarray(ins["L0_attn_bo"], f32).reshape(1, D),
        "bf": np.asarray(ins["L0_ffn_bias"], f32).reshape(1, D),
        "ws": np.full((1, 16), np.asarray(ins["L0_ffn_weight_scale"], f32)),
        "qnsr": np.ascontiguousarray(np.broadcast_to(np.tile(qns, 16), (P, D))),
        "knsr": np.ascontiguousarray(np.broadcast_to(np.tile(kns, 4), (P, 512))),
        "qns": qns, "kns": kns,
        "cosn": cost, "sinn": sint, "tri": tri,
    }


def kernel(**ins):
    TRUNC = 4
    nc = build_nc(0, TRUNC)
    in_maps = [make_inputs(ins, c, TRUNC) for c in range(8)]
    res = run_bass_kernel_spmd(nc, in_maps, core_ids=list(range(8)))
    out = np.zeros((2, 4096, 2048), np.float32)
    for c in range(8):
        out[c // 4, (c % 4)::4, :] = res.results[c]["y"]
    return out


# version 04
# speedup vs baseline: 1.0091x; 1.0091x over previous
import numpy as np
import concourse.bass as bass
import concourse.mybir as mybir
from concourse.bass_utils import run_bass_kernel_spmd

F32 = mybir.dt.float32
F16 = mybir.dt.float16
BF16 = mybir.dt.bfloat16
FP8 = mybir.dt.float8e4
AF = mybir.ActivationFunctionType
ALU = mybir.AluOpType
AX = mybir.AxisListType

P = 128
D = 2048
TOWN = 1024
NT = 8
HEADS, GROUPS, HD = 16, 4, 128
NB = 8
EPS_LN = 1e-5
EPS_RMS = 1e-6

ENGINES = ("tensor", "vector", "scalar", "sync", "gpsimd")


def ap_md(t_ap, offset_elems, dims):
    return bass.AP(tensor=t_ap.tensor, offset=t_ap.offset + offset_elems,
                   ap=[list(t_ap.ap[0])] + [[s, c] for (s, c) in dims])


def hap(t_ap, offset_elems, dims):
    return bass.AP(tensor=t_ap.tensor, offset=t_ap.offset + offset_elems,
                   ap=[[s, c] for (s, c) in dims])


ENG_SEM = {"tensor": "mm", "vector": "ve", "scalar": "se", "gpsimd": "gp"}


class Prog:

    def __init__(self):
        self.q = {e: [] for e in ENGINES}
        self.count = {}
        self.waited = {e: {} for e in ENGINES}
        self.force_next = {e: False for e in ENGINES}

    def inc(self, sem, n=1):
        self.count[sem] = self.count.get(sem, 0) + n
        return self.count[sem]

    def cur(self, sem):
        return self.count.get(sem, 0)

    def step(self, eng, build, waits=(), sem=None, n=1, ptr=False):
        waits = list(waits)
        if eng == "vector":
            ptr = True
        if (ptr or self.force_next[eng]) and eng in ENG_SEM:
            waits.append((ENG_SEM[eng], self.cur(ENG_SEM[eng])))
        self.force_next[eng] = bool(ptr)
        wl = []
        for s, v in waits:
            if v > 0 and self.waited[eng].get(s, -1) < v:
                wl.append((s, v))
                self.waited[eng][s] = v
        tgt = self.inc(sem, n) if sem else None
        def fn(e, SEM, wl=wl, sem=sem, n=n):
            for s, v in wl:
                e.wait_ge(SEM[s], v)
            inst = build(e)
            if sem is not None:
                if inst is None:
                    raise RuntimeError("step with sem must return instruction")
                inst.then_inc(SEM[sem], n)
        self.q[eng].append(fn)
        return tgt

    def barrier_wait(self, eng, waits):
        self.step(eng, lambda e: None, waits=waits)


def build_nc(j, TRUNC):
    nc = bass.Bass("TRN2", target_bir_lowering=False)

    x_in = nc.dram_tensor("x", [TOWN, D], F32, kind="ExternalInput").ap()
    wq_in = nc.dram_tensor("wq", [D, D], F32, kind="ExternalInput").ap()
    wk_in = nc.dram_tensor("wk", [D, 512], F32, kind="ExternalInput").ap()
    wv_in = nc.dram_tensor("wv", [D, 512], F32, kind="ExternalInput").ap()
    wo_in = nc.dram_tensor("wo", [D, D], F32, kind="ExternalInput").ap()
    wfT_in = nc.dram_tensor("wfT", [D, D], F32, kind="ExternalInput").ap()
    bq_in = nc.dram_tensor("bq", [1, D], F32, kind="ExternalInput").ap()
    bk_in = nc.dram_tensor("bk", [1, 512], F32, kind="ExternalInput").ap()
    bv_in = nc.dram_tensor("bv", [1, 512], F32, kind="ExternalInput").ap()
    bo_in = nc.dram_tensor("bo", [1, D], F32, kind="ExternalInput").ap()
    bf_in = nc.dram_tensor("bf", [1, D], F32, kind="ExternalInput").ap()
    ws_in = nc.dram_tensor("ws", [1, 16], F32, kind="ExternalInput").ap()
    qnsr_in = nc.dram_tensor("qnsr", [P, D], F32, kind="ExternalInput").ap()
    knsr_in = nc.dram_tensor("knsr", [P, 512], F32, kind="ExternalInput").ap()
    qns_in = nc.dram_tensor("qns", [1, HD], F32, kind="ExternalInput").ap()
    kns_in = nc.dram_tensor("kns", [1, HD], F32, kind="ExternalInput").ap()
    cos_in = nc.dram_tensor("cosn", [P, NT * 64], F32, kind="ExternalInput").ap()
    sin_in = nc.dram_tensor("sinn", [P, NT * 64], F32, kind="ExternalInput").ap()
    tri_in = nc.dram_tensor("tri", [P, 4 * 128], F16, kind="ExternalInput").ap()

    y_out = nc.dram_tensor("y", [TOWN, D], F32, kind="ExternalOutput").ap()

    ar1i = nc.dram_tensor("ar1i", [1, 16], F32)
    ar1o = nc.dram_tensor("ar1o", [1, 16], F32)
    ar2i = nc.dram_tensor("ar2i", [1, 16], F32)
    ar2o = nc.dram_tensor("ar2o", [1, 16], F32)
    AGN = GROUPS * HD * TOWN + TOWN * 512
    agi = nc.dram_tensor("agi", [1, AGN], F16)
    ago = nc.dram_tensor("ago", [4, AGN], F16)
    VOFF = GROUPS * HD * TOWN

    BASE = ((nc.sbuf_base + 63) // 64) * 64
    off = [BASE]
    def alloc(name, shape, dtype, nbytes):
        t = nc.alloc_sbuf_tensor_at(name, shape, dtype, offset=off[0])
        off[0] += (nbytes + 63) // 64 * 64
        return t
    def at(name, shape, dtype, byteoff):
        return nc.alloc_sbuf_tensor_at(name, shape, dtype, offset=BASE + byteoff)

    ones16 = alloc("ones16", [P, 128], F16, 256)
    onesbf = alloc("onesbf", [P, 128], BF16, 256)
    ones32 = alloc("ones32", [1, 128], F32, 512)
    stats = alloc("stats", [P, 96], F32, 384)
    misc = alloc("misc", [P, 16], F32, 64)
    qstat = alloc("qstat", [P, 64], F32, 256)
    grow = alloc("grow", [1, 128], F32, 512)
    g1 = alloc("g1", [1, 16], F32, 64)
    cosn = alloc("cosn_sb", [P, NT * 64], F32, NT * 64 * 4)
    sinn = alloc("sinn_sb", [P, NT * 64], F32, NT * 64 * 4)
    tri = alloc("tri_sb", [P, 4 * 128], F16, 4 * 128 * 2)
    bq_row = alloc("bq_row", [1, D], F16, D * 2)
    bo_row = alloc("bo_row", [1, D], F16, D * 2)
    bk_row = alloc("bk_row", [1, 512], F16, 512 * 2)
    bv_row = alloc("bv_row", [1, 512], F16, 512 * 2)
    bf_row = alloc("bf_row", [1, D], BF16, D * 2)
    qnsr = alloc("qnsr_sb", [P, D], F16, D * 2)
    knsr = alloc("knsr_sb", [P, 512], F16, 512 * 2)
    qnsraw = alloc("qns_raw", [1, HD], F32, 512)
    knsraw = alloc("kns_raw", [1, HD], F32, 512)
    wsrow = alloc("ws_sb", [1, 16], F32, 64)
    cbuf = alloc("cbuf", [1, 16], F16, 64)
    S_END = off[0] - BASE
    assert S_END <= 29 * 1024, S_END

    A_OFF = 29 * 1024
    W_OFF = A_OFF + 32 * 1024
    X_OFF = W_OFF + 64 * 1024
    C_OFF = X_OFF + 64 * 1024
    TOP = C_OFF + 17 * 1024
    assert BASE + TOP <= nc.sbuf_top, (BASE, TOP, nc.sbuf_top)

    xqf = at("xqf", [P, 16 * TOWN], BF16, A_OFF)
    xhf = at("xhf", [P, 16 * TOWN], F16, A_OFF)
    attn = at("attn", [P, 16 * TOWN], F16, A_OFF)
    wfT = at("wfT", [P, 16 * D], BF16, W_OFF)
    wqh = at("wqh", [P, 16 * 1024], F16, W_OFF)
    wk = at("wk_sb", [P, 16 * 512], F16, W_OFF + 32 * 1024)
    wv = at("wv_sb", [P, 16 * 512], F16, W_OFF + 48 * 1024)
    wo = at("wo_sb", [P, 16 * D], F16, W_OFF)
    x = at("x_sb", [P, NT * D], F32, X_OFF)
    qhat = at("qhat", [P, HEADS * TOWN], F16, X_OFF)
    kown = at("kown", [P, GROUPS * TOWN], F16, X_OFF + 32 * 1024)
    vown = at("vown", [P, NT * 512], F16, X_OFF + 40 * 1024)
    qscr = at("qscr", [P, D], F32, X_OFF + 48 * 1024)
    kblk = at("kblk", [P, 2 * 512], F16, X_OFF + 32 * 1024)
    vblk = at("vblk", [P, 2 * 512], F16, X_OFF + 34 * 1024)
    pblk = at("pblk", [P, 2 * 1024], F16, X_OFF + 36 * 1024)
    sums_row = at("sums_row", [1, 4096], F16, X_OFF + 40 * 1024)
    r_row = at("r_row", [1, 4096], F16, X_OFF + 48 * 1024)
    rspread = at("rspread", [P, 32], F32, X_OFF + 56 * 1024)
    rbrep = at("rbrep", [P, 1024], F16, X_OFF + 56 * 1024 + 256)
    scr32 = at("scr32", [P, D], F32, C_OFF)
    scr8 = at("scr8", [P, D], FP8, C_OFF + 8 * 1024)
    scr16 = at("scr16", [P, D], BF16, C_OFF + 10 * 1024)
    scr16h = at("scr16h", [P, D], F16, C_OFF + 10 * 1024)
    scr16q = at("scr16q", [P, 1024], F16, C_OFF + 14 * 1024)
    scr16k = at("scr16k", [P, 512], F16, C_OFF + 16 * 1024)

    xspill = nc.dram_tensor("xspill", [TOWN, D], F32)

    p = Prog()
    state = {}
    sems = ["wx", "ww", "wsp", "wst", "wag", "mm", "ve", "se",
            "gp", "cc"] + ["wtp%d" % i for i in range(8)]
    def rr_sem():
        sn = "wtp%d" % (tp_idx[0] % 8)
        tp_idx[0] += 1
        return sn
    tp_idx = [0]
    def tp_step(build, waits=()):
        sn = "wtp%d" % (tp_idx[0] % 8)
        tp_idx[0] += 1
        v = p.step("sync", build, waits=waits, sem=sn, n=16)
        return (sn, v)
    def tp_all():
        return [("wtp%d" % i, p.cur("wtp%d" % i)) for i in range(8)]

    def init_phase():
        p.step("sync", lambda e: e.dma_start(
            out=ap_md(x[:, :], 0, [(D, NT), (1, D)]),
            in_=hap(x_in, 0, [(D, P), (P * D, NT), (1, D)])), sem="wx", n=16)
        for nm, dst, src in [("cos", cosn, cos_in), ("sin", sinn, sin_in)]:
            p.step("sync", lambda e, d=dst, s=src: e.dma_start(out=d[:, :], in_=s),
                   sem="wx", n=16)
        p.step("sync", lambda e: e.dma_start(out=tri[:, :], in_=tri_in), sem="wx", n=16)
        for dst, src in [(wsrow, ws_in), (qnsraw, qns_in),
                         (knsraw, kns_in)]:
            p.step("sync", lambda e, d=dst, s=src: e.dma_start(out=d[:, :], in_=s),
                   sem="wx", n=16)
        for dst, src in [(bq_row, bq_in), (bk_row, bk_in), (bv_row, bv_in),
                         (bo_row, bo_in), (qnsr, qnsr_in), (knsr, knsr_in)]:
            p.step("gpsimd", lambda e, d=dst, s=src: e.dma_start(out=d[:, :], in_=s),
                   sem="ww", n=16)
        for c in range(16):
            p.step("gpsimd", lambda e, c=c: e.dma_start(
                out=wfT[:, c * D:(c + 1) * D],
                in_=hap(wfT_in, c * P * D, [(D, P), (1, D)])), sem="ww", n=16)
        for t, val in [(ones16, 1.0), (onesbf, 1.0), (ones32, 1.0), (g1, 0.0), (misc, 0.0)]:
            p.step("gpsimd", lambda e, t=t, v=val: e.memset(t[:, :], v), sem="gp")
        state["gp_init"] = p.cur("gp")
        p.step("sync", lambda e: e.dma_start(out=scr32[0:1, 0:D], in_=bf_in),
               sem="wx", n=16)
        wxv = p.cur("wx")
        p.step("vector", lambda e: e.reciprocal(misc[0:1, 3:4], wsrow[0:1, 0:1]),
               waits=[("wx", wxv), ("gp", state["gp_init"])], sem="ve")
        p.step("vector", lambda e: e.tensor_single_scalar(
            scr32[0:1, 0:D], scr32[0:1, 0:D], misc[0:1, 3:4], ALU.mult),
            sem="ve", ptr=True)
        p.step("vector", lambda e: e.tensor_copy(bf_row[:, :], scr32[0:1, 0:D]), sem="ve")
        wwv = p.cur("ww")
        p.step("tensor", lambda e: e.matmul(
            psB[:, 0:1], ones32[0:1, 0:128], wsrow[0:1, 0:1], start=True, stop=True),
            waits=[("wx", wxv), ("gp", p.cur("gp"))], sem="mm")
        p.step("vector", lambda e: e.tensor_scalar_mul(misc[:, 4:5], psB[:, 0:1], 0.5),
               waits=[("mm", p.cur("mm"))], sem="ve")
        state["ve_init"] = p.cur("ve")

    def ln_stats_tile(tb, minmax, w):
        sl = x[:, tb * D:(tb + 1) * D]
        p.step("scalar", lambda e, sl=sl, tb=tb: e.activation(
            scr16[:, :], sl, AF.Identity,
            accum_out=stats[:, tb:tb + 1]), waits=w, sem="se", ptr=True)
        p.step("scalar", lambda e, sl=sl, tb=tb: e.activation(
            scr16[:, :], sl, AF.Square,
            accum_out=stats[:, 8 + tb:9 + tb]), sem="se", ptr=True)
        if minmax:
            p.step("vector", lambda e, sl=sl, tb=tb: e.tensor_reduce(
                stats[:, 16 + tb:17 + tb], sl, axis=AX.X, op=ALU.min),
                waits=[x_ for x_ in w if x_[0] != "ve"], sem="ve")
            p.step("vector", lambda e, sl=sl, tb=tb: e.tensor_reduce(
                stats[:, 24 + tb:25 + tb], sl, axis=AX.X, op=ALU.max), sem="ve")

    def ln_stats(minmax, xwait, extra=()):
        for tb in range(NT):
            w0 = ([("wx", xwait)] + list(extra)) if tb == 0 else []
            ln_stats_tile(tb, minmax, w0)
        p.step("vector", lambda e: e.tensor_scalar_mul(
            stats[:, 32:40], stats[:, 0:8], 1.0 / D),
            waits=[("se", p.cur("se"))], sem="ve")
        p.step("vector", lambda e: e.tensor_scalar_mul(
            stats[:, 48:56], stats[:, 8:16], 1.0 / D), sem="ve")
        p.step("vector", lambda e: e.tensor_mul(
            stats[:, 56:64], stats[:, 32:40], stats[:, 32:40]), sem="ve")
        p.step("vector", lambda e: e.tensor_sub(
            stats[:, 48:56], stats[:, 48:56], stats[:, 56:64]), sem="ve")
        p.step("vector", lambda e: e.tensor_scalar_add(
            stats[:, 48:56], stats[:, 48:56], EPS_LN), sem="ve")
        vv = p.cur("ve")
        p.step("scalar", lambda e: e.activation(
            stats[:, 56:64], stats[:, 48:56], AF.Sqrt), waits=[("ve", vv)], sem="se")
        sv = p.cur("se")
        p.step("vector", lambda e: e.reciprocal(stats[:, 64:72], stats[:, 56:64]),
               waits=[("se", sv)], sem="ve")
        p.step("vector", lambda e: e.tensor_mul(
            stats[:, 64:72], stats[:, 64:72], stats[:, 48:56]), sem="ve")
        p.step("vector", lambda e: e.tensor_add(
            stats[:, 64:72], stats[:, 64:72], stats[:, 56:64]), sem="ve")
        p.step("vector", lambda e: e.tensor_scalar_mul(
            stats[:, 56:64], stats[:, 64:72], 0.5), sem="ve")
        p.step("vector", lambda e: e.reciprocal(stats[:, 40:48], stats[:, 56:64]),
               sem="ve")

    def ln_gmax_allreduce(ar_in_t, ar_out_t):
        p.step("vector", lambda e: e.tensor_sub(
            stats[:, 56:64], stats[:, 24:32], stats[:, 32:40]), sem="ve")
        p.step("vector", lambda e: e.tensor_sub(
            stats[:, 64:72], stats[:, 32:40], stats[:, 16:24]), sem="ve")
        p.step("vector", lambda e: e.tensor_max(
            stats[:, 88:96], stats[:, 56:64], stats[:, 64:72]), sem="ve")
        p.step("vector", lambda e: e.tensor_mul(
            stats[:, 88:96], stats[:, 88:96], stats[:, 40:48]), sem="ve")
        p.step("vector", lambda e: e.reduce_max(
            misc[:, 0:1], stats[:, 88:96], axis=AX.X), sem="ve")
        vv = p.cur("ve")
        p.step("sync", lambda e: e.dma_start(out=grow[0:1, :],
               in_=misc[:, 0:1]), waits=[("ve", vv)], sem="wsp", n=16)
        p.step("vector", lambda e: e.reduce_max(g1[0:1, 0:1], grow[0:1, :],
               axis=AX.X), waits=[("wsp", p.cur("wsp"))], sem="ve")
        vv = p.cur("ve")
        p.step("gpsimd", lambda e: e.dma_start(out=ar_in_t.ap(), in_=g1[:, :]),
               waits=[("ve", vv)], sem="wag", n=16)
        p.step("gpsimd", lambda e: e.collective_compute(
            "AllReduce", ALU.max, replica_groups=[list(range(8))],
            ins=[ar_in_t.ap()], outs=[ar_out_t.ap()]),
            waits=[("wag", p.cur("wag"))], sem="cc")
        p.step("gpsimd", lambda e: e.dma_start(out=g1[:, :], in_=ar_out_t.ap()),
               waits=[("cc", p.cur("cc"))], sem="wag", n=16)
        wagv = p.cur("wag")
        p.step("vector", lambda e: e.tensor_scalar(
            g1[0:1, 1:2], g1[0:1, 0:1], 1.0 / 448.0, 1e-7, ALU.mult, ALU.max),
            waits=[("wag", wagv)], sem="ve")
        p.step("vector", lambda e: e.reciprocal(g1[0:1, 2:3], g1[0:1, 1:2]), sem="ve")
        vv = p.cur("ve")
        p.step("tensor", lambda e: e.matmul(
            psB[:, 0:1], ones32[0:1, 0:128], g1[0:1, 2:3], start=True, stop=True),
            waits=[("ve", vv)], sem="mm")
        p.step("vector", lambda e: e.tensor_scalar_mul(
            misc[:, 2:3], psB[:, 0:1], 0.5), waits=[("mm", p.cur("mm"))], sem="ve")
        p.step("vector", lambda e: e.tensor_single_scalar(
            stats[:, 72:80], stats[:, 40:48], misc[:, 2:3], ALU.mult),
            sem="ve", ptr=True)
        p.step("vector", lambda e: e.tensor_mul(
            stats[:, 80:88], stats[:, 32:40], stats[:, 72:80]), sem="ve")
        p.step("vector", lambda e: e.tensor_scalar_mul(
            stats[:, 80:88], stats[:, 80:88], -1.0), sem="ve")
        return p.cur("ve")

    def ffn(scale_ready_ve, store_after, region_mm=0):
        tp_done = []
        for tb in range(NT):
            sl = x[:, tb * D:(tb + 1) * D]
            qw = [("ve", scale_ready_ve)]
            if tb == 0:
                qw.append(("mm", region_mm))
            else:
                qw.append(tp_done[tb - 1])
            p.step("vector", lambda e, sl=sl, tb=tb: e.tensor_scalar(
                scr8[:, :], sl, stats[:, 72 + tb:73 + tb], stats[:, 80 + tb:81 + tb],
                ALU.mult, ALU.add), waits=qw, sem="ve", ptr=True)
            p.step("vector", lambda e: e.tensor_scalar_mul(
                scr16[:, :], scr8[:, :], 2.0), sem="ve")
            vv = p.cur("ve")
            tp_done.append(tp_step(lambda e, tb=tb: e.dma_start(
                out=ap_md(xqf[:, :], tb * 128, [(TOWN, 16), (1, 128)]),
                in_=scr16[:, :], transpose=True),
                waits=[("ve", vv), ("mm", region_mm)]))
        wwv = p.cur("ww")
        ev_of_tb = []
        for tb in range(NT):
            ps = psA if tb % 2 == 0 else psB
            w = [tp_done[tb], ("ww", wwv)]
            if tb >= 2:
                w.append(("ve", ev_of_tb[tb - 2]))
            for c in range(16):
                lhs = xqf[:, c * TOWN + tb * 128: c * TOWN + tb * 128 + 128]
                for ob in range(4):
                    p.step("tensor", lambda e, lhs=lhs, ps=ps, c=c, ob=ob: e.matmul(
                        ps[:, ob * 512:(ob + 1) * 512], lhs,
                        wfT[:, c * D + ob * 512: c * D + ob * 512 + 512],
                        start=(c == 0), stop=False),
                        waits=(w if (c == 0 and ob == 0) else []),
                        sem=("mm" if (c == 15 and ob == 3) else None))
            for ob in range(4):
                p.step("tensor", lambda e, ps=ps, ob=ob: e.matmul(
                    ps[:, ob * 512:(ob + 1) * 512], onesbf[0:1, 0:128],
                    bf_row[0:1, ob * 512:(ob + 1) * 512], start=False, stop=True),
                    waits=[("ve", state["ve_init"])], sem="mm")
            mmv = p.cur("mm")
            for ob in range(4):
                p.step("vector", lambda e, ps=ps, tb=tb, ob=ob: e.scalar_tensor_tensor(
                    x[:, tb * D + ob * 512: tb * D + ob * 512 + 512],
                    ps[:, ob * 512:(ob + 1) * 512], misc[:, 4:5],
                    x[:, tb * D + ob * 512: tb * D + ob * 512 + 512],
                    ALU.mult, ALU.add),
                    waits=[("mm", mmv)], sem="ve", ptr=True)
            ev_of_tb.append(p.cur("ve"))
            if store_after:
                vv = p.cur("ve")
                p.step("sync", lambda e, tb=tb: e.dma_start(
                    out=hap(y_out, tb * 128 * D, [(D, P), (1, D)]),
                    in_=x[:, tb * D:(tb + 1) * D]), waits=[("ve", vv)],
                    sem="wst", n=16)
        return ev_of_tb, tp_done

    def rms_rope_transpose(src, nheads, scol, rcol, tcol, nsr, nsoff, tb, dst,
                           dstoff, dst16, h16_free):
        W = nheads * HD
        p.step("vector", lambda e: e.scalar_tensor_tensor(
            scr32[:, 0:W], src, 1.0, src, ALU.mult, ALU.mult),
            waits=[h16_free], sem="ve")
        p.step("vector", lambda e: e.reduce_sum(
            qstat[:, scol:scol + nheads],
            ap_md(scr32[:, :], 0, [(HD, nheads), (1, HD)]), axis=AX.X), sem="ve")
        p.step("vector", lambda e: e.tensor_scalar(
            qstat[:, scol:scol + nheads], qstat[:, scol:scol + nheads],
            1.0 / HD, EPS_RMS, ALU.mult, ALU.add), sem="ve")
        vv = p.cur("ve")
        p.step("scalar", lambda e: e.activation(
            qstat[:, rcol:rcol + nheads], qstat[:, scol:scol + nheads], AF.Sqrt),
            waits=[("ve", vv)], sem="se")
        sv = p.cur("se")
        p.step("vector", lambda e: e.reciprocal(
            qstat[:, tcol:tcol + nheads], qstat[:, rcol:rcol + nheads]),
            waits=[("se", sv)], sem="ve")
        p.step("vector", lambda e: e.tensor_mul(
            qstat[:, tcol:tcol + nheads], qstat[:, tcol:tcol + nheads],
            qstat[:, scol:scol + nheads]), sem="ve")
        p.step("vector", lambda e: e.tensor_add(
            qstat[:, tcol:tcol + nheads], qstat[:, tcol:tcol + nheads],
            qstat[:, rcol:rcol + nheads]), sem="ve")
        p.step("vector", lambda e: e.tensor_scalar_mul(
            qstat[:, rcol:rcol + nheads], qstat[:, tcol:tcol + nheads],
            0.5), sem="ve")
        p.step("vector", lambda e: e.reciprocal(
            qstat[:, rcol:rcol + nheads], qstat[:, rcol:rcol + nheads]), sem="ve")
        for hl in range(nheads):
            p.step("vector", lambda e, hl=hl: e.tensor_single_scalar(
                ap_md(src, hl * HD, [(1, HD)]), ap_md(src, hl * HD, [(1, HD)]),
                qstat[:, rcol + hl:rcol + hl + 1], ALU.mult), sem="ve")
        p.step("vector", lambda e: e.tensor_mul(
            src, src, nsr[:, nsoff:nsoff + W]), sem="ve")
        cosb = ap_md(cosn[:, :], tb * 64, [(0, nheads), (1, 64)])
        sinb = ap_md(sinn[:, :], tb * 64, [(0, nheads), (1, 64)])
        a3 = ap_md(src, 0, [(HD, nheads), (1, 64)])
        b3 = ap_md(src, 64, [(HD, nheads), (1, 64)])
        t1 = ap_md(scr32[:, :], 1024, [(64, nheads), (1, 64)])
        t2 = ap_md(scr32[:, :], 1024 + nheads * 64, [(64, nheads), (1, 64)])
        out_re = ap_md(dst16[:, :], 0, [(HD, nheads), (1, 64)])
        out_im = ap_md(dst16[:, :], 64, [(HD, nheads), (1, 64)])
        p.step("vector", lambda e: e.tensor_tensor(t1, a3, cosb, ALU.mult), sem="ve")
        p.step("vector", lambda e: e.tensor_tensor(t2, b3, sinb, ALU.mult), sem="ve")
        p.step("vector", lambda e: e.tensor_tensor(out_re, t1, t2, ALU.subtract), sem="ve")
        p.step("vector", lambda e: e.tensor_tensor(t1, a3, sinb, ALU.mult), sem="ve")
        p.step("vector", lambda e: e.tensor_tensor(t2, b3, cosb, ALU.mult), sem="ve")
        p.step("vector", lambda e: e.tensor_tensor(out_im, t1, t2, ALU.add), sem="ve")
        vv = p.cur("ve")
        return tp_step(lambda e: e.dma_start(
            out=ap_md(dst, dstoff + tb * 128, [(TOWN, nheads), (1, 128)]),
            in_=dst16[:, :], transpose=True),
            waits=[("ve", vv)])

    def qkv_phase(tp_ffn1):
        ln_stats(False, 0, extra=[("ve", p.cur("ve")), tp_ffn1[-1]])
        p.step("vector", lambda e: e.tensor_mul(
            stats[:, 80:88], stats[:, 32:40], stats[:, 40:48]), sem="ve")
        p.step("vector", lambda e: e.tensor_scalar_mul(
            stats[:, 80:88], stats[:, 80:88], -1.0), sem="ve")
        sve = p.cur("ve")
        mm_ffn1 = p.cur("mm")
        for c in range(16):
            p.step("gpsimd", lambda e, c=c: e.dma_start(
                out=wqh[:, c * 1024:(c + 1) * 1024],
                in_=hap(wq_in, c * P * D, [(D, P), (1, 1024)])),
                waits=[("mm", mm_ffn1)], sem="ww", n=16)
            p.step("gpsimd", lambda e, c=c: e.dma_start(
                out=wk[:, c * 512:(c + 1) * 512],
                in_=hap(wk_in, c * P * 512, [(512, P), (1, 512)])), sem="ww", n=16)
            p.step("gpsimd", lambda e, c=c: e.dma_start(
                out=wv[:, c * 512:(c + 1) * 512],
                in_=hap(wv_in, c * P * 512, [(512, P), (1, 512)])), sem="ww", n=16)
        ww_A = p.cur("ww")
        tpxh = []
        for tb in range(NT):
            sl = x[:, tb * D:(tb + 1) * D]
            qw = [("ve", sve)]
            qw.append(tp_ffn1[-1] if tb == 0 else tpxh[tb - 1])
            p.step("vector", lambda e, sl=sl, tb=tb: e.tensor_scalar(
                scr16h[:, :], sl, stats[:, 40 + tb:41 + tb],
                stats[:, 80 + tb:81 + tb], ALU.mult, ALU.add), waits=qw,
                sem="ve", ptr=True)
            vv = p.cur("ve")
            tpxh.append(tp_step(lambda e, tb=tb: e.dma_start(
                out=ap_md(xhf[:, :], tb * 128, [(TOWN, 16), (1, 128)]),
                in_=scr16h[:, :], transpose=True), waits=[("ve", vv)]))
            p.step("sync", lambda e, tb=tb: e.dma_start(
                out=hap(xspill.ap(), tb * 128 * D, [(D, P), (1, D)]),
                in_=x[:, tb * D:(tb + 1) * D]), sem="wsp", n=16)
        spill_done = p.cur("wsp")
        last_tp = {"q": tpxh[-1], "k": tpxh[-1]}
        ev_prev = {}
        for half in range(2):
            if half == 1:
                p.step("gpsimd", lambda e: e.collective_compute(
                    "AllGather", ALU.bypass,
                    replica_groups=[[0, 1, 2, 3], [4, 5, 6, 7]],
                    ins=[agi.ap()], outs=[ago.ap()]),
                    waits=[("wsp", p.cur("wsp"))], sem="cc")
                mm_passA = p.cur("mm")
                for c in range(16):
                    p.step("gpsimd", lambda e, c=c: e.dma_start(
                        out=wqh[:, c * 1024:(c + 1) * 1024],
                        in_=hap(wq_in, c * P * D + 1024, [(D, P), (1, 1024)])),
                        waits=[("mm", mm_passA)], sem="ww", n=16)
            wwv = p.cur("ww")
            for tb in range(NT):
                gt = half * NT + tb
                ps = psA if tb % 2 == 0 else psB
                w = [tpxh[tb], ("ww", wwv)]
                if gt >= 2:
                    w.append(("ve", ev_prev[gt - 2]))
                nmm = 4 if half == 0 else 2
                for c in range(16):
                    lhs = xhf[:, c * TOWN + tb * 128: c * TOWN + tb * 128 + 128]
                    outs = [ps[:, 0:512], ps[:, 512:1024]]
                    rhss = [wqh[:, c * 1024:c * 1024 + 512],
                            wqh[:, c * 1024 + 512:(c + 1) * 1024]]
                    if half == 0:
                        outs += [ps[:, 1024:1536], ps[:, 1536:2048]]
                        rhss += [wk[:, c * 512:(c + 1) * 512],
                                 wv[:, c * 512:(c + 1) * 512]]
                    for i in range(nmm):
                        p.step("tensor", lambda e, o=outs[i], lhs=lhs, r=rhss[i],
                               c=c: e.matmul(o, lhs, r, start=(c == 0), stop=False),
                               waits=(w if (c == 0 and i == 0) else []),
                               sem=None)
                boff = 0 if half == 0 else 1024
                bl = [(ps[:, 0:512], bq_row[0:1, boff:boff + 512]),
                      (ps[:, 512:1024], bq_row[0:1, boff + 512:boff + 1024])]
                if half == 0:
                    bl += [(ps[:, 1024:1536], bk_row[0:1, :]),
                           (ps[:, 1536:2048], bv_row[0:1, :])]
                for o, b in bl:
                    p.step("tensor", lambda e, o=o, b=b: e.matmul(
                        o, ones16[0:1, 0:128], b, start=False, stop=True), sem="mm")
                mmv = p.cur("mm")
                p.step("vector", lambda e, ps=ps: e.tensor_copy(
                    qscr[:, 0:1024], ps[:, 0:1024]),
                    waits=[("mm", mmv), ("wsp", spill_done)], sem="ve")
                if half == 0:
                    p.step("vector", lambda e, ps=ps: e.tensor_copy(
                        qscr[:, 1024:1536], ps[:, 1024:1536]), sem="ve")
                    p.step("vector", lambda e, ps=ps, tb=tb: e.tensor_copy(
                        vown[:, tb * 512:(tb + 1) * 512], ps[:, 1536:2048]), sem="ve")
                ev_prev[gt] = p.cur("ve")
                last_tp["q"] = rms_rope_transpose(
                    qscr[:, 0:1024], 8, 0, 16, 32, qnsr, half * 1024,
                    tb, qhat[:, :], half * 8 * TOWN, scr16q, last_tp["q"])
                if half == 0:
                    last_tp["k"] = rms_rope_transpose(
                        qscr[:, 1024:1536], 4, 8, 40, 56, knsr, 0,
                        tb, kown[:, :], 0, scr16k, last_tp["k"])
                    p.step("sync", lambda e, tb=tb: e.dma_start(
                        out=hap(agi.ap(), tb * 128,
                                [(TOWN, P), (131072, 4), (1, 128)]),
                        in_=ap_md(kown[:, :], tb * 128, [(TOWN, 4), (1, 128)])),
                        waits=[last_tp["k"]], sem="wsp", n=16)
                    p.step("sync", lambda e, tb=tb: e.dma_start(
                        out=hap(agi.ap(), VOFF + tb * 128 * 512,
                                [(512, P), (1, 512)]),
                        in_=ap_md(vown[:, :], tb * 512, [(1, 512)])),
                        waits=[("ve", ev_prev[tb])], sem="wsp", n=16)
        qhat_tp = tp_all()
        return p.cur("cc"), qhat_tp

    def attention_phase(cc_ag, qhat_tp):
        p.step("vector", lambda e: e.reduce_max(
            g1[0:1, 4:5], qnsraw[0:1, :], axis=AX.X, apply_absolute_value=True),
            sem="ve")
        p.step("vector", lambda e: e.reduce_max(
            g1[0:1, 5:6], knsraw[0:1, :], axis=AX.X, apply_absolute_value=True),
            sem="ve")
        p.step("vector", lambda e: e.tensor_mul(
            g1[0:1, 6:7], g1[0:1, 4:5], g1[0:1, 5:6]), sem="ve")
        p.step("vector", lambda e: e.tensor_copy(cbuf[0:1, 0:1], g1[0:1, 6:7]),
               sem="ve")
        vv = p.cur("ve")
        p.step("tensor", lambda e: e.matmul(
            psB[:, 0:1], ones16[0:1, 0:128], cbuf[0:1, 0:1], start=True, stop=True),
            waits=[("ve", vv)], sem="mm")
        p.step("vector", lambda e: e.tensor_scalar_mul(
            misc[:, 5:6], psB[:, 0:1], -1.0), waits=[("mm", p.cur("mm"))], sem="ve")
        negc_ve = p.cur("ve")
        mm_qkv = p.cur("mm")
        for c in range(16):
            p.step("gpsimd", lambda e, c=c: e.dma_start(
                out=wo[:, c * D:(c + 1) * D],
                in_=hap(wo_in, c * P * D, [(D, P), (1, D)])),
                waits=[("mm", mm_qkv)], sem="ww", n=16)
        wo_done = p.cur("ww")

        ikv = 0
        it = 0
        kv_done = {}
        kv_free_mm = {0: 0, 1: 0}
        sp_free_se = {0: 0, 1: 0}
        pb_free_mm = {0: 0, 1: 0}
        for s in range(4):
            for g in range(4):
                g_ev = p.cur("ve")
                first = {0: True, 1: True}
                for kb in range(2 * s + 2):
                    buf = ikv % 2
                    ikv += 1
                    kvsem = rr_sem()
                    p.step("sync", lambda e, g=g, kb=kb, buf=buf: e.dma_start(
                        out=kblk[:, buf * 512:(buf + 1) * 512],
                        in_=hap(ago.ap(), g * 131072 + kb * 128,
                                [(TOWN, P), (AGN, 4), (1, 128)])),
                        waits=[("cc", cc_ag), ("mm", kv_free_mm[buf])],
                        sem=kvsem, n=16)
                    p.step("sync", lambda e, g=g, kb=kb, buf=buf: e.dma_start(
                        out=vblk[:, buf * 512:(buf + 1) * 512],
                        in_=hap(ago.ap(), VOFF + kb * 65536 + g * 128,
                                [(512, P), (AGN, 4), (1, 128)])), sem=kvsem, n=16)
                    kvv = (kvsem, p.cur(kvsem))
                    B0rel = max(kb - 2 * s, 0)
                    nB = 2 - B0rel
                    W = nB * 128
                    diag = kb - 2 * s if kb >= 2 * s else None
                    for r in range(4):
                        pa = it % 2
                        it += 1
                        for hl in range(4):
                            w = []
                            if hl == 0:
                                w = [kvv, ("se", sp_free_se[pa]),
                                     ("ve", negc_ve)] + qhat_tp
                            hg = g * 4 + hl
                            p.step("tensor", lambda e, hl=hl, hg=hg, r=r, pa=pa,
                                   s=s, B0rel=B0rel, W=W, buf=buf: e.matmul(
                                psB[:, pa * 1024 + hl * 256: pa * 1024 + hl * 256 + W],
                                kblk[:, buf * 512 + r * 128: buf * 512 + (r + 1) * 128],
                                qhat[:, hg * TOWN + s * 256 + B0rel * 128:
                                     hg * TOWN + s * 256 + B0rel * 128 + W],
                                start=True, stop=True), waits=w,
                                sem=("mm" if hl == 3 else None))
                        mm_sc = p.cur("mm")
                        p.step("scalar", lambda e, pa=pa, B0rel=B0rel, nB=nB: e.activation(
                            ap_md(pblk[:, :], pa * 1024 + B0rel * 512,
                                  [(128, 4), (512, nB), (1, 128)]),
                            ap_md(psB[:, :], pa * 1024,
                                  [(256, 4), (128, nB), (1, 128)]),
                            AF.Exp, bias=misc[:, 5:6], scale=1.0 / 128.0),
                            waits=[("mm", mm_sc), ("ve", negc_ve),
                                   ("mm", pb_free_mm[pa])], sem="se", ptr=True)
                        sp_free_se[pa] = p.cur("se")
                        sev = p.cur("se")
                        avw = [("se", sev)]
                        if diag is not None:
                            p.step("vector", lambda e, pa=pa, diag=diag, r=r:
                                   e.tensor_tensor(
                                ap_md(pblk[:, :], pa * 1024 + diag * 512,
                                      [(128, 4), (1, 128)]),
                                ap_md(pblk[:, :], pa * 1024 + diag * 512,
                                      [(128, 4), (1, 128)]),
                                ap_md(tri[:, :], r * 128, [(0, 4), (1, 128)]),
                                ALU.mult), waits=[("se", sev)], sem="ve")
                            avw.append(("ve", p.cur("ve")))
                        for Brel in range(B0rel, 2):
                            st_ = first[Brel]
                            lastk = (kb == 2 * s + Brel and r == 3)
                            if st_:
                                avw = avw + [("ve", g_ev)]
                            p.step("tensor", lambda e, Brel=Brel, pa=pa, st_=st_,
                                   lastk=lastk, buf=buf, r=r: e.matmul(
                                psA[:, Brel * 512:(Brel + 1) * 512],
                                vblk[:, buf * 512 + r * 128: buf * 512 + (r + 1) * 128],
                                pblk[:, pa * 1024 + Brel * 512: pa * 1024 + (Brel + 1) * 512],
                                start=st_, stop=lastk), waits=avw, sem="mm")
                            p.step("tensor", lambda e, Brel=Brel, pa=pa, st_=st_,
                                   lastk=lastk: e.matmul(
                                psA[0:1, 1024 + Brel * 512: 1024 + (Brel + 1) * 512],
                                ones16[:, 0:1],
                                pblk[:, pa * 1024 + Brel * 512: pa * 1024 + (Brel + 1) * 512],
                                start=st_, stop=lastk), sem="mm")
                            first[Brel] = False
                        pb_free_mm[pa] = p.cur("mm")
                        kv_free_mm[buf] = p.cur("mm")
                mmv = p.cur("mm")
                for Brel in range(2):
                    p.step("vector", lambda e, g=g, Brel=Brel: e.tensor_copy(
                        sums_row[0:1, (g * 2 + Brel) * 512:(g * 2 + Brel + 1) * 512],
                        psA[0:1, 1024 + Brel * 512:1024 + (Brel + 1) * 512]),
                        waits=[("mm", mmv)], sem="ve")
                    p.step("vector", lambda e, g=g, s=s, Brel=Brel: e.tensor_copy(
                        ap_md(attn[:, :], g * 4 * TOWN + s * 256 + Brel * 128,
                              [(TOWN, 4), (1, 128)]),
                        psA[:, Brel * 512:(Brel + 1) * 512]), sem="ve")
            vev = p.cur("ve")
            p.step("gpsimd", lambda e: e.dma_start(
                out=rspread[:, :],
                in_=ap_md(sums_row[0:1, :], 0, [(32, P), (1, 32)])),
                waits=[("ve", vev)], sem="wag", n=16)
            p.step("vector", lambda e: e.reciprocal(rspread[:, :], rspread[:, :]),
                   waits=[("wag", p.cur("wag"))], sem="ve")
            vv = p.cur("ve")
            p.step("gpsimd", lambda e: e.dma_start(
                out=ap_md(r_row[0:1, :], 0, [(32, P), (1, 32)]),
                in_=rspread[:, :]), waits=[("ve", vv)], sem="wag", n=16)
            rv = p.cur("wag")
            mults_ve = []
            for g in range(4):
                for Brel in range(2):
                    i = g * 2 + Brel
                    rrep = tp_step(lambda e, i=i: e.dma_start(
                        out=rbrep[:, (i % 2) * 512:(i % 2) * 512 + 512],
                        in_=ap_md(r_row[0:1, :], i * 512, [(0, P), (1, 512)])),
                        waits=[("wag", rv)] + ([mults_ve[i - 2]] if i >= 2 else []))
                    p.step("vector", lambda e, g=g, s=s, Brel=Brel, i=i:
                           e.tensor_tensor(
                        ap_md(attn[:, :], g * 4 * TOWN + s * 256 + Brel * 128,
                              [(TOWN, 4), (1, 128)]),
                        ap_md(attn[:, :], g * 4 * TOWN + s * 256 + Brel * 128,
                              [(TOWN, 4), (1, 128)]),
                        rbrep[:, (i % 2) * 512:(i % 2) * 512 + 512], ALU.mult),
                        waits=[rrep], sem="ve")
                    mults_ve.append(("ve", p.cur("ve")))
        return wo_done

    def oproj_phase(wo_done):
        mmv, vev = p.cur("mm"), p.cur("ve")
        xrel = []
        for tb in range(NT):
            sn = rr_sem()
            p.step("sync", lambda e, tb=tb: e.dma_start(
                out=x[:, tb * D:(tb + 1) * D],
                in_=hap(xspill.ap(), tb * 128 * D, [(D, P), (1, D)])),
                waits=[("mm", mmv), ("ve", vev), ("wsp", p.cur("wsp"))],
                sem=sn, n=16)
            xrel.append((sn, p.cur(sn)))
        ev_prev = {}
        for tb in range(NT):
            ps = psA if tb % 2 == 0 else psB
            w = [("ww", wo_done), ("ve", vev)]
            if tb >= 2:
                w.append(("ve", ev_prev[tb - 2]))
            for c in range(16):
                lhs = attn[:, c * TOWN + tb * 128: c * TOWN + tb * 128 + 128]
                for ob in range(4):
                    p.step("tensor", lambda e, lhs=lhs, ps=ps, c=c, ob=ob: e.matmul(
                        ps[:, ob * 512:(ob + 1) * 512], lhs,
                        wo[:, c * D + ob * 512: c * D + ob * 512 + 512],
                        start=(c == 0), stop=False),
                        waits=(w if (c == 0 and ob == 0) else []), sem=None)
            for ob in range(4):
                p.step("tensor", lambda e, ps=ps, ob=ob: e.matmul(
                    ps[:, ob * 512:(ob + 1) * 512], ones16[0:1, 0:128],
                    bo_row[0:1, ob * 512:(ob + 1) * 512], start=False, stop=True),
                    sem="mm")
            mmv2 = p.cur("mm")
            for ob in range(4):
                p.step("vector", lambda e, ps=ps, tb=tb, ob=ob: e.scalar_tensor_tensor(
                    x[:, tb * D + ob * 512: tb * D + ob * 512 + 512],
                    ps[:, ob * 512:(ob + 1) * 512], 1.0,
                    x[:, tb * D + ob * 512: tb * D + ob * 512 + 512],
                    ALU.mult, ALU.add),
                    waits=[("mm", mmv2), xrel[tb]], sem="ve")
            ev_prev[tb] = p.cur("ve")
        return p.cur("mm")

    import contextlib
    with contextlib.ExitStack() as es:
        psA = es.enter_context(nc.psum_tensor("psA", [P, 2048], F32))
        psB = es.enter_context(nc.psum_tensor("psB", [P, 2048], F32))
        SEM = {}
        for sname in sems:
            SEM[sname] = es.enter_context(nc.semaphore(sname))
        block = es.enter_context(nc.Block())

        init_phase()
        ln_stats(True, p.cur("wx"))
        sve = ln_gmax_allreduce(ar1i, ar1o)
        _, tp_ffn1 = ffn(sve, store_after=(TRUNC == 1))
        if TRUNC >= 2:
            cc_ag, qhat_tp = qkv_phase(tp_ffn1)
        if TRUNC >= 3:
            wo_done = attention_phase(cc_ag, qhat_tp)
            mm_oproj = oproj_phase(wo_done)
            if TRUNC == 3:
                vv = p.cur("ve")
                for tb in range(NT):
                    p.step("sync", lambda e, tb=tb: e.dma_start(
                        out=hap(y_out, tb * 128 * D, [(D, P), (1, D)]),
                        in_=x[:, tb * D:(tb + 1) * D]), waits=[("ve", vv)],
                        sem="wst", n=16)
        if TRUNC >= 4:
            for c in range(16):
                p.step("gpsimd", lambda e, c=c: e.dma_start(
                    out=wfT[:, c * D:(c + 1) * D],
                    in_=hap(wfT_in, c * P * D, [(D, P), (1, D)])),
                    waits=[("mm", mm_oproj)], sem="ww", n=16)
            ln_stats(True, 0, extra=[("ve", p.cur("ve")), ("mm", mm_oproj)])
            sve2 = ln_gmax_allreduce(ar2i, ar2o)
            ffn(sve2, store_after=True, region_mm=mm_oproj)

        stv = p.cur("wst")
        p.barrier_wait("sync", [("wst", stv)])

        for ename, attr in [("tensor", block.tensor), ("vector", block.vector),
                            ("scalar", block.scalar), ("sync", block.sync),
                            ("gpsimd", block.gpsimd)]:
            def body(eng, ename=ename):
                for fn in p.q[ename]:
                    fn(eng, SEM)
            attr(body)
    return nc


def make_inputs(ins, c, TRUNC):
    j = c % 4
    b = c // 4
    f32 = np.float32
    xfull = np.asarray(ins["L0_attn_x"], f32)
    xc = np.ascontiguousarray(xfull[b, j::4, :])
    pos = np.arange(j, 4096, 4)
    hdhalf = np.arange(0, HD, 2)
    inv = (1.0 / 10000.0 ** (hdhalf / HD))
    fr = np.outer(pos, inv)
    cosf, sinf = np.cos(fr).astype(f32), np.sin(fr).astype(f32)
    cost = np.ascontiguousarray(
        cosf.reshape(NT, 128, 64).transpose(1, 0, 2).reshape(128, NT * 64))
    sint = np.ascontiguousarray(
        sinf.reshape(NT, 128, 64).transpose(1, 0, 2).reshape(128, NT * 64))
    t_ = np.arange(128)
    tri = np.zeros((128, 4 * 128), np.float16)
    for r in range(4):
        m = (t_[None, :] > t_[:, None]) | ((t_[None, :] == t_[:, None]) & (j >= r))
        tri[:, r * 128:(r + 1) * 128] = m.astype(np.float16)
    qns = np.asarray(ins["L0_attn_q_norm_scale"], f32).reshape(1, HD)
    kns = np.asarray(ins["L0_attn_k_norm_scale"], f32).reshape(1, HD)
    return {
        "x": xc,
        "wq": np.asarray(ins["L0_attn_wq"], f32),
        "wk": np.asarray(ins["L0_attn_wk"], f32),
        "wv": np.asarray(ins["L0_attn_wv"], f32),
        "wo": np.asarray(ins["L0_attn_wo"], f32),
        "wfT": np.ascontiguousarray(np.asarray(ins["L0_ffn_weight"], f32).T),
        "bq": np.asarray(ins["L0_attn_bq"], f32).reshape(1, D),
        "bk": np.asarray(ins["L0_attn_bk"], f32).reshape(1, 512),
        "bv": np.asarray(ins["L0_attn_bv"], f32).reshape(1, 512),
        "bo": np.asarray(ins["L0_attn_bo"], f32).reshape(1, D),
        "bf": np.asarray(ins["L0_ffn_bias"], f32).reshape(1, D),
        "ws": np.full((1, 16), np.asarray(ins["L0_ffn_weight_scale"], f32)),
        "qnsr": np.ascontiguousarray(np.broadcast_to(np.tile(qns, 16), (P, D))),
        "knsr": np.ascontiguousarray(np.broadcast_to(np.tile(kns, 4), (P, 512))),
        "qns": qns, "kns": kns,
        "cosn": cost, "sinn": sint, "tri": tri,
    }


def kernel(**ins):
    TRUNC = 4
    nc = build_nc(0, TRUNC)
    in_maps = [make_inputs(ins, c, TRUNC) for c in range(8)]
    res = run_bass_kernel_spmd(nc, in_maps, core_ids=list(range(8)))
    out = np.zeros((2, 4096, 2048), np.float32)
    for c in range(8):
        out[c // 4, (c % 4)::4, :] = res.results[c]["y"]
    return out


# version 06
# speedup vs baseline: 1.3923x; 1.3798x over previous
import numpy as np
import concourse.bass as bass
import concourse.mybir as mybir
from concourse.bass_utils import run_bass_kernel_spmd

F32 = mybir.dt.float32
F16 = mybir.dt.float16
BF16 = mybir.dt.bfloat16
FP8 = mybir.dt.float8e4
AF = mybir.ActivationFunctionType
ALU = mybir.AluOpType
AX = mybir.AxisListType

P = 128
D = 2048
TOWN = 1024
NT = 8
HEADS, GROUPS, HD = 16, 4, 128
NB = 8
EPS_LN = 1e-5
EPS_RMS = 1e-6

ENGINES = ("tensor", "vector", "scalar", "sync", "gpsimd")


def ap_md(t_ap, offset_elems, dims):
    return bass.AP(tensor=t_ap.tensor, offset=t_ap.offset + offset_elems,
                   ap=[list(t_ap.ap[0])] + [[s, c] for (s, c) in dims])


def hap(t_ap, offset_elems, dims):
    return bass.AP(tensor=t_ap.tensor, offset=t_ap.offset + offset_elems,
                   ap=[[s, c] for (s, c) in dims])


ENG_SEM = {"tensor": "mm", "vector": "ve", "scalar": "se", "gpsimd": "gp"}


class Prog:

    def __init__(self):
        self.q = {e: [] for e in ENGINES}
        self.count = {}
        self.waited = {e: {} for e in ENGINES}
        self.force_next = {e: False for e in ENGINES}

    def inc(self, sem, n=1):
        self.count[sem] = self.count.get(sem, 0) + n
        return self.count[sem]

    def cur(self, sem):
        return self.count.get(sem, 0)

    def step(self, eng, build, waits=(), sem=None, n=1, ptr=False):
        waits = list(waits)
        if eng == "vector":
            ptr = True
        if (ptr or self.force_next[eng]) and eng in ENG_SEM:
            waits.append((ENG_SEM[eng], self.cur(ENG_SEM[eng])))
        self.force_next[eng] = bool(ptr)
        wl = []
        for s, v in waits:
            if v > 0 and self.waited[eng].get(s, -1) < v:
                wl.append((s, v))
                self.waited[eng][s] = v
        tgt = self.inc(sem, n) if sem else None
        def fn(e, SEM, wl=wl, sem=sem, n=n):
            for s, v in wl:
                e.wait_ge(SEM[s], v)
            inst = build(e)
            if sem is not None:
                if inst is None:
                    raise RuntimeError("step with sem must return instruction")
                inst.then_inc(SEM[sem], n)
        self.q[eng].append(fn)
        return tgt

    def barrier_wait(self, eng, waits):
        self.step(eng, lambda e: None, waits=waits)


def build_nc(j, TRUNC):
    nc = bass.Bass("TRN2", target_bir_lowering=False)

    x_in = nc.dram_tensor("x", [TOWN, D], F32, kind="ExternalInput").ap()
    wq_in = nc.dram_tensor("wq", [D, D], F32, kind="ExternalInput").ap()
    wk_in = nc.dram_tensor("wk", [D, 512], F32, kind="ExternalInput").ap()
    wv_in = nc.dram_tensor("wv", [D, 512], F32, kind="ExternalInput").ap()
    wo_in = nc.dram_tensor("wo", [D, D], F32, kind="ExternalInput").ap()
    wfT_in = nc.dram_tensor("wfT", [D, D], F32, kind="ExternalInput").ap()
    bq_in = nc.dram_tensor("bq", [1, D], F32, kind="ExternalInput").ap()
    bk_in = nc.dram_tensor("bk", [1, 512], F32, kind="ExternalInput").ap()
    bv_in = nc.dram_tensor("bv", [1, 512], F32, kind="ExternalInput").ap()
    bo_in = nc.dram_tensor("bo", [1, D], F32, kind="ExternalInput").ap()
    bf_in = nc.dram_tensor("bf", [1, D], F32, kind="ExternalInput").ap()
    ws_in = nc.dram_tensor("ws", [1, 16], F32, kind="ExternalInput").ap()
    qnsr_in = nc.dram_tensor("qnsr", [P, D], F32, kind="ExternalInput").ap()
    knsr_in = nc.dram_tensor("knsr", [P, 512], F32, kind="ExternalInput").ap()
    qns_in = nc.dram_tensor("qns", [1, HD], F32, kind="ExternalInput").ap()
    kns_in = nc.dram_tensor("kns", [1, HD], F32, kind="ExternalInput").ap()
    cos_in = nc.dram_tensor("cosn", [P, NT * 64], F32, kind="ExternalInput").ap()
    sin_in = nc.dram_tensor("sinn", [P, NT * 64], F32, kind="ExternalInput").ap()
    tri_in = nc.dram_tensor("tri", [P, 4 * 256], F16, kind="ExternalInput").ap()
    id_in = nc.dram_tensor("ident", [P, 128], F16, kind="ExternalInput").ap()

    y_out = nc.dram_tensor("y", [TOWN, D], F32, kind="ExternalOutput").ap()

    ar1i = nc.dram_tensor("ar1i", [1, 16], F32)
    ar1o = nc.dram_tensor("ar1o", [1, 16], F32)
    ar2i = nc.dram_tensor("ar2i", [1, 16], F32)
    ar2o = nc.dram_tensor("ar2o", [1, 16], F32)
    gdram = nc.dram_tensor("gdram", [P, 8], F32)
    sums_dram = nc.dram_tensor("sums_dram", [1, 4096], F16)
    r_dram = nc.dram_tensor("r_dram", [P, 32], F16)
    AGN = GROUPS * HD * TOWN + TOWN * 512
    agi = nc.dram_tensor("agi", [1, AGN], F16)
    ago = nc.dram_tensor("ago", [4, AGN], F16)
    VOFF = GROUPS * HD * TOWN

    BASE = ((nc.sbuf_base + 63) // 64) * 64
    off = [BASE]
    def alloc(name, shape, dtype, nbytes):
        t = nc.alloc_sbuf_tensor_at(name, shape, dtype, offset=off[0])
        off[0] += (nbytes + 63) // 64 * 64
        return t
    def at(name, shape, dtype, byteoff):
        return nc.alloc_sbuf_tensor_at(name, shape, dtype, offset=BASE + byteoff)

    ones16 = alloc("ones16", [P, 128], F16, 256)
    onesbf = alloc("onesbf", [P, 128], BF16, 256)
    ones32 = alloc("ones32", [1, 128], F32, 512)
    stats = alloc("stats", [P, 96], F32, 384)
    misc = alloc("misc", [P, 16], F32, 64)
    qstat = alloc("qstat", [P, 64], F32, 256)

    g1 = alloc("g1", [1, 16], F32, 64)
    cosn = alloc("cosn_sb", [P, NT * 64], F32, NT * 64 * 4)
    sinn = alloc("sinn_sb", [P, NT * 64], F32, NT * 64 * 4)
    tri = alloc("tri_sb", [P, 4 * 256], F16, 4 * 256 * 2)
    ident = alloc("ident_sb", [P, 128], F16, 128 * 2)
    bq_row = alloc("bq_row", [1, D], F16, D * 2)
    bo_row = alloc("bo_row", [1, D], F16, D * 2)
    bk_row = alloc("bk_row", [1, 512], F16, 512 * 2)
    bv_row = alloc("bv_row", [1, 512], F16, 512 * 2)
    bf_row = alloc("bf_row", [1, D], BF16, D * 2)
    qnsr = alloc("qnsr_sb", [P, D], F16, D * 2)
    knsr = alloc("knsr_sb", [P, 512], F16, 512 * 2)
    qnsraw = alloc("qns_raw", [1, HD], F32, 512)
    knsraw = alloc("kns_raw", [1, HD], F32, 512)
    wsrow = alloc("ws_sb", [1, 16], F32, 64)
    cbuf = alloc("cbuf", [1, 16], F16, 64)
    S_END = off[0] - BASE
    assert S_END <= 29 * 1024, S_END

    A_OFF = 29 * 1024
    W_OFF = A_OFF + 32 * 1024
    X_OFF = W_OFF + 64 * 1024
    C_OFF = X_OFF + 64 * 1024
    TOP = C_OFF + 17 * 1024
    assert BASE + TOP <= nc.sbuf_top, (BASE, TOP, nc.sbuf_top)

    xqf = at("xqf", [P, 16 * TOWN], BF16, A_OFF)
    grow = at("grow", [1, 1024], F32, A_OFF + 28 * 1024)
    xhf = at("xhf", [P, 16 * TOWN], F16, A_OFF)
    attn = at("attn", [P, 16 * TOWN], F16, A_OFF)
    wfT = at("wfT", [P, 16 * D], BF16, W_OFF)
    wqh = at("wqh", [P, 16 * 1024], F16, W_OFF)
    wk = at("wk_sb", [P, 16 * 512], F16, W_OFF + 32 * 1024)
    wv = at("wv_sb", [P, 16 * 512], F16, W_OFF + 48 * 1024)
    wo = at("wo_sb", [P, 16 * D], F16, W_OFF)
    x = at("x_sb", [P, NT * D], F32, X_OFF)
    qhat = at("qhat", [P, HEADS * TOWN], F16, X_OFF)
    kown = at("kown", [P, GROUPS * TOWN], F16, X_OFF + 32 * 1024)
    vown = at("vown", [P, NT * 512], F16, X_OFF + 40 * 1024)
    qscr = at("qscr", [P, D], F32, X_OFF + 48 * 1024)
    kblk = at("kblk", [P, 2 * 512], F16, X_OFF + 32 * 1024)
    vblk = at("vblk", [P, 2 * 512], F16, X_OFF + 34 * 1024)
    pblk = at("pblk", [P, 2 * 1024], F16, X_OFF + 36 * 1024)
    sums_row = at("sums_row", [1, 4096], F16, X_OFF + 40 * 1024)
    r_row = at("r_row", [1, 4096], F16, X_OFF + 48 * 1024)
    rspread = at("rspread", [P, 32], F32, X_OFF + 56 * 1024)
    rbrep = at("rbrep", [P, 1024], F16, X_OFF + 56 * 1024 + 256)
    scr32 = at("scr32", [P, D], F32, C_OFF)
    scr8 = at("scr8", [P, D], FP8, C_OFF + 8 * 1024)
    scr16 = at("scr16", [P, D], BF16, C_OFF + 10 * 1024)
    scr16h = at("scr16h", [P, D], F16, C_OFF + 10 * 1024)
    scr16q = at("scr16q", [P, 1024], F16, C_OFF + 14 * 1024)
    scr16k = at("scr16k", [P, 512], F16, C_OFF + 16 * 1024)

    xspill = nc.dram_tensor("xspill", [TOWN, D], F32)

    p = Prog()
    state = {}
    sems = ["wx", "ww", "wsp", "wst", "mm", "ve", "se",
            "gp", "cc"] + ["wtp%d" % i for i in range(8)] + ["wgp%d" % i for i in range(4)]
    gp_idx = [0]
    def gp_step(build, waits=()):
        sn = "wgp%d" % (gp_idx[0] % 4)
        gp_idx[0] += 1
        v = p.step("gpsimd", build, waits=list(waits) + [(sn, p.cur(sn))],
                   sem=sn, n=16)
        return (sn, v)
    def rr_sem():
        sn = "wtp%d" % (tp_idx[0] % 8)
        tp_idx[0] += 1
        return sn
    tp_idx = [0]
    def tp_step(build, waits=()):
        sn = "wtp%d" % (tp_idx[0] % 8)
        tp_idx[0] += 1
        v = p.step("sync", build, waits=list(waits) + [(sn, p.cur(sn))],
                   sem=sn, n=16)
        return (sn, v)
    def tp_all():
        return [("wtp%d" % i, p.cur("wtp%d" % i)) for i in range(8)]

    def init_phase():
        p.step("sync", lambda e: e.dma_start(
            out=ap_md(x[:, :], 0, [(D, NT), (1, D)]),
            in_=hap(x_in, 0, [(D, P), (P * D, NT), (1, D)])), sem="wx", n=16)
        for nm, dst, src in [("cos", cosn, cos_in), ("sin", sinn, sin_in)]:
            p.step("sync", lambda e, d=dst, s=src: e.dma_start(out=d[:, :], in_=s),
                   sem="wx", n=16)
        p.step("sync", lambda e: e.dma_start(out=tri[:, :], in_=tri_in), sem="wx", n=16)
        p.step("sync", lambda e: e.dma_start(out=ident[:, :], in_=id_in), sem="wx", n=16)
        for dst, src in [(wsrow, ws_in), (qnsraw, qns_in),
                         (knsraw, kns_in)]:
            p.step("sync", lambda e, d=dst, s=src: e.dma_start(out=d[:, :], in_=s),
                   sem="wx", n=16)
        for dst, src in [(bq_row, bq_in), (bk_row, bk_in), (bv_row, bv_in),
                         (bo_row, bo_in), (qnsr, qnsr_in), (knsr, knsr_in)]:
            p.step("gpsimd", lambda e, d=dst, s=src: e.dma_start(out=d[:, :], in_=s),
                   sem="ww", n=16)
        for c in range(16):
            p.step("gpsimd", lambda e, c=c: e.dma_start(
                out=wfT[:, c * D:(c + 1) * D],
                in_=hap(wfT_in, c * P * D, [(D, P), (1, D)])), sem="ww", n=16)
        for t, val in [(ones16, 1.0), (onesbf, 1.0), (ones32, 1.0), (g1, 0.0), (misc, 0.0)]:
            p.step("gpsimd", lambda e, t=t, v=val: e.memset(t[:, :], v), sem="gp")
        state["gp_init"] = p.cur("gp")
        p.step("sync", lambda e: e.dma_start(out=scr32[0:1, 0:D], in_=bf_in),
               sem="wx", n=16)
        wxv = p.cur("wx")
        p.step("vector", lambda e: e.reciprocal(misc[0:1, 3:4], wsrow[0:1, 0:1]),
               waits=[("wx", wxv), ("gp", state["gp_init"])], sem="ve")
        p.step("vector", lambda e: e.tensor_single_scalar(
            scr32[0:1, 0:D], scr32[0:1, 0:D], misc[0:1, 3:4], ALU.mult),
            sem="ve", ptr=True)
        p.step("vector", lambda e: e.tensor_copy(bf_row[:, :], scr32[0:1, 0:D]), sem="ve")
        wwv = p.cur("ww")
        p.step("tensor", lambda e: e.matmul(
            psB[:, 0:1], ones32[0:1, 0:128], wsrow[0:1, 0:1], start=True, stop=True),
            waits=[("wx", wxv), ("gp", p.cur("gp"))], sem="mm")
        p.step("vector", lambda e: e.tensor_scalar_mul(misc[:, 4:5], psB[:, 0:1], 0.5),
               waits=[("mm", p.cur("mm"))], sem="ve")
        state["ve_init"] = p.cur("ve")

    def ln_stats_tile(tb, minmax, w):
        sl = x[:, tb * D:(tb + 1) * D]
        p.step("scalar", lambda e, sl=sl, tb=tb: e.activation(
            scr16[:, :], sl, AF.Identity,
            accum_out=stats[:, tb:tb + 1]), waits=w, sem="se", ptr=True)
        p.step("scalar", lambda e, sl=sl, tb=tb: e.activation(
            scr16[:, :], sl, AF.Square,
            accum_out=stats[:, 8 + tb:9 + tb]), sem="se", ptr=True)
        if minmax:
            p.step("vector", lambda e, sl=sl, tb=tb: e.tensor_reduce(
                stats[:, 16 + tb:17 + tb], sl, axis=AX.X, op=ALU.min),
                waits=[x_ for x_ in w if x_[0] != "ve"], sem="ve")
            p.step("vector", lambda e, sl=sl, tb=tb: e.tensor_reduce(
                stats[:, 24 + tb:25 + tb], sl, axis=AX.X, op=ALU.max), sem="ve")

    def ln_stats(minmax, xwait, extra=()):
        for tb in range(NT):
            w0 = ([("wx", xwait)] + list(extra)) if tb == 0 else []
            ln_stats_tile(tb, minmax, w0)
        p.step("vector", lambda e: e.tensor_scalar_mul(
            stats[:, 32:40], stats[:, 0:8], 1.0 / D),
            waits=[("se", p.cur("se"))], sem="ve")
        p.step("vector", lambda e: e.tensor_scalar_mul(
            stats[:, 48:56], stats[:, 8:16], 1.0 / D), sem="ve")
        p.step("vector", lambda e: e.tensor_mul(
            stats[:, 56:64], stats[:, 32:40], stats[:, 32:40]), sem="ve")
        p.step("vector", lambda e: e.tensor_sub(
            stats[:, 48:56], stats[:, 48:56], stats[:, 56:64]), sem="ve")
        p.step("vector", lambda e: e.tensor_scalar_add(
            stats[:, 48:56], stats[:, 48:56], EPS_LN), sem="ve")
        vv = p.cur("ve")
        p.step("scalar", lambda e: e.activation(
            stats[:, 56:64], stats[:, 48:56], AF.Sqrt), waits=[("ve", vv)], sem="se")
        sv = p.cur("se")
        p.step("vector", lambda e: e.reciprocal(stats[:, 64:72], stats[:, 56:64]),
               waits=[("se", sv)], sem="ve")
        p.step("vector", lambda e: e.tensor_mul(
            stats[:, 64:72], stats[:, 64:72], stats[:, 48:56]), sem="ve")
        p.step("vector", lambda e: e.tensor_add(
            stats[:, 64:72], stats[:, 64:72], stats[:, 56:64]), sem="ve")
        p.step("vector", lambda e: e.tensor_scalar_mul(
            stats[:, 56:64], stats[:, 64:72], 0.5), sem="ve")
        p.step("vector", lambda e: e.reciprocal(stats[:, 40:48], stats[:, 56:64]),
               sem="ve")

    def ln_gmax_allreduce(ar_in_t, ar_out_t):
        p.step("vector", lambda e: e.tensor_sub(
            stats[:, 56:64], stats[:, 24:32], stats[:, 32:40]), sem="ve")
        p.step("vector", lambda e: e.tensor_sub(
            stats[:, 64:72], stats[:, 32:40], stats[:, 16:24]), sem="ve")
        p.step("vector", lambda e: e.tensor_max(
            stats[:, 88:96], stats[:, 56:64], stats[:, 64:72]), sem="ve")
        p.step("vector", lambda e: e.tensor_mul(
            stats[:, 88:96], stats[:, 88:96], stats[:, 40:48]), sem="ve")
        vv = p.cur("ve")
        p.step("sync", lambda e: e.dma_start(out=gdram.ap(),
               in_=stats[:, 88:96]), waits=[("ve", vv)], sem="wsp", n=16)
        p.step("sync", lambda e: e.dma_start(out=grow[0:1, :],
               in_=hap(gdram.ap(), 0, [(1024, 1), (1, 1024)])),
               waits=[("wsp", p.cur("wsp"))], sem="wsp", n=16)
        p.step("vector", lambda e: e.reduce_max(g1[0:1, 0:1], grow[0:1, :],
               axis=AX.X, apply_absolute_value=False),
               waits=[("wsp", p.cur("wsp"))], sem="ve")
        vv = p.cur("ve")
        w1 = gp_step(lambda e: e.dma_start(out=ar_in_t.ap(), in_=g1[:, :]),
                     waits=[("ve", vv)])
        p.step("gpsimd", lambda e: e.collective_compute(
            "AllReduce", ALU.max, replica_groups=[list(range(8))],
            ins=[ar_in_t.ap()], outs=[ar_out_t.ap()]),
            waits=[w1], sem="cc")
        w2 = gp_step(lambda e: e.dma_start(out=g1[:, :], in_=ar_out_t.ap()),
                     waits=[("cc", p.cur("cc"))])
        p.step("vector", lambda e: e.tensor_scalar(
            g1[0:1, 1:2], g1[0:1, 0:1], 1.0 / 448.0, 1e-7, ALU.mult, ALU.max),
            waits=[w2], sem="ve")
        p.step("vector", lambda e: e.reciprocal(g1[0:1, 2:3], g1[0:1, 1:2]), sem="ve")
        vv = p.cur("ve")
        p.step("tensor", lambda e: e.matmul(
            psB[:, 0:1], ones32[0:1, 0:128], g1[0:1, 2:3], start=True, stop=True),
            waits=[("ve", vv)], sem="mm")
        p.step("vector", lambda e: e.tensor_scalar_mul(
            misc[:, 2:3], psB[:, 0:1], 0.5), waits=[("mm", p.cur("mm"))], sem="ve")
        p.step("vector", lambda e: e.tensor_single_scalar(
            stats[:, 72:80], stats[:, 40:48], misc[:, 2:3], ALU.mult),
            sem="ve", ptr=True)
        p.step("vector", lambda e: e.tensor_mul(
            stats[:, 80:88], stats[:, 32:40], stats[:, 72:80]), sem="ve")
        p.step("vector", lambda e: e.tensor_scalar_mul(
            stats[:, 80:88], stats[:, 80:88], -1.0), sem="ve")
        return p.cur("ve")

    def ffn(scale_ready_ve, store_after, region_mm=0):
        tp_done = []
        for tb in range(NT):
            sl = x[:, tb * D:(tb + 1) * D]
            qw = [("ve", scale_ready_ve)]
            if tb == 0:
                qw.append(("mm", region_mm))
            else:
                qw.append(tp_done[tb - 1])
            p.step("vector", lambda e, sl=sl, tb=tb: e.tensor_scalar(
                scr8[:, :], sl, stats[:, 72 + tb:73 + tb], stats[:, 80 + tb:81 + tb],
                ALU.mult, ALU.add), waits=qw, sem="ve", ptr=True)
            p.step("vector", lambda e: e.tensor_scalar_mul(
                scr16[:, :], scr8[:, :], 2.0), sem="ve")
            vv = p.cur("ve")
            tp_done.append(tp_step(lambda e, tb=tb: e.dma_start(
                out=ap_md(xqf[:, :], tb * 128, [(TOWN, 16), (1, 128)]),
                in_=scr16[:, :], transpose=True),
                waits=[("ve", vv), ("mm", region_mm)]))
        wwv = p.cur("ww")
        ev_of_tb = []
        for tb in range(NT):
            ps = psA if tb % 2 == 0 else psB
            w = [tp_done[tb], ("ww", wwv)]
            if tb >= 2:
                w.append(("ve", ev_of_tb[tb - 2]))
            for c in range(16):
                lhs = xqf[:, c * TOWN + tb * 128: c * TOWN + tb * 128 + 128]
                for ob in range(4):
                    p.step("tensor", lambda e, lhs=lhs, ps=ps, c=c, ob=ob: e.matmul(
                        ps[:, ob * 512:(ob + 1) * 512], lhs,
                        wfT[:, c * D + ob * 512: c * D + ob * 512 + 512],
                        start=(c == 0), stop=False),
                        waits=(w if (c == 0 and ob == 0) else []),
                        sem=("mm" if (c == 15 and ob == 3) else None))
            for ob in range(4):
                p.step("tensor", lambda e, ps=ps, ob=ob: e.matmul(
                    ps[:, ob * 512:(ob + 1) * 512], onesbf[0:1, 0:128],
                    bf_row[0:1, ob * 512:(ob + 1) * 512], start=False, stop=True),
                    waits=[("ve", state["ve_init"])], sem="mm")
            mmv = p.cur("mm")
            for ob in range(4):
                p.step("vector", lambda e, ps=ps, tb=tb, ob=ob: e.scalar_tensor_tensor(
                    x[:, tb * D + ob * 512: tb * D + ob * 512 + 512],
                    ps[:, ob * 512:(ob + 1) * 512], misc[:, 4:5],
                    x[:, tb * D + ob * 512: tb * D + ob * 512 + 512],
                    ALU.mult, ALU.add),
                    waits=[("mm", mmv)], sem="ve", ptr=True)
            ev_of_tb.append(p.cur("ve"))
            if store_after:
                vv = p.cur("ve")
                p.step("sync", lambda e, tb=tb: e.dma_start(
                    out=hap(y_out, tb * 128 * D, [(D, P), (1, D)]),
                    in_=x[:, tb * D:(tb + 1) * D]), waits=[("ve", vv)],
                    sem="wst", n=16)
        return ev_of_tb, tp_done

    def rms_rope_transpose(src, nheads, scol, rcol, tcol, nsr, nsoff, tb, dst,
                           dstoff, dst16, h16_free):
        W = nheads * HD
        p.step("vector", lambda e: e.scalar_tensor_tensor(
            scr32[:, 0:W], src, 1.0, src, ALU.mult, ALU.mult),
            waits=[h16_free], sem="ve")
        p.step("vector", lambda e: e.reduce_sum(
            qstat[:, scol:scol + nheads],
            ap_md(scr32[:, :], 0, [(HD, nheads), (1, HD)]), axis=AX.X), sem="ve")
        p.step("vector", lambda e: e.tensor_scalar(
            qstat[:, scol:scol + nheads], qstat[:, scol:scol + nheads],
            1.0 / HD, EPS_RMS, ALU.mult, ALU.add), sem="ve")
        vv = p.cur("ve")
        p.step("scalar", lambda e: e.activation(
            qstat[:, rcol:rcol + nheads], qstat[:, scol:scol + nheads], AF.Sqrt),
            waits=[("ve", vv)], sem="se")
        sv = p.cur("se")
        p.step("vector", lambda e: e.reciprocal(
            qstat[:, tcol:tcol + nheads], qstat[:, rcol:rcol + nheads]),
            waits=[("se", sv)], sem="ve")
        p.step("vector", lambda e: e.tensor_mul(
            qstat[:, tcol:tcol + nheads], qstat[:, tcol:tcol + nheads],
            qstat[:, scol:scol + nheads]), sem="ve")
        p.step("vector", lambda e: e.tensor_add(
            qstat[:, tcol:tcol + nheads], qstat[:, tcol:tcol + nheads],
            qstat[:, rcol:rcol + nheads]), sem="ve")
        p.step("vector", lambda e: e.tensor_scalar_mul(
            qstat[:, rcol:rcol + nheads], qstat[:, tcol:tcol + nheads],
            0.5), sem="ve")
        p.step("vector", lambda e: e.reciprocal(
            qstat[:, rcol:rcol + nheads], qstat[:, rcol:rcol + nheads]), sem="ve")
        for hl in range(nheads):
            p.step("vector", lambda e, hl=hl: e.tensor_single_scalar(
                ap_md(src, hl * HD, [(1, HD)]), ap_md(src, hl * HD, [(1, HD)]),
                qstat[:, rcol + hl:rcol + hl + 1], ALU.mult), sem="ve")
        p.step("vector", lambda e: e.tensor_mul(
            src, src, nsr[:, nsoff:nsoff + W]), sem="ve")
        cosb = ap_md(cosn[:, :], tb * 64, [(0, nheads), (1, 64)])
        sinb = ap_md(sinn[:, :], tb * 64, [(0, nheads), (1, 64)])
        a3 = ap_md(src, 0, [(HD, nheads), (1, 64)])
        b3 = ap_md(src, 64, [(HD, nheads), (1, 64)])
        t1 = ap_md(scr32[:, :], 1024, [(64, nheads), (1, 64)])
        t2 = ap_md(scr32[:, :], 1024 + nheads * 64, [(64, nheads), (1, 64)])
        out_re = ap_md(dst16[:, :], 0, [(HD, nheads), (1, 64)])
        out_im = ap_md(dst16[:, :], 64, [(HD, nheads), (1, 64)])
        p.step("vector", lambda e: e.tensor_tensor(t1, a3, cosb, ALU.mult), sem="ve")
        p.step("vector", lambda e: e.tensor_tensor(t2, b3, sinb, ALU.mult), sem="ve")
        p.step("vector", lambda e: e.tensor_tensor(out_re, t1, t2, ALU.subtract), sem="ve")
        p.step("vector", lambda e: e.tensor_tensor(t1, a3, sinb, ALU.mult), sem="ve")
        p.step("vector", lambda e: e.tensor_tensor(t2, b3, cosb, ALU.mult), sem="ve")
        p.step("vector", lambda e: e.tensor_tensor(out_im, t1, t2, ALU.add), sem="ve")
        vv = p.cur("ve")
        return tp_step(lambda e: e.dma_start(
            out=ap_md(dst, dstoff + tb * 128, [(TOWN, nheads), (1, 128)]),
            in_=dst16[:, :], transpose=True),
            waits=[("ve", vv)])

    def qkv_phase(tp_ffn1):
        ln_stats(False, 0, extra=[("ve", p.cur("ve")), tp_ffn1[-1]])
        p.step("vector", lambda e: e.tensor_mul(
            stats[:, 80:88], stats[:, 32:40], stats[:, 40:48]), sem="ve")
        p.step("vector", lambda e: e.tensor_scalar_mul(
            stats[:, 80:88], stats[:, 80:88], -1.0), sem="ve")
        sve = p.cur("ve")
        mm_ffn1 = p.cur("mm")
        for c in range(16):
            p.step("gpsimd", lambda e, c=c: e.dma_start(
                out=wqh[:, c * 1024:(c + 1) * 1024],
                in_=hap(wq_in, c * P * D, [(D, P), (1, 1024)])),
                waits=[("mm", mm_ffn1)], sem="ww", n=16)
            p.step("gpsimd", lambda e, c=c: e.dma_start(
                out=wk[:, c * 512:(c + 1) * 512],
                in_=hap(wk_in, c * P * 512, [(512, P), (1, 512)])), sem="ww", n=16)
            p.step("gpsimd", lambda e, c=c: e.dma_start(
                out=wv[:, c * 512:(c + 1) * 512],
                in_=hap(wv_in, c * P * 512, [(512, P), (1, 512)])), sem="ww", n=16)
        ww_A = p.cur("ww")
        tpxh = []
        for tb in range(NT):
            sl = x[:, tb * D:(tb + 1) * D]
            qw = [("ve", sve)]
            qw.append(tp_ffn1[-1] if tb == 0 else tpxh[tb - 1])
            p.step("vector", lambda e, sl=sl, tb=tb: e.tensor_scalar(
                scr16h[:, :], sl, stats[:, 40 + tb:41 + tb],
                stats[:, 80 + tb:81 + tb], ALU.mult, ALU.add), waits=qw,
                sem="ve", ptr=True)
            vv = p.cur("ve")
            tpxh.append(tp_step(lambda e, tb=tb: e.dma_start(
                out=ap_md(xhf[:, :], tb * 128, [(TOWN, 16), (1, 128)]),
                in_=scr16h[:, :], transpose=True), waits=[("ve", vv)]))
            p.step("sync", lambda e, tb=tb: e.dma_start(
                out=hap(xspill.ap(), tb * 128 * D, [(D, P), (1, D)]),
                in_=x[:, tb * D:(tb + 1) * D]), sem="wsp", n=16)
        spill_done = p.cur("wsp")
        last_tp = {"q": tpxh[-1], "k": tpxh[-1]}
        ev_prev = {}
        for half in range(2):
            if half == 1:
                p.step("gpsimd", lambda e: e.collective_compute(
                    "AllGather", ALU.bypass,
                    replica_groups=[[0, 1, 2, 3], [4, 5, 6, 7]],
                    ins=[agi.ap()], outs=[ago.ap()]),
                    waits=[("wsp", p.cur("wsp"))], sem="cc")
                mm_passA = p.cur("mm")
                for c in range(16):
                    p.step("gpsimd", lambda e, c=c: e.dma_start(
                        out=wqh[:, c * 1024:(c + 1) * 1024],
                        in_=hap(wq_in, c * P * D + 1024, [(D, P), (1, 1024)])),
                        waits=[("mm", mm_passA)], sem="ww", n=16)
            wwv = p.cur("ww")
            for tb in range(NT):
                gt = half * NT + tb
                ps = psA if tb % 2 == 0 else psB
                w = [tpxh[tb], ("ww", wwv)]
                if gt >= 2:
                    w.append(("ve", ev_prev[gt - 2]))
                nmm = 4 if half == 0 else 2
                for c in range(16):
                    lhs = xhf[:, c * TOWN + tb * 128: c * TOWN + tb * 128 + 128]
                    outs = [ps[:, 0:512], ps[:, 512:1024]]
                    rhss = [wqh[:, c * 1024:c * 1024 + 512],
                            wqh[:, c * 1024 + 512:(c + 1) * 1024]]
                    if half == 0:
                        outs += [ps[:, 1024:1536], ps[:, 1536:2048]]
                        rhss += [wk[:, c * 512:(c + 1) * 512],
                                 wv[:, c * 512:(c + 1) * 512]]
                    for i in range(nmm):
                        p.step("tensor", lambda e, o=outs[i], lhs=lhs, r=rhss[i],
                               c=c: e.matmul(o, lhs, r, start=(c == 0), stop=False),
                               waits=(w if (c == 0 and i == 0) else []),
                               sem=None)
                boff = 0 if half == 0 else 1024
                bl = [(ps[:, 0:512], bq_row[0:1, boff:boff + 512]),
                      (ps[:, 512:1024], bq_row[0:1, boff + 512:boff + 1024])]
                if half == 0:
                    bl += [(ps[:, 1024:1536], bk_row[0:1, :]),
                           (ps[:, 1536:2048], bv_row[0:1, :])]
                for o, b in bl:
                    p.step("tensor", lambda e, o=o, b=b: e.matmul(
                        o, ones16[0:1, 0:128], b, start=False, stop=True), sem="mm")
                mmv = p.cur("mm")
                p.step("vector", lambda e, ps=ps: e.tensor_copy(
                    qscr[:, 0:1024], ps[:, 0:1024]),
                    waits=[("mm", mmv), ("wsp", spill_done)], sem="ve")
                if half == 0:
                    p.step("vector", lambda e, ps=ps: e.tensor_copy(
                        qscr[:, 1024:1536], ps[:, 1024:1536]), sem="ve")
                    p.step("vector", lambda e, ps=ps, tb=tb: e.tensor_copy(
                        vown[:, tb * 512:(tb + 1) * 512], ps[:, 1536:2048]), sem="ve")
                ev_prev[gt] = p.cur("ve")
                if half == 0:
                    last_tp["k"] = rms_rope_transpose(
                        qscr[:, 1024:1536], 4, 8, 40, 56, knsr, 0,
                        tb, kown[:, :], 0, scr16k, last_tp["k"])
                last_tp["q"] = rms_rope_transpose(
                    qscr[:, 0:1024], 8, 0, 16, 32, qnsr, half * 1024,
                    tb, qhat[:, :], half * 8 * TOWN, scr16q, last_tp["q"])
                if half == 0:
                    p.step("sync", lambda e, tb=tb: e.dma_start(
                        out=hap(agi.ap(), tb * 128,
                                [(TOWN, P), (131072, 4), (1, 128)]),
                        in_=ap_md(kown[:, :], tb * 128, [(TOWN, 4), (1, 128)])),
                        waits=[last_tp["k"]], sem="wsp", n=16)
                    p.step("sync", lambda e, tb=tb: e.dma_start(
                        out=hap(agi.ap(), VOFF + tb * 128 * 512,
                                [(512, P), (1, 512)]),
                        in_=ap_md(vown[:, :], tb * 512, [(1, 512)])),
                        waits=[("ve", ev_prev[tb])], sem="wsp", n=16)
        qhat_tp = tp_all()
        return p.cur("cc"), qhat_tp

    def attention_phase(cc_ag, qhat_tp):
        p.step("vector", lambda e: e.reduce_max(
            g1[0:1, 4:5], qnsraw[0:1, :], axis=AX.X, apply_absolute_value=True),
            sem="ve")
        p.step("vector", lambda e: e.reduce_max(
            g1[0:1, 5:6], knsraw[0:1, :], axis=AX.X, apply_absolute_value=True),
            sem="ve")
        p.step("vector", lambda e: e.tensor_mul(
            g1[0:1, 6:7], g1[0:1, 4:5], g1[0:1, 5:6]), sem="ve")
        p.step("vector", lambda e: e.tensor_copy(cbuf[0:1, 0:1], g1[0:1, 6:7]),
               sem="ve")
        vv = p.cur("ve")
        p.step("tensor", lambda e: e.matmul(
            psB[:, 0:1], ones16[0:1, 0:128], cbuf[0:1, 0:1], start=True, stop=True),
            waits=[("ve", vv)], sem="mm")
        p.step("vector", lambda e: e.tensor_scalar_mul(
            misc[:, 5:6], psB[:, 0:1], -1.0), waits=[("mm", p.cur("mm"))], sem="ve")
        negc_ve = p.cur("ve")
        mm_qkv = p.cur("mm")
        for c in range(16):
            p.step("gpsimd", lambda e, c=c: e.dma_start(
                out=wo[:, c * D:(c + 1) * D],
                in_=hap(wo_in, c * P * D, [(D, P), (1, D)])),
                waits=[("mm", mm_qkv)], sem="ww", n=16)
        wo_done = p.cur("ww")

        ikv = 0
        it = 0
        kv_done = {}
        kv_free_mm = {0: 0, 1: 0}
        sp_free_se = {0: 0, 1: 0}
        pb_free_mm = {0: 0, 1: 0}
        for s in range(4):
            for g in range(4):
                g_ev = state.get("postev_ve", 0) if g == 0 else p.cur("ve")
                first = {0: True, 1: True}
                pending = None
                def flush(pending):
                    pa, buf, r, B0rel, lasts = pending
                    for Brel in range(B0rel, 2):
                        st_ = first[Brel]
                        avw = [("se", sp_free_se[pa])]
                        if st_:
                            avw.append(("ve", g_ev))
                        p.step("tensor", lambda e, Brel=Brel, pa=pa,
                               st_=st_, lastk=lasts[Brel], buf=buf, r=r: e.matmul(
                            psA[:, Brel * 512:(Brel + 1) * 512],
                            vblk[:, buf * 512 + r * 128: buf * 512 + (r + 1) * 128],
                            pblk[:, pa * 1024 + Brel * 512: pa * 1024 + (Brel + 1) * 512],
                            start=st_, stop=lasts[Brel]), waits=avw, sem="mm")
                        p.step("tensor", lambda e, Brel=Brel, pa=pa,
                               st_=st_, lastk=lasts[Brel]: e.matmul(
                            psA[0:1, 1024 + Brel * 512: 1024 + (Brel + 1) * 512],
                            ones16[:, 0:1],
                            pblk[:, pa * 1024 + Brel * 512: pa * 1024 + (Brel + 1) * 512],
                            start=st_, stop=lasts[Brel]), sem="mm")
                        first[Brel] = False
                    pb_free_mm[pa] = p.cur("mm")
                    kv_free_mm[buf] = p.cur("mm")
                for kb in range(2 * s + 2):
                    buf = ikv % 2
                    ikv += 1
                    kvsem = rr_sem()
                    p.step("sync", lambda e, g=g, kb=kb, buf=buf: e.dma_start(
                        out=kblk[:, buf * 512:(buf + 1) * 512],
                        in_=hap(ago.ap(), g * 131072 + kb * 128,
                                [(TOWN, P), (AGN, 4), (1, 128)])),
                        waits=[("cc", cc_ag), ("mm", kv_free_mm[buf]),
                               (kvsem, p.cur(kvsem))],
                        sem=kvsem, n=16)
                    p.step("sync", lambda e, g=g, kb=kb, buf=buf: e.dma_start(
                        out=vblk[:, buf * 512:(buf + 1) * 512],
                        in_=hap(ago.ap(), VOFF + kb * 65536 + g * 128,
                                [(512, P), (AGN, 4), (1, 128)])), sem=kvsem, n=16)
                    kvv = (kvsem, p.cur(kvsem))
                    B0rel = max(kb - 2 * s, 0)
                    nB = 2 - B0rel
                    W = nB * 128
                    diag = kb - 2 * s if kb >= 2 * s else None
                    for r in range(4):
                        pa = it % 2
                        it += 1
                        for hl in range(4):
                            w = []
                            if hl == 0:
                                w = [kvv, ("se", sp_free_se[pa]),
                                     ("ve", negc_ve)] + qhat_tp
                            hg = g * 4 + hl
                            if diag is not None:
                                p.step("tensor", lambda e, hl=hl, r=r, pa=pa,
                                       W=W: e.matmul(
                                    psB[:, pa * 1024 + hl * 256:
                                         pa * 1024 + hl * 256 + W],
                                    ident[:, :], tri[:, r * 256: r * 256 + W],
                                    start=True, stop=False), waits=w)
                                w = []
                            p.step("tensor", lambda e, hl=hl, hg=hg, r=r, pa=pa,
                                   s=s, B0rel=B0rel, W=W, buf=buf,
                                   st=(diag is None): e.matmul(
                                psB[:, pa * 1024 + hl * 256: pa * 1024 + hl * 256 + W],
                                kblk[:, buf * 512 + r * 128: buf * 512 + (r + 1) * 128],
                                qhat[:, hg * TOWN + s * 256 + B0rel * 128:
                                     hg * TOWN + s * 256 + B0rel * 128 + W],
                                start=st, stop=True),
                                waits=w, sem=("mm" if hl == 3 else None))
                        mm_sc = p.cur("mm")
                        p.step("scalar", lambda e, pa=pa, B0rel=B0rel, nB=nB: e.activation(
                            ap_md(pblk[:, :], pa * 1024 + B0rel * 512,
                                  [(128, 4), (512, nB), (1, 128)]),
                            ap_md(psB[:, :], pa * 1024,
                                  [(256, 4), (128, nB), (1, 128)]),
                            AF.Exp, bias=misc[:, 5:6], scale=1.0 / 128.0),
                            waits=[("mm", mm_sc), ("ve", negc_ve),
                                   ("mm", pb_free_mm[pa])], sem="se", ptr=True)
                        sp_free_se[pa] = p.cur("se")
                        if pending is not None:
                            flush(pending)
                        lasts = {Brel: (kb == 2 * s + Brel and r == 3)
                                 for Brel in range(2)}
                        pending = (pa, buf, r, B0rel, lasts)
                flush(pending)
                pending = None
                mmv = p.cur("mm")
                for Brel in range(2):
                    p.step("vector", lambda e, g=g, Brel=Brel: e.tensor_copy(
                        sums_row[0:1, (g * 2 + Brel) * 512:(g * 2 + Brel + 1) * 512],
                        psA[0:1, 1024 + Brel * 512:1024 + (Brel + 1) * 512]),
                        waits=[("mm", mmv), ("wsp", state.get("sumsd", 0))], sem="ve")
                    p.step("vector", lambda e, g=g, s=s, Brel=Brel: e.tensor_copy(
                        ap_md(attn[:, :], g * 4 * TOWN + s * 256 + Brel * 128,
                              [(TOWN, 4), (1, 128)]),
                        psA[:, Brel * 512:(Brel + 1) * 512]), sem="ve")
            state["postev_ve"] = p.cur("ve")
            vev = p.cur("ve")
            p.step("sync", lambda e: e.dma_start(
                out=sums_dram.ap(), in_=sums_row[0:1, :]),
                waits=[("ve", vev)], sem="wsp", n=16)
            state["sumsd"] = p.cur("wsp")
            w1 = gp_step(lambda e: e.dma_start(
                out=rspread[:, :],
                in_=hap(sums_dram.ap(), 0, [(32, P), (1, 32)])),
                waits=[("wsp", state["sumsd"])])
            p.step("vector", lambda e: e.reciprocal(rspread[:, :], rspread[:, :]),
                   waits=[w1], sem="ve")
            vv = p.cur("ve")
            rv = gp_step(lambda e: e.dma_start(
                out=r_dram.ap(), in_=rspread[:, :]),
                waits=[("ve", vv)])
            mults_ve = []
            for g in range(4):
                for Brel in range(2):
                    i = g * 2 + Brel
                    rrep = tp_step(lambda e, i=i: e.dma_start(
                        out=rbrep[:, (i % 2) * 512:(i % 2) * 512 + 512],
                        in_=hap(r_dram.ap(), i * 512, [(0, P), (1, 512)])),
                        waits=[rv] + ([mults_ve[i - 2]] if i >= 2 else []))
                    p.step("vector", lambda e, g=g, s=s, Brel=Brel, i=i:
                           e.tensor_tensor(
                        ap_md(attn[:, :], g * 4 * TOWN + s * 256 + Brel * 128,
                              [(TOWN, 4), (1, 128)]),
                        ap_md(attn[:, :], g * 4 * TOWN + s * 256 + Brel * 128,
                              [(TOWN, 4), (1, 128)]),
                        rbrep[:, (i % 2) * 512:(i % 2) * 512 + 512], ALU.mult),
                        waits=[rrep], sem="ve")
                    mults_ve.append(("ve", p.cur("ve")))
        return wo_done

    def oproj_phase(wo_done):
        mmv, vev = p.cur("mm"), p.cur("ve")
        xrel = []
        for tb in range(NT):
            sn = rr_sem()
            p.step("sync", lambda e, tb=tb: e.dma_start(
                out=x[:, tb * D:(tb + 1) * D],
                in_=hap(xspill.ap(), tb * 128 * D, [(D, P), (1, D)])),
                waits=[("mm", mmv), ("ve", vev), ("wsp", p.cur("wsp")),
                       (sn, p.cur(sn))],
                sem=sn, n=16)
            xrel.append((sn, p.cur(sn)))
        ev_prev = {}
        for tb in range(NT):
            ps = psA if tb % 2 == 0 else psB
            w = [("ww", wo_done), ("ve", vev)]
            if tb >= 2:
                w.append(("ve", ev_prev[tb - 2]))
            for c in range(16):
                lhs = attn[:, c * TOWN + tb * 128: c * TOWN + tb * 128 + 128]
                for ob in range(4):
                    p.step("tensor", lambda e, lhs=lhs, ps=ps, c=c, ob=ob: e.matmul(
                        ps[:, ob * 512:(ob + 1) * 512], lhs,
                        wo[:, c * D + ob * 512: c * D + ob * 512 + 512],
                        start=(c == 0), stop=False),
                        waits=(w if (c == 0 and ob == 0) else []), sem=None)
            for ob in range(4):
                p.step("tensor", lambda e, ps=ps, ob=ob: e.matmul(
                    ps[:, ob * 512:(ob + 1) * 512], ones16[0:1, 0:128],
                    bo_row[0:1, ob * 512:(ob + 1) * 512], start=False, stop=True),
                    sem="mm")
            mmv2 = p.cur("mm")
            for ob in range(4):
                p.step("vector", lambda e, ps=ps, tb=tb, ob=ob: e.scalar_tensor_tensor(
                    x[:, tb * D + ob * 512: tb * D + ob * 512 + 512],
                    ps[:, ob * 512:(ob + 1) * 512], 1.0,
                    x[:, tb * D + ob * 512: tb * D + ob * 512 + 512],
                    ALU.mult, ALU.add),
                    waits=[("mm", mmv2), xrel[tb]], sem="ve")
            ev_prev[tb] = p.cur("ve")
        return p.cur("mm")

    import contextlib
    with contextlib.ExitStack() as es:
        psA = es.enter_context(nc.psum_tensor("psA", [P, 2048], F32))
        psB = es.enter_context(nc.psum_tensor("psB", [P, 2048], F32))
        SEM = {}
        for sname in sems:
            SEM[sname] = es.enter_context(nc.semaphore(sname))
        block = es.enter_context(nc.Block())

        init_phase()
        ln_stats(True, p.cur("wx"))
        sve = ln_gmax_allreduce(ar1i, ar1o)
        _, tp_ffn1 = ffn(sve, store_after=(TRUNC == 1))
        if TRUNC >= 2:
            cc_ag, qhat_tp = qkv_phase(tp_ffn1)
        if TRUNC >= 3:
            wo_done = attention_phase(cc_ag, qhat_tp)
            mm_oproj = oproj_phase(wo_done)
            if TRUNC == 3:
                vv = p.cur("ve")
                for tb in range(NT):
                    p.step("sync", lambda e, tb=tb: e.dma_start(
                        out=hap(y_out, tb * 128 * D, [(D, P), (1, D)]),
                        in_=x[:, tb * D:(tb + 1) * D]), waits=[("ve", vv)],
                        sem="wst", n=16)
        if TRUNC >= 4:
            for c in range(16):
                p.step("gpsimd", lambda e, c=c: e.dma_start(
                    out=wfT[:, c * D:(c + 1) * D],
                    in_=hap(wfT_in, c * P * D, [(D, P), (1, D)])),
                    waits=[("mm", mm_oproj)], sem="ww", n=16)
            ln_stats(True, 0, extra=[("ve", p.cur("ve")), ("mm", mm_oproj)])
            sve2 = ln_gmax_allreduce(ar2i, ar2o)
            ffn(sve2, store_after=True, region_mm=mm_oproj)

        stv = p.cur("wst")
        p.barrier_wait("sync", [("wst", stv)])

        for ename, attr in [("tensor", block.tensor), ("vector", block.vector),
                            ("scalar", block.scalar), ("sync", block.sync),
                            ("gpsimd", block.gpsimd)]:
            def body(eng, ename=ename):
                for fn in p.q[ename]:
                    fn(eng, SEM)
            attr(body)
    return nc


def make_inputs(ins, c, TRUNC):
    j = c % 4
    b = c // 4
    f32 = np.float32
    xfull = np.asarray(ins["L0_attn_x"], f32)
    xc = np.ascontiguousarray(xfull[b, j::4, :])
    pos = np.arange(j, 4096, 4)
    hdhalf = np.arange(0, HD, 2)
    inv = (1.0 / 10000.0 ** (hdhalf / HD))
    fr = np.outer(pos, inv)
    cosf, sinf = np.cos(fr).astype(f32), np.sin(fr).astype(f32)
    cost = np.ascontiguousarray(
        cosf.reshape(NT, 128, 64).transpose(1, 0, 2).reshape(128, NT * 64))
    sint = np.ascontiguousarray(
        sinf.reshape(NT, 128, 64).transpose(1, 0, 2).reshape(128, NT * 64))
    t_ = np.arange(128)
    tri = np.zeros((128, 4 * 256), np.float16)
    for r in range(4):
        m = (t_[None, :] > t_[:, None]) | ((t_[None, :] == t_[:, None]) & (j >= r))
        tri[:, r * 256:r * 256 + 128] = np.where(m, 0.0, -3840.0).astype(np.float16)
    qns = np.asarray(ins["L0_attn_q_norm_scale"], f32).reshape(1, HD)
    kns = np.asarray(ins["L0_attn_k_norm_scale"], f32).reshape(1, HD)
    return {
        "x": xc,
        "wq": np.asarray(ins["L0_attn_wq"], f32),
        "wk": np.asarray(ins["L0_attn_wk"], f32),
        "wv": np.asarray(ins["L0_attn_wv"], f32),
        "wo": np.asarray(ins["L0_attn_wo"], f32),
        "wfT": np.ascontiguousarray(np.asarray(ins["L0_ffn_weight"], f32).T),
        "bq": np.asarray(ins["L0_attn_bq"], f32).reshape(1, D),
        "bk": np.asarray(ins["L0_attn_bk"], f32).reshape(1, 512),
        "bv": np.asarray(ins["L0_attn_bv"], f32).reshape(1, 512),
        "bo": np.asarray(ins["L0_attn_bo"], f32).reshape(1, D),
        "bf": np.asarray(ins["L0_ffn_bias"], f32).reshape(1, D),
        "ws": np.full((1, 16), np.asarray(ins["L0_ffn_weight_scale"], f32)),
        "qnsr": np.ascontiguousarray(np.broadcast_to(np.tile(qns, 16), (P, D))),
        "knsr": np.ascontiguousarray(np.broadcast_to(np.tile(kns, 4), (P, 512))),
        "qns": qns, "kns": kns,
        "cosn": cost, "sinn": sint, "tri": tri,
        "ident": np.eye(128, dtype=np.float16),
    }


def kernel(**ins):
    TRUNC = 4
    nc = build_nc(0, TRUNC)
    in_maps = [make_inputs(ins, c, TRUNC) for c in range(8)]
    res = run_bass_kernel_spmd(nc, in_maps, core_ids=list(range(8)))
    out = np.zeros((2, 4096, 2048), np.float32)
    for c in range(8):
        out[c // 4, (c % 4)::4, :] = res.results[c]["y"]
    return out


# version 07
# speedup vs baseline: 1.4685x; 1.0547x over previous
import numpy as np
import concourse.bass as bass
import concourse.mybir as mybir
from concourse.bass_utils import run_bass_kernel_spmd

F32 = mybir.dt.float32
F16 = mybir.dt.float16
BF16 = mybir.dt.bfloat16
FP8 = mybir.dt.float8e4
AF = mybir.ActivationFunctionType
ALU = mybir.AluOpType
AX = mybir.AxisListType

P = 128
D = 2048
TOWN = 1024
NT = 8
HEADS, GROUPS, HD = 16, 4, 128
NB = 8
EPS_LN = 1e-5
EPS_RMS = 1e-6

ENGINES = ("tensor", "vector", "scalar", "sync", "gpsimd")


def ap_md(t_ap, offset_elems, dims):
    return bass.AP(tensor=t_ap.tensor, offset=t_ap.offset + offset_elems,
                   ap=[list(t_ap.ap[0])] + [[s, c] for (s, c) in dims])


def hap(t_ap, offset_elems, dims):
    return bass.AP(tensor=t_ap.tensor, offset=t_ap.offset + offset_elems,
                   ap=[[s, c] for (s, c) in dims])


ENG_SEM = {"tensor": "mm", "vector": "ve", "scalar": "se", "gpsimd": "gp"}


class Prog:

    def __init__(self):
        self.q = {e: [] for e in ENGINES}
        self.count = {}
        self.waited = {e: {} for e in ENGINES}
        self.force_next = {e: False for e in ENGINES}

    def inc(self, sem, n=1):
        self.count[sem] = self.count.get(sem, 0) + n
        return self.count[sem]

    def cur(self, sem):
        return self.count.get(sem, 0)

    def step(self, eng, build, waits=(), sem=None, n=1, ptr=False):
        waits = list(waits)
        if eng == "vector":
            ptr = True
        if (ptr or self.force_next[eng]) and eng in ENG_SEM:
            waits.append((ENG_SEM[eng], self.cur(ENG_SEM[eng])))
        self.force_next[eng] = bool(ptr)
        wl = []
        for s, v in waits:
            if v > 0 and self.waited[eng].get(s, -1) < v:
                wl.append((s, v))
                self.waited[eng][s] = v
        tgt = self.inc(sem, n) if sem else None
        def fn(e, SEM, wl=wl, sem=sem, n=n):
            for s, v in wl:
                e.wait_ge(SEM[s], v)
            inst = build(e)
            if sem is not None:
                if inst is None:
                    raise RuntimeError("step with sem must return instruction")
                inst.then_inc(SEM[sem], n)
        self.q[eng].append(fn)
        return tgt

    def barrier_wait(self, eng, waits):
        self.step(eng, lambda e: None, waits=waits)


def build_nc(j, TRUNC):
    nc = bass.Bass("TRN2", target_bir_lowering=False)

    x_in = nc.dram_tensor("x", [TOWN, D], F32, kind="ExternalInput").ap()
    wq_in = nc.dram_tensor("wq", [D, D], F32, kind="ExternalInput").ap()
    wk_in = nc.dram_tensor("wk", [D, 512], F32, kind="ExternalInput").ap()
    wv_in = nc.dram_tensor("wv", [D, 512], F32, kind="ExternalInput").ap()
    wo_in = nc.dram_tensor("wo", [D, D], F32, kind="ExternalInput").ap()
    wfT_in = nc.dram_tensor("wfT", [D, D], F32, kind="ExternalInput").ap()
    bq_in = nc.dram_tensor("bq", [1, D], F32, kind="ExternalInput").ap()
    bk_in = nc.dram_tensor("bk", [1, 512], F32, kind="ExternalInput").ap()
    bv_in = nc.dram_tensor("bv", [1, 512], F32, kind="ExternalInput").ap()
    bo_in = nc.dram_tensor("bo", [1, D], F32, kind="ExternalInput").ap()
    bf_in = nc.dram_tensor("bf", [1, D], F32, kind="ExternalInput").ap()
    ws_in = nc.dram_tensor("ws", [1, 16], F32, kind="ExternalInput").ap()
    qnsr_in = nc.dram_tensor("qnsr", [P, D], F32, kind="ExternalInput").ap()
    knsr_in = nc.dram_tensor("knsr", [P, 512], F32, kind="ExternalInput").ap()
    qns_in = nc.dram_tensor("qns", [1, HD], F32, kind="ExternalInput").ap()
    kns_in = nc.dram_tensor("kns", [1, HD], F32, kind="ExternalInput").ap()
    cos_in = nc.dram_tensor("cosn", [P, NT * 64], F32, kind="ExternalInput").ap()
    sin_in = nc.dram_tensor("sinn", [P, NT * 64], F32, kind="ExternalInput").ap()
    tri_in = nc.dram_tensor("tri", [P, 4 * 256], F16, kind="ExternalInput").ap()
    id_in = nc.dram_tensor("ident", [P, 128], F16, kind="ExternalInput").ap()

    y_out = nc.dram_tensor("y", [TOWN, D], F32, kind="ExternalOutput").ap()

    ar1i = nc.dram_tensor("ar1i", [1, 16], F32)
    ar1o = nc.dram_tensor("ar1o", [1, 16], F32)
    ar2i = nc.dram_tensor("ar2i", [1, 16], F32)
    ar2o = nc.dram_tensor("ar2o", [1, 16], F32)
    gdram = nc.dram_tensor("gdram", [P, 8], F32)
    sums_dram = nc.dram_tensor("sums_dram", [1, 4096], F16)
    r_dram = nc.dram_tensor("r_dram", [P, 32], F16)
    AGN = GROUPS * HD * TOWN + TOWN * 512
    agi = nc.dram_tensor("agi", [1, AGN], F16)
    ago = nc.dram_tensor("ago", [4, AGN], F16)
    VOFF = GROUPS * HD * TOWN

    BASE = ((nc.sbuf_base + 63) // 64) * 64
    off = [BASE]
    def alloc(name, shape, dtype, nbytes):
        t = nc.alloc_sbuf_tensor_at(name, shape, dtype, offset=off[0])
        off[0] += (nbytes + 63) // 64 * 64
        return t
    def at(name, shape, dtype, byteoff):
        return nc.alloc_sbuf_tensor_at(name, shape, dtype, offset=BASE + byteoff)

    ones16 = alloc("ones16", [P, 128], F16, 256)
    onesbf = alloc("onesbf", [P, 128], BF16, 256)
    ones32 = alloc("ones32", [1, 128], F32, 512)
    stats = alloc("stats", [P, 96], F32, 384)
    misc = alloc("misc", [P, 16], F32, 64)
    qstat = alloc("qstat", [P, 64], F32, 256)

    g1 = alloc("g1", [1, 16], F32, 64)
    cosn = alloc("cosn_sb", [P, NT * 64], F32, NT * 64 * 4)
    sinn = alloc("sinn_sb", [P, NT * 64], F32, NT * 64 * 4)
    tri = alloc("tri_sb", [P, 4 * 256], F16, 4 * 256 * 2)
    ident = alloc("ident_sb", [P, 128], F16, 128 * 2)
    bq_row = alloc("bq_row", [1, D], F16, D * 2)
    bo_row = alloc("bo_row", [1, D], F16, D * 2)
    bk_row = alloc("bk_row", [1, 512], F16, 512 * 2)
    bv_row = alloc("bv_row", [1, 512], F16, 512 * 2)
    bf_row = alloc("bf_row", [1, D], BF16, D * 2)
    qnsr = alloc("qnsr_sb", [P, D], F16, D * 2)
    knsr = alloc("knsr_sb", [P, 512], F16, 512 * 2)
    qnsraw = alloc("qns_raw", [1, HD], F32, 512)
    knsraw = alloc("kns_raw", [1, HD], F32, 512)
    wsrow = alloc("ws_sb", [1, 16], F32, 64)
    cbuf = alloc("cbuf", [1, 16], F16, 64)
    S_END = off[0] - BASE
    assert S_END <= 29 * 1024, S_END

    A_OFF = 29 * 1024
    W_OFF = A_OFF + 32 * 1024
    X_OFF = W_OFF + 64 * 1024
    C_OFF = X_OFF + 64 * 1024
    TOP = C_OFF + 17 * 1024
    assert BASE + TOP <= nc.sbuf_top, (BASE, TOP, nc.sbuf_top)

    xqf = at("xqf", [P, 16 * TOWN], BF16, A_OFF)
    grow = at("grow", [1, 1024], F32, A_OFF + 28 * 1024)
    xhf = at("xhf", [P, 16 * TOWN], F16, A_OFF)
    attn = at("attn", [P, 16 * TOWN], F16, A_OFF)
    wfT = at("wfT", [P, 16 * D], BF16, W_OFF)
    wqh = at("wqh", [P, 16 * 1024], F16, W_OFF)
    wk = at("wk_sb", [P, 16 * 512], F16, W_OFF + 32 * 1024)
    wv = at("wv_sb", [P, 16 * 512], F16, W_OFF + 48 * 1024)
    wo = at("wo_sb", [P, 16 * D], F16, W_OFF)
    x = at("x_sb", [P, NT * D], F32, X_OFF)
    qhat = at("qhat", [P, HEADS * TOWN], F16, X_OFF)
    kown = at("kown", [P, GROUPS * TOWN], F16, X_OFF + 32 * 1024)
    vown = at("vown", [P, NT * 512], F16, X_OFF + 40 * 1024)
    qscr = at("qscr", [P, D], F32, X_OFF + 48 * 1024)
    kblk = at("kblk", [P, 2 * 512], F16, X_OFF + 32 * 1024)
    vblk = at("vblk", [P, 2 * 512], F16, X_OFF + 34 * 1024)
    pblk = at("pblk", [P, 2 * 1024], F16, X_OFF + 36 * 1024)
    sums_row = at("sums_row", [1, 4096], F16, X_OFF + 40 * 1024)
    r_row = at("r_row", [1, 4096], F16, X_OFF + 48 * 1024)
    rspread = at("rspread", [P, 32], F32, X_OFF + 56 * 1024)
    rbrep = at("rbrep", [P, 1024], F16, X_OFF + 56 * 1024 + 256)
    scr32 = at("scr32", [P, D], F32, C_OFF)
    scr8 = at("scr8", [P, D], FP8, C_OFF + 8 * 1024)
    scr16 = at("scr16", [P, D], BF16, C_OFF + 10 * 1024)
    scr16h = at("scr16h", [P, D], F16, C_OFF + 10 * 1024)
    scr16q = at("scr16q", [P, 1024], F16, C_OFF + 14 * 1024)
    scr16k = at("scr16k", [P, 512], F16, C_OFF + 16 * 1024)

    xspill = nc.dram_tensor("xspill", [TOWN, D], F32)

    p = Prog()
    state = {}
    sems = ["wx", "ww", "wsp", "wst", "mm", "ve", "se",
            "gp", "cc"] + ["wtp%d" % i for i in range(8)] + ["wgp%d" % i for i in range(4)]
    gp_idx = [0]
    def gp_step(build, waits=()):
        sn = "wgp%d" % (gp_idx[0] % 4)
        gp_idx[0] += 1
        v = p.step("gpsimd", build, waits=list(waits) + [(sn, p.cur(sn))],
                   sem=sn, n=16)
        return (sn, v)
    def rr_sem():
        sn = "wtp%d" % (tp_idx[0] % 8)
        tp_idx[0] += 1
        return sn
    tp_idx = [0]
    def tp_step(build, waits=()):
        sn = "wtp%d" % (tp_idx[0] % 8)
        tp_idx[0] += 1
        v = p.step("sync", build, waits=list(waits) + [(sn, p.cur(sn))],
                   sem=sn, n=16)
        return (sn, v)
    def tp_all():
        return [("wtp%d" % i, p.cur("wtp%d" % i)) for i in range(8)]

    def init_phase():
        p.step("sync", lambda e: e.dma_start(
            out=ap_md(x[:, :], 0, [(D, NT), (1, D)]),
            in_=hap(x_in, 0, [(D, P), (P * D, NT), (1, D)])), sem="wx", n=16)
        for nm, dst, src in [("cos", cosn, cos_in), ("sin", sinn, sin_in)]:
            p.step("sync", lambda e, d=dst, s=src: e.dma_start(out=d[:, :], in_=s),
                   sem="wx", n=16)
        p.step("sync", lambda e: e.dma_start(out=tri[:, :], in_=tri_in), sem="wx", n=16)
        p.step("sync", lambda e: e.dma_start(out=ident[:, :], in_=id_in), sem="wx", n=16)
        for dst, src in [(wsrow, ws_in), (qnsraw, qns_in),
                         (knsraw, kns_in)]:
            p.step("sync", lambda e, d=dst, s=src: e.dma_start(out=d[:, :], in_=s),
                   sem="wx", n=16)
        for dst, src in [(bq_row, bq_in), (bk_row, bk_in), (bv_row, bv_in),
                         (bo_row, bo_in), (qnsr, qnsr_in), (knsr, knsr_in)]:
            p.step("gpsimd", lambda e, d=dst, s=src: e.dma_start(out=d[:, :], in_=s),
                   sem="ww", n=16)
        for c in range(16):
            p.step("gpsimd", lambda e, c=c: e.dma_start(
                out=wfT[:, c * D:(c + 1) * D],
                in_=hap(wfT_in, c * P * D, [(D, P), (1, D)])), sem="ww", n=16)
        for t, val in [(ones16, 1.0), (onesbf, 1.0), (ones32, 1.0), (g1, 0.0), (misc, 0.0)]:
            p.step("gpsimd", lambda e, t=t, v=val: e.memset(t[:, :], v), sem="gp")
        state["gp_init"] = p.cur("gp")
        p.step("sync", lambda e: e.dma_start(out=scr32[0:1, 0:D], in_=bf_in),
               sem="wx", n=16)
        wxv = p.cur("wx")
        p.step("vector", lambda e: e.reciprocal(misc[0:1, 3:4], wsrow[0:1, 0:1]),
               waits=[("wx", wxv), ("gp", state["gp_init"])], sem="ve")
        p.step("vector", lambda e: e.tensor_single_scalar(
            scr32[0:1, 0:D], scr32[0:1, 0:D], misc[0:1, 3:4], ALU.mult),
            sem="ve", ptr=True)
        p.step("vector", lambda e: e.tensor_copy(bf_row[:, :], scr32[0:1, 0:D]), sem="ve")
        wwv = p.cur("ww")
        p.step("tensor", lambda e: e.matmul(
            psB[:, 0:1], ones32[0:1, 0:128], wsrow[0:1, 0:1], start=True, stop=True),
            waits=[("wx", wxv), ("gp", p.cur("gp"))], sem="mm")
        p.step("vector", lambda e: e.tensor_scalar_mul(misc[:, 4:5], psB[:, 0:1], 0.5),
               waits=[("mm", p.cur("mm"))], sem="ve")
        state["ve_init"] = p.cur("ve")

    def ln_stats_tile(tb, minmax, w):
        sl = x[:, tb * D:(tb + 1) * D]
        p.step("scalar", lambda e, sl=sl, tb=tb: e.activation(
            scr32[:, :], sl, AF.Identity,
            accum_out=stats[:, tb:tb + 1]), waits=w, sem="se", ptr=True)
        p.step("scalar", lambda e, sl=sl, tb=tb: e.activation(
            scr32[:, :], sl, AF.Square,
            accum_out=stats[:, 8 + tb:9 + tb]), sem="se", ptr=True)
        if minmax:
            p.step("vector", lambda e, sl=sl, tb=tb: e.tensor_reduce(
                stats[:, 16 + tb:17 + tb], sl, axis=AX.X, op=ALU.min),
                waits=[x_ for x_ in w if x_[0] != "ve"], sem="ve")
            p.step("vector", lambda e, sl=sl, tb=tb: e.tensor_reduce(
                stats[:, 24 + tb:25 + tb], sl, axis=AX.X, op=ALU.max), sem="ve")

    def ln_stats(minmax, xwait, extra=(), pretiled=False):
        if not pretiled:
            for tb in range(NT):
                w0 = ([("wx", xwait)] + list(extra)) if tb == 0 else []
                ln_stats_tile(tb, minmax, w0)
        p.step("vector", lambda e: e.tensor_scalar_mul(
            stats[:, 32:40], stats[:, 0:8], 1.0 / D),
            waits=[("se", p.cur("se"))], sem="ve")
        p.step("vector", lambda e: e.tensor_scalar_mul(
            stats[:, 48:56], stats[:, 8:16], 1.0 / D), sem="ve")
        p.step("vector", lambda e: e.tensor_mul(
            stats[:, 56:64], stats[:, 32:40], stats[:, 32:40]), sem="ve")
        p.step("vector", lambda e: e.tensor_sub(
            stats[:, 48:56], stats[:, 48:56], stats[:, 56:64]), sem="ve")
        p.step("vector", lambda e: e.tensor_scalar_add(
            stats[:, 48:56], stats[:, 48:56], EPS_LN), sem="ve")
        vv = p.cur("ve")
        p.step("scalar", lambda e: e.activation(
            stats[:, 56:64], stats[:, 48:56], AF.Sqrt), waits=[("ve", vv)], sem="se")
        sv = p.cur("se")
        p.step("vector", lambda e: e.reciprocal(stats[:, 64:72], stats[:, 56:64]),
               waits=[("se", sv)], sem="ve")
        p.step("vector", lambda e: e.tensor_mul(
            stats[:, 64:72], stats[:, 64:72], stats[:, 48:56]), sem="ve")
        p.step("vector", lambda e: e.tensor_add(
            stats[:, 64:72], stats[:, 64:72], stats[:, 56:64]), sem="ve")
        p.step("vector", lambda e: e.tensor_scalar_mul(
            stats[:, 56:64], stats[:, 64:72], 0.5), sem="ve")
        p.step("vector", lambda e: e.reciprocal(stats[:, 40:48], stats[:, 56:64]),
               sem="ve")

    def ln_gmax_allreduce(ar_in_t, ar_out_t):
        p.step("vector", lambda e: e.tensor_sub(
            stats[:, 56:64], stats[:, 24:32], stats[:, 32:40]), sem="ve")
        p.step("vector", lambda e: e.tensor_sub(
            stats[:, 64:72], stats[:, 32:40], stats[:, 16:24]), sem="ve")
        p.step("vector", lambda e: e.tensor_max(
            stats[:, 88:96], stats[:, 56:64], stats[:, 64:72]), sem="ve")
        p.step("vector", lambda e: e.tensor_mul(
            stats[:, 88:96], stats[:, 88:96], stats[:, 40:48]), sem="ve")
        vv = p.cur("ve")
        p.step("sync", lambda e: e.dma_start(out=gdram.ap(),
               in_=stats[:, 88:96]), waits=[("ve", vv)], sem="wsp", n=16)
        p.step("sync", lambda e: e.dma_start(out=grow[0:1, :],
               in_=hap(gdram.ap(), 0, [(1024, 1), (1, 1024)])),
               waits=[("wsp", p.cur("wsp"))], sem="wsp", n=16)
        p.step("vector", lambda e: e.reduce_max(g1[0:1, 0:1], grow[0:1, :],
               axis=AX.X, apply_absolute_value=False),
               waits=[("wsp", p.cur("wsp"))], sem="ve")
        vv = p.cur("ve")
        w1 = gp_step(lambda e: e.dma_start(out=ar_in_t.ap(), in_=g1[:, :]),
                     waits=[("ve", vv)])
        p.step("gpsimd", lambda e: e.collective_compute(
            "AllReduce", ALU.max, replica_groups=[list(range(8))],
            ins=[ar_in_t.ap()], outs=[ar_out_t.ap()]),
            waits=[w1], sem="cc")
        w2 = gp_step(lambda e: e.dma_start(out=g1[:, :], in_=ar_out_t.ap()),
                     waits=[("cc", p.cur("cc"))])
        p.step("vector", lambda e: e.tensor_scalar(
            g1[0:1, 1:2], g1[0:1, 0:1], 1.0 / 448.0, 1e-7, ALU.mult, ALU.max),
            waits=[w2], sem="ve")
        p.step("vector", lambda e: e.reciprocal(g1[0:1, 2:3], g1[0:1, 1:2]), sem="ve")
        vv = p.cur("ve")
        p.step("tensor", lambda e: e.matmul(
            psB[:, 0:1], ones32[0:1, 0:128], g1[0:1, 2:3], start=True, stop=True),
            waits=[("ve", vv)], sem="mm")
        p.step("vector", lambda e: e.tensor_scalar_mul(
            misc[:, 2:3], psB[:, 0:1], 0.5), waits=[("mm", p.cur("mm"))], sem="ve")
        p.step("vector", lambda e: e.tensor_single_scalar(
            stats[:, 72:80], stats[:, 40:48], misc[:, 2:3], ALU.mult),
            sem="ve", ptr=True)
        p.step("vector", lambda e: e.tensor_mul(
            stats[:, 80:88], stats[:, 32:40], stats[:, 72:80]), sem="ve")
        p.step("vector", lambda e: e.tensor_scalar_mul(
            stats[:, 80:88], stats[:, 80:88], -1.0), sem="ve")
        return p.cur("ve")

    def ffn(scale_ready_ve, store_after, region_mm=0, post_evict=None):
        tp_done = []
        for tb in range(NT):
            sl = x[:, tb * D:(tb + 1) * D]
            qw = [("ve", scale_ready_ve)]
            if tb == 0:
                qw.append(("mm", region_mm))
            else:
                qw.append(tp_done[tb - 1])
            p.step("vector", lambda e, sl=sl, tb=tb: e.tensor_scalar(
                scr8[:, :], sl, stats[:, 72 + tb:73 + tb], stats[:, 80 + tb:81 + tb],
                ALU.mult, ALU.add), waits=qw, sem="ve", ptr=True)
            p.step("vector", lambda e: e.tensor_scalar_mul(
                scr16[:, :], scr8[:, :], 2.0), sem="ve")
            vv = p.cur("ve")
            tp_done.append(tp_step(lambda e, tb=tb: e.dma_start(
                out=ap_md(xqf[:, :], tb * 128, [(TOWN, 16), (1, 128)]),
                in_=scr16[:, :], transpose=True),
                waits=[("ve", vv), ("mm", region_mm)]))
        wwv = p.cur("ww")
        ev_of_tb = []
        for tb in range(NT):
            ps = psA if tb % 2 == 0 else psB
            w = [tp_done[tb], ("ww", wwv)]
            if tb >= 2:
                w.append(("ve", ev_of_tb[tb - 2]))
            for c in range(16):
                lhs = xqf[:, c * TOWN + tb * 128: c * TOWN + tb * 128 + 128]
                for ob in range(4):
                    p.step("tensor", lambda e, lhs=lhs, ps=ps, c=c, ob=ob: e.matmul(
                        ps[:, ob * 512:(ob + 1) * 512], lhs,
                        wfT[:, c * D + ob * 512: c * D + ob * 512 + 512],
                        start=(c == 0), stop=False),
                        waits=(w if (c == 0 and ob == 0) else []),
                        sem=("mm" if (c == 15 and ob == 3) else None))
            for ob in range(4):
                p.step("tensor", lambda e, ps=ps, ob=ob: e.matmul(
                    ps[:, ob * 512:(ob + 1) * 512], onesbf[0:1, 0:128],
                    bf_row[0:1, ob * 512:(ob + 1) * 512], start=False, stop=True),
                    waits=[("ve", state["ve_init"])], sem="mm")
            mmv = p.cur("mm")
            for ob in range(4):
                p.step("vector", lambda e, ps=ps, tb=tb, ob=ob: e.scalar_tensor_tensor(
                    x[:, tb * D + ob * 512: tb * D + ob * 512 + 512],
                    ps[:, ob * 512:(ob + 1) * 512], misc[:, 4:5],
                    x[:, tb * D + ob * 512: tb * D + ob * 512 + 512],
                    ALU.mult, ALU.add),
                    waits=[("mm", mmv)], sem="ve", ptr=True)
            ev_of_tb.append(p.cur("ve"))
            if post_evict is not None:
                post_evict(tb, p.cur("ve"))
            if store_after:
                vv = p.cur("ve")
                p.step("sync", lambda e, tb=tb: e.dma_start(
                    out=hap(y_out, tb * 128 * D, [(D, P), (1, D)]),
                    in_=x[:, tb * D:(tb + 1) * D]), waits=[("ve", vv)],
                    sem="wst", n=16)
        return ev_of_tb, tp_done

    def rms_rope_transpose(src, nheads, scol, rcol, tcol, nsr, nsoff, tb, dst,
                           dstoff, dst16, h16_free):
        W = nheads * HD
        p.step("vector", lambda e: e.scalar_tensor_tensor(
            scr32[:, 0:W], src, 1.0, src, ALU.mult, ALU.mult),
            waits=[h16_free], sem="ve")
        p.step("vector", lambda e: e.reduce_sum(
            qstat[:, scol:scol + nheads],
            ap_md(scr32[:, :], 0, [(HD, nheads), (1, HD)]), axis=AX.X), sem="ve")
        p.step("vector", lambda e: e.tensor_scalar(
            qstat[:, scol:scol + nheads], qstat[:, scol:scol + nheads],
            1.0 / HD, EPS_RMS, ALU.mult, ALU.add), sem="ve")
        vv = p.cur("ve")
        p.step("scalar", lambda e: e.activation(
            qstat[:, rcol:rcol + nheads], qstat[:, scol:scol + nheads], AF.Sqrt),
            waits=[("ve", vv)], sem="se")
        sv = p.cur("se")
        p.step("vector", lambda e: e.reciprocal(
            qstat[:, tcol:tcol + nheads], qstat[:, rcol:rcol + nheads]),
            waits=[("se", sv)], sem="ve")
        p.step("vector", lambda e: e.tensor_mul(
            qstat[:, tcol:tcol + nheads], qstat[:, tcol:tcol + nheads],
            qstat[:, scol:scol + nheads]), sem="ve")
        p.step("vector", lambda e: e.tensor_add(
            qstat[:, tcol:tcol + nheads], qstat[:, tcol:tcol + nheads],
            qstat[:, rcol:rcol + nheads]), sem="ve")
        p.step("vector", lambda e: e.tensor_scalar_mul(
            qstat[:, rcol:rcol + nheads], qstat[:, tcol:tcol + nheads],
            0.5), sem="ve")
        p.step("vector", lambda e: e.reciprocal(
            qstat[:, rcol:rcol + nheads], qstat[:, rcol:rcol + nheads]), sem="ve")
        for hl in range(nheads):
            p.step("vector", lambda e, hl=hl: e.tensor_single_scalar(
                ap_md(src, hl * HD, [(1, HD)]), ap_md(src, hl * HD, [(1, HD)]),
                qstat[:, rcol + hl:rcol + hl + 1], ALU.mult), sem="ve")
        p.step("vector", lambda e: e.tensor_mul(
            src, src, nsr[:, nsoff:nsoff + W]), sem="ve")
        cosb = ap_md(cosn[:, :], tb * 64, [(0, nheads), (1, 64)])
        sinb = ap_md(sinn[:, :], tb * 64, [(0, nheads), (1, 64)])
        a3 = ap_md(src, 0, [(HD, nheads), (1, 64)])
        b3 = ap_md(src, 64, [(HD, nheads), (1, 64)])
        t1 = ap_md(scr32[:, :], 1024, [(64, nheads), (1, 64)])
        t2 = ap_md(scr32[:, :], 1024 + nheads * 64, [(64, nheads), (1, 64)])
        out_re = ap_md(dst16[:, :], 0, [(HD, nheads), (1, 64)])
        out_im = ap_md(dst16[:, :], 64, [(HD, nheads), (1, 64)])
        p.step("vector", lambda e: e.tensor_tensor(t1, a3, cosb, ALU.mult), sem="ve")
        p.step("vector", lambda e: e.tensor_tensor(t2, b3, sinb, ALU.mult), sem="ve")
        p.step("vector", lambda e: e.tensor_tensor(out_re, t1, t2, ALU.subtract), sem="ve")
        p.step("vector", lambda e: e.tensor_tensor(t1, a3, sinb, ALU.mult), sem="ve")
        p.step("vector", lambda e: e.tensor_tensor(t2, b3, cosb, ALU.mult), sem="ve")
        p.step("vector", lambda e: e.tensor_tensor(out_im, t1, t2, ALU.add), sem="ve")
        vv = p.cur("ve")
        return tp_step(lambda e: e.dma_start(
            out=ap_md(dst, dstoff + tb * 128, [(TOWN, nheads), (1, 128)]),
            in_=dst16[:, :], transpose=True),
            waits=[("ve", vv)])

    def qkv_phase(tp_ffn1):
        ln_stats(False, 0, extra=[], pretiled=(TRUNC > 1))
        p.step("vector", lambda e: e.tensor_mul(
            stats[:, 80:88], stats[:, 32:40], stats[:, 40:48]), sem="ve")
        p.step("vector", lambda e: e.tensor_scalar_mul(
            stats[:, 80:88], stats[:, 80:88], -1.0), sem="ve")
        sve = p.cur("ve")
        mm_ffn1 = p.cur("mm")
        for c in range(16):
            p.step("gpsimd", lambda e, c=c: e.dma_start(
                out=wqh[:, c * 1024:(c + 1) * 1024],
                in_=hap(wq_in, c * P * D, [(D, P), (1, 1024)])),
                waits=[("mm", mm_ffn1)], sem="ww", n=16)
            p.step("gpsimd", lambda e, c=c: e.dma_start(
                out=wk[:, c * 512:(c + 1) * 512],
                in_=hap(wk_in, c * P * 512, [(512, P), (1, 512)])), sem="ww", n=16)
            p.step("gpsimd", lambda e, c=c: e.dma_start(
                out=wv[:, c * 512:(c + 1) * 512],
                in_=hap(wv_in, c * P * 512, [(512, P), (1, 512)])), sem="ww", n=16)
        ww_A = p.cur("ww")
        tpxh = []
        for tb in range(NT):
            sl = x[:, tb * D:(tb + 1) * D]
            qw = [("ve", sve)]
            qw.append(tp_ffn1[-1] if tb == 0 else tpxh[tb - 1])
            p.step("vector", lambda e, sl=sl, tb=tb: e.tensor_scalar(
                scr16h[:, :], sl, stats[:, 40 + tb:41 + tb],
                stats[:, 80 + tb:81 + tb], ALU.mult, ALU.add), waits=qw,
                sem="ve", ptr=True)
            vv = p.cur("ve")
            tpxh.append(tp_step(lambda e, tb=tb: e.dma_start(
                out=ap_md(xhf[:, :], tb * 128, [(TOWN, 16), (1, 128)]),
                in_=scr16h[:, :], transpose=True), waits=[("ve", vv)]))
            p.step("sync", lambda e, tb=tb: e.dma_start(
                out=hap(xspill.ap(), tb * 128 * D, [(D, P), (1, D)]),
                in_=x[:, tb * D:(tb + 1) * D]), sem="wsp", n=16)
        spill_done = p.cur("wsp")
        last_tp = {"q": tpxh[-1], "k": tpxh[-1]}
        ev_prev = {}
        for half in range(2):
            if half == 1:
                p.step("gpsimd", lambda e: e.collective_compute(
                    "AllGather", ALU.bypass,
                    replica_groups=[[0, 1, 2, 3], [4, 5, 6, 7]],
                    ins=[agi.ap()], outs=[ago.ap()]),
                    waits=[("wsp", p.cur("wsp"))], sem="cc")
                mm_passA = p.cur("mm")
                for c in range(16):
                    p.step("gpsimd", lambda e, c=c: e.dma_start(
                        out=wqh[:, c * 1024:(c + 1) * 1024],
                        in_=hap(wq_in, c * P * D + 1024, [(D, P), (1, 1024)])),
                        waits=[("mm", mm_passA)], sem="ww", n=16)
            wwv = p.cur("ww")
            for tb in range(NT):
                gt = half * NT + tb
                ps = psA if tb % 2 == 0 else psB
                w = [tpxh[tb], ("ww", wwv)]
                if gt >= 2:
                    w.append(("ve", ev_prev[gt - 2]))
                nmm = 4 if half == 0 else 2
                for c in range(16):
                    lhs = xhf[:, c * TOWN + tb * 128: c * TOWN + tb * 128 + 128]
                    outs = [ps[:, 0:512], ps[:, 512:1024]]
                    rhss = [wqh[:, c * 1024:c * 1024 + 512],
                            wqh[:, c * 1024 + 512:(c + 1) * 1024]]
                    if half == 0:
                        outs += [ps[:, 1024:1536], ps[:, 1536:2048]]
                        rhss += [wk[:, c * 512:(c + 1) * 512],
                                 wv[:, c * 512:(c + 1) * 512]]
                    for i in range(nmm):
                        p.step("tensor", lambda e, o=outs[i], lhs=lhs, r=rhss[i],
                               c=c: e.matmul(o, lhs, r, start=(c == 0), stop=False),
                               waits=(w if (c == 0 and i == 0) else []),
                               sem=None)
                boff = 0 if half == 0 else 1024
                bl = [(ps[:, 0:512], bq_row[0:1, boff:boff + 512]),
                      (ps[:, 512:1024], bq_row[0:1, boff + 512:boff + 1024])]
                if half == 0:
                    bl += [(ps[:, 1024:1536], bk_row[0:1, :]),
                           (ps[:, 1536:2048], bv_row[0:1, :])]
                for o, b in bl:
                    p.step("tensor", lambda e, o=o, b=b: e.matmul(
                        o, ones16[0:1, 0:128], b, start=False, stop=True), sem="mm")
                mmv = p.cur("mm")
                p.step("vector", lambda e, ps=ps: e.tensor_copy(
                    qscr[:, 0:1024], ps[:, 0:1024]),
                    waits=[("mm", mmv), ("wsp", spill_done)], sem="ve")
                if half == 0:
                    p.step("vector", lambda e, ps=ps: e.tensor_copy(
                        qscr[:, 1024:1536], ps[:, 1024:1536]), sem="ve")
                    p.step("vector", lambda e, ps=ps, tb=tb: e.tensor_copy(
                        vown[:, tb * 512:(tb + 1) * 512], ps[:, 1536:2048]), sem="ve")
                ev_prev[gt] = p.cur("ve")
                if half == 0:
                    last_tp["k"] = rms_rope_transpose(
                        qscr[:, 1024:1536], 4, 8, 40, 56, knsr, 0,
                        tb, kown[:, :], 0, scr16k, last_tp["k"])
                last_tp["q"] = rms_rope_transpose(
                    qscr[:, 0:1024], 8, 0, 16, 32, qnsr, half * 1024,
                    tb, qhat[:, :], half * 8 * TOWN, scr16q, last_tp["q"])
                if half == 0:
                    p.step("sync", lambda e, tb=tb: e.dma_start(
                        out=hap(agi.ap(), tb * 128,
                                [(TOWN, P), (131072, 4), (1, 128)]),
                        in_=ap_md(kown[:, :], tb * 128, [(TOWN, 4), (1, 128)])),
                        waits=[last_tp["k"]], sem="wsp", n=16)
                    p.step("sync", lambda e, tb=tb: e.dma_start(
                        out=hap(agi.ap(), VOFF + tb * 128 * 512,
                                [(512, P), (1, 512)]),
                        in_=ap_md(vown[:, :], tb * 512, [(1, 512)])),
                        waits=[("ve", ev_prev[tb])], sem="wsp", n=16)
        qhat_tp = tp_all()
        return p.cur("cc"), qhat_tp

    def attention_phase(cc_ag, qhat_tp):
        p.step("vector", lambda e: e.reduce_max(
            g1[0:1, 4:5], qnsraw[0:1, :], axis=AX.X, apply_absolute_value=True),
            sem="ve")
        p.step("vector", lambda e: e.reduce_max(
            g1[0:1, 5:6], knsraw[0:1, :], axis=AX.X, apply_absolute_value=True),
            sem="ve")
        p.step("vector", lambda e: e.tensor_mul(
            g1[0:1, 6:7], g1[0:1, 4:5], g1[0:1, 5:6]), sem="ve")
        p.step("vector", lambda e: e.tensor_copy(cbuf[0:1, 0:1], g1[0:1, 6:7]),
               sem="ve")
        vv = p.cur("ve")
        p.step("tensor", lambda e: e.matmul(
            psB[:, 0:1], ones16[0:1, 0:128], cbuf[0:1, 0:1], start=True, stop=True),
            waits=[("ve", vv)], sem="mm")
        p.step("vector", lambda e: e.tensor_scalar_mul(
            misc[:, 5:6], psB[:, 0:1], -1.0), waits=[("mm", p.cur("mm"))], sem="ve")
        negc_ve = p.cur("ve")
        mm_qkv = p.cur("mm")
        for c in range(16):
            p.step("gpsimd", lambda e, c=c: e.dma_start(
                out=wo[:, c * D:(c + 1) * D],
                in_=hap(wo_in, c * P * D, [(D, P), (1, D)])),
                waits=[("mm", mm_qkv)], sem="ww", n=16)
        wo_done = p.cur("ww")

        ikv = 0
        it = 0
        kv_done = {}
        kv_free_mm = {0: 0, 1: 0}
        sp_free_se = {0: 0, 1: 0}
        pb_free_mm = {0: 0, 1: 0}
        for s in range(4):
            for g in range(4):
                g_ev = state.get("postev_ve", 0) if g == 0 else p.cur("ve")
                first = {0: True, 1: True}
                pending = None
                def flush(pending):
                    pa, buf, r, B0rel, lasts = pending
                    for Brel in range(B0rel, 2):
                        st_ = first[Brel]
                        avw = [("se", sp_free_se[pa])]
                        if st_:
                            avw.append(("ve", g_ev))
                        p.step("tensor", lambda e, Brel=Brel, pa=pa,
                               st_=st_, lastk=lasts[Brel], buf=buf, r=r: e.matmul(
                            psA[:, Brel * 512:(Brel + 1) * 512],
                            vblk[:, buf * 512 + r * 128: buf * 512 + (r + 1) * 128],
                            pblk[:, pa * 1024 + Brel * 512: pa * 1024 + (Brel + 1) * 512],
                            start=st_, stop=lasts[Brel]), waits=avw, sem="mm")
                        p.step("tensor", lambda e, Brel=Brel, pa=pa,
                               st_=st_, lastk=lasts[Brel]: e.matmul(
                            psA[0:1, 1024 + Brel * 512: 1024 + (Brel + 1) * 512],
                            ones16[:, 0:1],
                            pblk[:, pa * 1024 + Brel * 512: pa * 1024 + (Brel + 1) * 512],
                            start=st_, stop=lasts[Brel]), sem="mm")
                        first[Brel] = False
                    pb_free_mm[pa] = p.cur("mm")
                    kv_free_mm[buf] = p.cur("mm")
                for kb in range(2 * s + 2):
                    buf = ikv % 2
                    ikv += 1
                    kvsem = rr_sem()
                    p.step("sync", lambda e, g=g, kb=kb, buf=buf: e.dma_start(
                        out=kblk[:, buf * 512:(buf + 1) * 512],
                        in_=hap(ago.ap(), g * 131072 + kb * 128,
                                [(TOWN, P), (AGN, 4), (1, 128)])),
                        waits=[("cc", cc_ag), ("mm", kv_free_mm[buf]),
                               (kvsem, p.cur(kvsem))],
                        sem=kvsem, n=16)
                    p.step("sync", lambda e, g=g, kb=kb, buf=buf: e.dma_start(
                        out=vblk[:, buf * 512:(buf + 1) * 512],
                        in_=hap(ago.ap(), VOFF + kb * 65536 + g * 128,
                                [(512, P), (AGN, 4), (1, 128)])), sem=kvsem, n=16)
                    kvv = (kvsem, p.cur(kvsem))
                    B0rel = max(kb - 2 * s, 0)
                    nB = 2 - B0rel
                    W = nB * 128
                    diag = kb - 2 * s if kb >= 2 * s else None
                    for r in range(4):
                        pa = it % 2
                        it += 1
                        for hl in range(4):
                            w = []
                            if hl == 0:
                                w = [kvv, ("se", sp_free_se[pa]),
                                     ("ve", negc_ve)] + qhat_tp
                            hg = g * 4 + hl
                            if diag is not None:
                                p.step("tensor", lambda e, hl=hl, r=r, pa=pa,
                                       W=W: e.matmul(
                                    psB[:, pa * 1024 + hl * 256:
                                         pa * 1024 + hl * 256 + W],
                                    ident[:, :], tri[:, r * 256: r * 256 + W],
                                    start=True, stop=False), waits=w)
                                w = []
                            p.step("tensor", lambda e, hl=hl, hg=hg, r=r, pa=pa,
                                   s=s, B0rel=B0rel, W=W, buf=buf,
                                   st=(diag is None): e.matmul(
                                psB[:, pa * 1024 + hl * 256: pa * 1024 + hl * 256 + W],
                                kblk[:, buf * 512 + r * 128: buf * 512 + (r + 1) * 128],
                                qhat[:, hg * TOWN + s * 256 + B0rel * 128:
                                     hg * TOWN + s * 256 + B0rel * 128 + W],
                                start=st, stop=True),
                                waits=w, sem=("mm" if hl == 3 else None))
                        mm_sc = p.cur("mm")
                        p.step("scalar", lambda e, pa=pa, B0rel=B0rel, nB=nB: e.activation(
                            ap_md(pblk[:, :], pa * 1024 + B0rel * 512,
                                  [(128, 4), (512, nB), (1, 128)]),
                            ap_md(psB[:, :], pa * 1024,
                                  [(256, 4), (128, nB), (1, 128)]),
                            AF.Exp, bias=misc[:, 5:6], scale=1.0 / 128.0),
                            waits=[("mm", mm_sc), ("ve", negc_ve),
                                   ("mm", pb_free_mm[pa])], sem="se", ptr=True)
                        sp_free_se[pa] = p.cur("se")
                        if pending is not None:
                            flush(pending)
                        lasts = {Brel: (kb == 2 * s + Brel and r == 3)
                                 for Brel in range(2)}
                        pending = (pa, buf, r, B0rel, lasts)
                flush(pending)
                pending = None
                mmv = p.cur("mm")
                for Brel in range(2):
                    p.step("vector", lambda e, g=g, Brel=Brel: e.tensor_copy(
                        sums_row[0:1, (g * 2 + Brel) * 512:(g * 2 + Brel + 1) * 512],
                        psA[0:1, 1024 + Brel * 512:1024 + (Brel + 1) * 512]),
                        waits=[("mm", mmv), state.get("sumsd", ("gp", 0))], sem="ve")
                    p.step("vector", lambda e, g=g, s=s, Brel=Brel: e.tensor_copy(
                        ap_md(attn[:, :], g * 4 * TOWN + s * 256 + Brel * 128,
                              [(TOWN, 4), (1, 128)]),
                        psA[:, Brel * 512:(Brel + 1) * 512]), sem="ve")
            state["postev_ve"] = p.cur("ve")
            vev = p.cur("ve")
            sd = gp_step(lambda e: e.dma_start(
                out=sums_dram.ap(), in_=sums_row[0:1, :]),
                waits=[("ve", vev)])
            state["sumsd"] = sd
            w1 = gp_step(lambda e: e.dma_start(
                out=rspread[:, :],
                in_=hap(sums_dram.ap(), 0, [(32, P), (1, 32)])),
                waits=[sd])
            p.step("vector", lambda e: e.reciprocal(rspread[:, :], rspread[:, :]),
                   waits=[w1], sem="ve")
            vv = p.cur("ve")
            rv = gp_step(lambda e: e.dma_start(
                out=r_dram.ap(), in_=rspread[:, :]),
                waits=[("ve", vv)])
            mults_ve = []
            for g in range(4):
                for Brel in range(2):
                    i = g * 2 + Brel
                    rrep = gp_step(lambda e, i=i: e.dma_start(
                        out=rbrep[:, (i % 2) * 512:(i % 2) * 512 + 512],
                        in_=hap(r_dram.ap(), i * 512, [(0, P), (1, 512)])),
                        waits=[rv] + ([mults_ve[i - 2]] if i >= 2 else []))
                    p.step("vector", lambda e, g=g, s=s, Brel=Brel, i=i:
                           e.tensor_tensor(
                        ap_md(attn[:, :], g * 4 * TOWN + s * 256 + Brel * 128,
                              [(TOWN, 4), (1, 128)]),
                        ap_md(attn[:, :], g * 4 * TOWN + s * 256 + Brel * 128,
                              [(TOWN, 4), (1, 128)]),
                        rbrep[:, (i % 2) * 512:(i % 2) * 512 + 512], ALU.mult),
                        waits=[rrep], sem="ve")
                    mults_ve.append(("ve", p.cur("ve")))
        return wo_done

    def oproj_phase(wo_done):
        mmv, vev = p.cur("mm"), p.cur("ve")
        xrel = []
        for tb in range(NT):
            sn = rr_sem()
            p.step("sync", lambda e, tb=tb: e.dma_start(
                out=x[:, tb * D:(tb + 1) * D],
                in_=hap(xspill.ap(), tb * 128 * D, [(D, P), (1, D)])),
                waits=[("mm", mmv), ("ve", vev), ("wsp", p.cur("wsp")),
                       (sn, p.cur(sn))],
                sem=sn, n=16)
            xrel.append((sn, p.cur(sn)))
        ev_prev = {}
        for tb in range(NT):
            ps = psA if tb % 2 == 0 else psB
            w = [("ww", wo_done), ("ve", vev)]
            if tb >= 2:
                w.append(("ve", ev_prev[tb - 2]))
            for c in range(16):
                lhs = attn[:, c * TOWN + tb * 128: c * TOWN + tb * 128 + 128]
                for ob in range(4):
                    p.step("tensor", lambda e, lhs=lhs, ps=ps, c=c, ob=ob: e.matmul(
                        ps[:, ob * 512:(ob + 1) * 512], lhs,
                        wo[:, c * D + ob * 512: c * D + ob * 512 + 512],
                        start=(c == 0), stop=False),
                        waits=(w if (c == 0 and ob == 0) else []), sem=None)
            for ob in range(4):
                p.step("tensor", lambda e, ps=ps, ob=ob: e.matmul(
                    ps[:, ob * 512:(ob + 1) * 512], ones16[0:1, 0:128],
                    bo_row[0:1, ob * 512:(ob + 1) * 512], start=False, stop=True),
                    sem="mm")
            mmv2 = p.cur("mm")
            for ob in range(4):
                p.step("vector", lambda e, ps=ps, tb=tb, ob=ob: e.scalar_tensor_tensor(
                    x[:, tb * D + ob * 512: tb * D + ob * 512 + 512],
                    ps[:, ob * 512:(ob + 1) * 512], 1.0,
                    x[:, tb * D + ob * 512: tb * D + ob * 512 + 512],
                    ALU.mult, ALU.add),
                    waits=[("mm", mmv2), xrel[tb]], sem="ve")
            ev_prev[tb] = p.cur("ve")
            ln_stats_tile(tb, True, [("ve", p.cur("ve"))])
        return p.cur("mm")

    tp_ffn1_box = [("gp", 0)]
    import contextlib
    with contextlib.ExitStack() as es:
        psA = es.enter_context(nc.psum_tensor("psA", [P, 2048], F32))
        psB = es.enter_context(nc.psum_tensor("psB", [P, 2048], F32))
        SEM = {}
        for sname in sems:
            SEM[sname] = es.enter_context(nc.semaphore(sname))
        block = es.enter_context(nc.Block())

        init_phase()
        ln_stats(True, p.cur("wx"), extra=[("ve", state["ve_init"])])
        sve = ln_gmax_allreduce(ar1i, ar1o)
        _, tp_ffn1 = ffn(sve, store_after=(TRUNC == 1),
                         post_evict=(None if TRUNC == 1 else
                                     lambda tb, vv: ln_stats_tile(tb, False, [("ve", vv)])))
        if TRUNC >= 2:
            cc_ag, qhat_tp = qkv_phase(tp_ffn1)
        if TRUNC >= 3:
            wo_done = attention_phase(cc_ag, qhat_tp)
            mm_oproj = oproj_phase(wo_done)
            if TRUNC == 3:
                vv = p.cur("ve")
                for tb in range(NT):
                    p.step("sync", lambda e, tb=tb: e.dma_start(
                        out=hap(y_out, tb * 128 * D, [(D, P), (1, D)]),
                        in_=x[:, tb * D:(tb + 1) * D]), waits=[("ve", vv)],
                        sem="wst", n=16)
        if TRUNC >= 4:
            for c in range(16):
                p.step("gpsimd", lambda e, c=c: e.dma_start(
                    out=wfT[:, c * D:(c + 1) * D],
                    in_=hap(wfT_in, c * P * D, [(D, P), (1, D)])),
                    waits=[("mm", mm_oproj)], sem="ww", n=16)
            ln_stats(True, 0, extra=[], pretiled=True)
            sve2 = ln_gmax_allreduce(ar2i, ar2o)
            ffn(sve2, store_after=True, region_mm=mm_oproj)

        stv = p.cur("wst")
        p.barrier_wait("sync", [("wst", stv)])

        for ename, attr in [("tensor", block.tensor), ("vector", block.vector),
                            ("scalar", block.scalar), ("sync", block.sync),
                            ("gpsimd", block.gpsimd)]:
            def body(eng, ename=ename):
                for fn in p.q[ename]:
                    fn(eng, SEM)
            attr(body)
    return nc


def make_inputs(ins, c, TRUNC):
    j = c % 4
    b = c // 4
    f32 = np.float32
    xfull = np.asarray(ins["L0_attn_x"], f32)
    xc = np.ascontiguousarray(xfull[b, j::4, :])
    pos = np.arange(j, 4096, 4)
    hdhalf = np.arange(0, HD, 2)
    inv = (1.0 / 10000.0 ** (hdhalf / HD))
    fr = np.outer(pos, inv)
    cosf, sinf = np.cos(fr).astype(f32), np.sin(fr).astype(f32)
    cost = np.ascontiguousarray(
        cosf.reshape(NT, 128, 64).transpose(1, 0, 2).reshape(128, NT * 64))
    sint = np.ascontiguousarray(
        sinf.reshape(NT, 128, 64).transpose(1, 0, 2).reshape(128, NT * 64))
    t_ = np.arange(128)
    tri = np.zeros((128, 4 * 256), np.float16)
    for r in range(4):
        m = (t_[None, :] > t_[:, None]) | ((t_[None, :] == t_[:, None]) & (j >= r))
        tri[:, r * 256:r * 256 + 128] = np.where(m, 0.0, -3840.0).astype(np.float16)
    qns = np.asarray(ins["L0_attn_q_norm_scale"], f32).reshape(1, HD)
    kns = np.asarray(ins["L0_attn_k_norm_scale"], f32).reshape(1, HD)
    return {
        "x": xc,
        "wq": np.asarray(ins["L0_attn_wq"], f32),
        "wk": np.asarray(ins["L0_attn_wk"], f32),
        "wv": np.asarray(ins["L0_attn_wv"], f32),
        "wo": np.asarray(ins["L0_attn_wo"], f32),
        "wfT": np.ascontiguousarray(np.asarray(ins["L0_ffn_weight"], f32).T),
        "bq": np.asarray(ins["L0_attn_bq"], f32).reshape(1, D),
        "bk": np.asarray(ins["L0_attn_bk"], f32).reshape(1, 512),
        "bv": np.asarray(ins["L0_attn_bv"], f32).reshape(1, 512),
        "bo": np.asarray(ins["L0_attn_bo"], f32).reshape(1, D),
        "bf": np.asarray(ins["L0_ffn_bias"], f32).reshape(1, D),
        "ws": np.full((1, 16), np.asarray(ins["L0_ffn_weight_scale"], f32)),
        "qnsr": np.ascontiguousarray(np.broadcast_to(np.tile(qns, 16), (P, D))),
        "knsr": np.ascontiguousarray(np.broadcast_to(np.tile(kns, 4), (P, 512))),
        "qns": qns, "kns": kns,
        "cosn": cost, "sinn": sint, "tri": tri,
        "ident": np.eye(128, dtype=np.float16),
    }


def kernel(**ins):
    TRUNC = 4
    nc = build_nc(0, TRUNC)
    in_maps = [make_inputs(ins, c, TRUNC) for c in range(8)]
    res = run_bass_kernel_spmd(nc, in_maps, core_ids=list(range(8)))
    out = np.zeros((2, 4096, 2048), np.float32)
    for c in range(8):
        out[c // 4, (c % 4)::4, :] = res.results[c]["y"]
    return out
